# version 20
# baseline (speedup 1.0000x reference)
import numpy as np
import concourse.bass as bass
import concourse.mybir as mybir
from concourse.bass_utils import run_bass_kernel_spmd

F32 = mybir.dt.float32
BF16 = mybir.dt.bfloat16
U8 = mybir.dt.uint8
AF = mybir.ActivationFunctionType
ALU = mybir.AluOpType
AX = mybir.AxisListType

NCORES = 8
T, C, NH, HD, NE, FF = 2048, 1024, 16, 64, 8, 4096
KI = C // 128
NTT = T // 128
NJ = T // 512
EXP_SHIFT = -12.0


class Rec:
    ENGS = ("tensor", "vector", "scalar", "sync", "gpsimd")

    def __init__(self):
        self.ops = {e: [] for e in self.ENGS}
        self.cnt = {s: 0 for s in ("T", "V", "S", "q0", "q1", "q2", "G")}

    SELF_SEM = {"vector": "V", "scalar": "S"}

    def emit(self, eng, fn, waits=(), inc=None, inc_n=1):
        waits = [w for w in waits if w is not None]
        for s, v in waits:
            assert v <= self.cnt[s], (s, v, self.cnt[s])
        own = self.SELF_SEM.get(eng)
        if own is not None:
            waits.append((own, self.cnt[own]))
            if inc is None:
                inc = own
                inc_n = 1
        post = None
        if inc is not None:
            self.cnt[inc] += inc_n
            post = self.cnt[inc]
        self.ops[eng].append((fn, tuple(waits), inc, inc_n))
        return post

    def replay(self, eng_name, eng, sems):
        waited = {}
        for fn, waits, inc, inc_n in self.ops[eng_name]:
            for s, v in waits:
                if waited.get(s, -1) < v:
                    eng.wait_ge(sems[s], v)
                    waited[s] = v
            ins = fn(eng)
            if inc is not None:
                if isinstance(ins, list):
                    ins = ins[-1]
                ins.then_inc(sems[inc], inc_n)


def build_core_program(B, debug=False):
    nc = bass.Bass("TRN2", target_bir_lowering=False)

    dx = nc.dram_tensor("x", [B, T, C], F32, kind="ExternalInput").ap()
    dwqkv = nc.dram_tensor("wqkv", [C, 3 * C], F32, kind="ExternalInput").ap()
    dbqkv = nc.dram_tensor("bqkv", [1, 3 * C], F32, kind="ExternalInput").ap()
    dwproj = nc.dram_tensor("wproj", [C, C], F32, kind="ExternalInput").ap()
    dbproj = nc.dram_tensor("bproj", [1, C], F32, kind="ExternalInput").ap()
    dw1 = nc.dram_tensor("w1", [NE, C, FF], F32, kind="ExternalInput").ap()
    db1 = nc.dram_tensor("b1", [NE, FF], F32, kind="ExternalInput").ap()
    dw2 = nc.dram_tensor("w2", [NE, FF, C], F32, kind="ExternalInput").ap()
    db2 = nc.dram_tensor("b2", [NE, C], F32, kind="ExternalInput").ap()
    dgw = nc.dram_tensor("gw", [C, NE], F32, kind="ExternalInput").ap()
    dgb = nc.dram_tensor("gb", [1, NE], F32, kind="ExternalInput").ap()
    dtemp = nc.dram_tensor("temp", [1, 1], F32, kind="ExternalInput").ap()
    dmask = nc.dram_tensor("cmask", [128, 4, 512], F32, kind="ExternalInput").ap()
    dident = nc.dram_tensor("ident", [128, 128], F32, kind="ExternalInput").ap()
    dout = nc.dram_tensor("out", [B, T, C], F32, kind="ExternalOutput").ap()
    swqkv = nc.dram_tensor("swqkv", [C, 3 * C], BF16, kind="Internal").ap()
    swproj = nc.dram_tensor("swproj", [C, C], BF16, kind="Internal").ap()
    sw1 = nc.dram_tensor("sw1", [NE, 16, 128, KI, 256], BF16, kind="Internal").ap()
    sw2 = nc.dram_tensor("sw2", [NE, FF, C], BF16, kind="Internal").ap()
    sxns = nc.dram_tensor("sxns", [NJ, 128, KI, 512], BF16, kind="Internal").ap()
    if debug:
        d_wbuf = nc.dram_tensor("d_wbuf", [128, NTT, NE], F32, kind="ExternalOutput").ap()
        d_acc1 = nc.dram_tensor("d_acc1", [128, NTT, C], F32, kind="ExternalOutput").ap()
        d_xn = nc.dram_tensor("d_xn", [NJ, 128, KI, 512], BF16, kind="ExternalOutput").ap()
        d_hb = nc.dram_tensor("d_hb", [128, 32, 512], BF16, kind="ExternalOutput").ap()
        d_kt = nc.dram_tensor("d_kt", [128, 8, T], BF16, kind="ExternalOutput").ap()
        d_vb = nc.dram_tensor("d_vb", [128, NTT, NH, HD + 1], BF16, kind="ExternalOutput").ap()
        d_qj = nc.dram_tensor("d_qj", [128, 8, 512], BF16, kind="ExternalOutput").ap()
        d_stat = nc.dram_tensor("d_stat", [128, 16], F32, kind="ExternalOutput").ap()
        d_xnt = nc.dram_tensor("d_xnt", [128, C], BF16, kind="ExternalOutput").ap()

    TOTAL = 211456
    MAPBASE = 16896
    nc.alloc_sbuf_tensor("PAD", [128, MAPBASE - 16512], U8)
    nc.alloc_sbuf_tensor("ARENA", [128, TOTAL], U8)

    off = [MAPBASE]

    class _Placed:
        __slots__ = ("t", "_at", "_end")

        def __init__(self, t, at, end):
            self.t = t
            self._at = at
            self._end = end

        def __getitem__(self, idx):
            return self.t[idx]

        def ap(self):
            return self.t.ap()

    def place(name, shape, dtype, at=None):
        esz = 2 if dtype == BF16 else 4
        width = int(np.prod(shape[1:])) * esz
        if at is None:
            at = (off[0] + 63) & ~63
            off[0] = at + width
            assert off[0] <= MAPBASE + TOTAL, (name, off[0])
        t = nc.alloc_sbuf_tensor_at(name, list(shape), dtype, offset=at)
        return _Placed(t, at, at + width)

    ACC = place("ACC", [128, NTT, C], F32)
    MASK = place("MASKB", [128, 4, 512], BF16)
    IDB = place("IDB", [128, 128], BF16)
    IDF = place("IDF", [128, 128], F32)
    CONST = place("CONSTS", [128, 8], F32)
    BQKVT = place("BQKVT", [128, 8], F32)
    B1T = place("B1T", [128, NE, 32], F32)
    GWB = place("GWB", [128, KI, NE], BF16)
    GBROW = place("GBROW", [1, NE], BF16)
    BKVROW = place("BKVROW", [1, 2 * C], BF16)
    BPROW = place("BPROW", [1, C], BF16)
    B2B = place("B2B", [NE, C], BF16)
    ONES = place("ONESB", [1, 128], BF16)
    RTEMP = place("RTEMP", [128, 1], F32)
    TTMP = place("TTMP", [1, 1], F32)
    WBUF = place("WBUF", [128, NTT, NE], F32)
    WBB = place("WBB", [128, 8], BF16)
    STAT = place("STATS", [128, 16], F32)
    GTMP = place("GTMP", [128, 32], F32)
    KT = place("KT", [128, 8, T], BF16)
    VB = place("VBV", [128, NTT, NH, HD + 1], BF16)
    W2E = place("W2E", [128, 32, C], BF16, at=KT._at)
    BQR = place("BQR", [1, C], F32, at=KT._at)
    B1R = place("B1R", [NE, FF], F32, at=KT._at)
    assert W2E._end <= VB._end
    WKV = place("WKV", [128, KI, 2 * C], BF16)
    WQQ = place("WQQ", [128, KI, C], BF16, at=WKV._at)
    WPJ = place("WPJ", [128, KI, C], BF16, at=WKV._at + KI * C * 2)
    HBUF = place("HBUF", [128, 32, 512], BF16, at=WKV._at)
    XST = place("XST", [128, 2, C], F32)
    XNT = place("XNT", [128, 2, C], BF16)
    XNJ = place("XNJ", [128, KI, 512], BF16)
    KTM = place("KTM", [128, C], BF16)
    LNSC = place("LNSC", [128, C], BF16)
    XNJ2 = place("XNJ2", [128, KI, 512], BF16, at=XST._at)
    EB = place("EB", [128, 2, 2, 512], BF16, at=XST._at)
    WTJ = place("WTJ", [8, 512], BF16, at=XST._at)
    YTMP = place("YTMP", [64, 2, 512], BF16, at=XST._at + 4096)
    RDEN = place("RDEN", [1, 2, 512], BF16, at=XST._at + 6144)
    QJ = place("QJ", [128, 8, 512], BF16, at=XNT._at)
    YJ = place("YJ", [128, 8, 512], BF16, at=XNT._at + 8192)
    XNJ3 = place("XNJ3", [128, 2, KI, 512], BF16, at=XNT._at)
    W1S = place("W1S", [128, 2, KI, 256], BF16, at=XST._at)
    XRS = place("XRS", [128, 2, C], F32, at=XNT._at)
    ORS = place("ORS", [128, 2, C], F32, at=XNT._at + 8192)
    assert max(LNSC._end, YJ._end, W1S._end, ORS._end) <= MAPBASE + TOTAL

    pA = nc.alloc_psum_tensor("pA", [128, 1024], F32)
    pB = nc.alloc_psum_tensor("pB", [128, 1024], F32)
    pS = nc.alloc_psum_tensor("pS", [128, 1024], F32)
    pE = nc.alloc_psum_tensor("pE", [128, 512], F32)
    pT = nc.alloc_psum_tensor("pT", [128, 1024], BF16)

    r = Rec()

    def em(eng, fn, waits=(), inc=None, inc_n=1):
        return r.emit(eng, fn, waits, inc, inc_n)

    class Tr:
        def __init__(self):
            self.evac = None
            self.grp = 0
    ptr = {id(p): Tr() for p in (pA, pB, pS, pE, pT)}

    def pw(p):
        return ptr[id(p)].evac

    def mm_group(p, mms, extra_waits=(), track=True):
        waits = [w for w in extra_waits if w is not None]
        if track:
            waits.append(pw(p))
        last = None
        for i, fn in enumerate(mms):
            last = em("tensor", fn, waits=waits if i == 0 else (),
                      inc="T" if i == len(mms) - 1 else None)
        if track:
            ptr[id(p)].grp = last
        return last

    em("gpsimd", lambda g: g.memset(ONES[:], 1.0), inc="G")
    em("gpsimd", lambda g: g.memset(CONST[:, 2:3], EXP_SHIFT), inc="G")
    em("gpsimd", lambda g: g.memset(CONST[:, 3:4], 0.125), inc="G")
    em("gpsimd", lambda g: g.memset(CONST[:, 4:5], -0.5), inc="G")
    em("gpsimd", lambda g: g.memset(CONST[:, 5:6], -1.0), inc="G")
    g_done = r.cnt["G"]
    q2 = {}
    q2["ident"] = em("gpsimd", lambda g: g.dma_start(IDB[:], dident), inc="q2", inc_n=16)
    q2["identf"] = em("gpsimd", lambda g: g.dma_start(IDF[:], dident), inc="q2", inc_n=16)
    q2["mask"] = em("gpsimd", lambda g: g.dma_start(MASK[:], dmask), inc="q2", inc_n=16)
    q2["gwb"] = em("gpsimd", lambda g: g.dma_start(
        GWB[:], dgw.rearrange("(ki p) e -> p ki e", p=128)), inc="q2", inc_n=16)
    q2["gb"] = em("gpsimd", lambda g: g.dma_start(GBROW[:], dgb), inc="q2", inc_n=16)
    q2["bkv"] = em("gpsimd", lambda g: g.dma_start(BKVROW[:], dbqkv[:, C:3 * C]),
                   inc="q2", inc_n=16)
    q2["bp"] = em("gpsimd", lambda g: g.dma_start(BPROW[:], dbproj), inc="q2", inc_n=16)
    q2["b2"] = em("gpsimd", lambda g: g.dma_start(B2B[:], db2), inc="q2", inc_n=16)
    q2["temp"] = em("gpsimd", lambda g: g.dma_start(TTMP[:], dtemp), inc="q2", inc_n=16)
    q2["bqr"] = em("gpsimd", lambda g: g.dma_start(BQR[:], dbqkv[:, 0:C]), inc="q2", inc_n=16)
    v_rt = em("vector", lambda v: v.reciprocal(STAT[0:1, 15:16], TTMP[:]),
              waits=[("q2", q2["temp"])], inc="V")
    em("vector", lambda v: v.tensor_copy(WBB[0:1, 0:1], STAT[0:1, 15:16]), inc="V")
    t_rt = em("tensor", lambda t: t.matmul(pE[:, 0:1], ONES[:], WBB[0:1, 0:1],
                                           start=True, stop=True, skip_group_check=True),
              waits=[("V", r.cnt["V"]), ("G", g_done)], inc="T")
    v_rt2 = em("vector", lambda v: v.tensor_copy(RTEMP[:], pE[:, 0:1]),
               waits=[("T", t_rt)], inc="V")
    ptr[id(pE)].evac = ("V", v_rt2)
    tb = None
    for fc in range(8):
        tb = em("tensor", lambda t, fc=fc: t.transpose(
            pE[:, 8 + fc:9 + fc], BQR[0:1, fc * 128:(fc + 1) * 128], IDF[0:1, 0:1]),
            waits=[("q2", q2["bqr"]), ("q2", q2["identf"]), pw(pE)] if fc == 0 else (),
            inc="T" if fc == 7 else None)
    ptr[id(pE)].grp = tb
    v_bq = em("vector", lambda v: v.tensor_copy(BQKVT[:], pE[:, 8:16]),
              waits=[("T", tb)], inc="V")
    ptr[id(pE)].evac = ("V", v_bq)
    q2["b1r"] = em("gpsimd", lambda g: g.dma_start(B1R[:], db1),
                   waits=[("T", tb)], inc="q2", inc_n=16)
    for hc in range(32):
        tb1 = em("tensor", lambda t, hc=hc: t.transpose(
            pE[:, 0:8], B1R[:, hc * 128:(hc + 1) * 128], IDF[0:8, 0:8]),
            waits=[("q2", q2["b1r"]), pw(pE)], inc="T")
        ptr[id(pE)].grp = tb1
        vb1 = em("vector", lambda v, hc=hc: v.tensor_copy(B1T[:, :, hc], pE[:, 0:8]),
                 waits=[("T", tb1)], inc="V")
        ptr[id(pE)].evac = ("V", vb1)
    b1t_done_T = r.cnt["T"]
    v_b1_done = r.cnt["V"]
    q2["wqkv"] = em("gpsimd", lambda g: g.dma_start(swqkv, dwqkv), inc="q2", inc_n=16)
    q2["wproj"] = em("gpsimd", lambda g: g.dma_start(swproj, dwproj), inc="q2", inc_n=16)
    q2_w2, q2_w1 = {}, {}
    for e in range(NE):
        q2_w2[e] = em("gpsimd", lambda g, e=e: g.dma_start(sw2[e], dw2[e]),
                      inc="q2", inc_n=16)
        for sb in range(16):
            q2_w1[e] = em("gpsimd", lambda g, e=e, sb=sb: g.dma_start(
                sw1[e, sb],
                dw1[e].rearrange("(ki p) (sb f) -> sb p ki f", p=128, f=256)[sb]),
                inc="q2", inc_n=16)

    nc.const_aps.register(F32, EXP_SHIFT, CONST[:, 2:3])
    nc.const_aps.register(F32, 0.125, CONST[:, 3:4])
    nc.const_aps.register(F32, -0.5, CONST[:, 4:5])
    nc.const_aps.register(F32, -1.0, CONST[:, 5:6])

    NONE2 = [None, None]
    last = {
        "XST": list(NONE2), "XNT": list(NONE2), "XNJ": None, "KTM": None,
        "LNSC": None, "EB": list(NONE2), "XNJ3": list(NONE2), "W1S": list(NONE2),
        "XRS": list(NONE2), "ORS": list(NONE2), "XNJ2": None, "QJ": None, "ACCga": None,
        "YJ": None, "WTJ": None, "WBB": ("T", t_rt), "KVseg": ("T", b1t_done_T),
        "Bseg": None, "yun": list(NONE2),
    }
    last_w2_T = b1t_done_T
    gW_final = None

    for b in range(B):
        vb_ones = em("vector", lambda v: v.memset(VB[:, :, :, HD:HD + 1], 1.0),
                     waits=[last["KVseg"]], inc="V")
        wkv_k = []
        for ki in range(KI):
            wkv_k.append(("q1", em("scalar", lambda s, ki=ki: s.dma_start(
                WKV[:, ki, :],
                swqkv.rearrange("(ki p) c -> p ki c", p=128)[:, ki, C:3 * C]),
                waits=[("q2", q2["wqkv"]), last["Bseg"]] if ki == 0 else (),
                inc="q1", inc_n=16)))
        wkv_ready = wkv_k[-1]
        xnj_ready = None
        spv = None
        ktr_pend = None
        for j in range(NJ):
            for tt4 in range(4):
                tt = j * 4 + tt4
                sl = tt % 2
                if sl == 0:
                    xv = em("scalar", lambda s, b=b, tt=tt: s.dma_start(
                        XST[:], dx[b].rearrange("(n p) c -> p n c", p=128)[:, tt:tt + 2, :]),
                        waits=[last["XST"][0]], inc="q1", inc_n=16)
                    xv_cur = ("q1", xv)
                    em("vector", lambda v: v.reduce_sum(
                        STAT[:, 0:2].rearrange("p c -> p c 1") if False else STAT[:, 0:2],
                        XST[:], AX.X), waits=[xv_cur], inc="V")
                    s_sq0 = em("scalar", lambda s: s.activation(
                        LNSC[:], XST[:, 0, :], AF.Square, accum_out=STAT[:, 2:3]),
                        waits=[xv_cur, last["LNSC"], ("G", g_done)], inc="S")
                    s_sq = em("scalar", lambda s: s.activation(
                        LNSC[:], XST[:, 1, :], AF.Square, accum_out=STAT[:, 3:4]),
                        waits=[("S", s_sq0)], inc="S")
                    em("vector", lambda v: v.tensor_scalar(
                        STAT[:, 4:6], STAT[:, 0:2], 1.0 / C, None, ALU.mult), inc="V")
                    em("vector", lambda v: v.tensor_scalar(
                        STAT[:, 6:8], STAT[:, 2:4], 1.0 / C, 1e-5, ALU.mult, ALU.add),
                        waits=[("S", s_sq)], inc="V")
                    em("vector", lambda v: v.tensor_mul(
                        STAT[:, 8:10], STAT[:, 4:6], STAT[:, 4:6]), inc="V")
                    v5 = em("vector", lambda v: v.tensor_sub(
                        STAT[:, 10:12], STAT[:, 6:8], STAT[:, 8:10]), inc="V")
                    sln = em("scalar", lambda s: s.activation(
                        STAT[:, 12:14], STAT[:, 10:12], AF.Ln),
                        waits=[("V", v5), ("G", g_done)], inc="S")
                    s3 = em("scalar", lambda s: s.activation(
                        STAT[:, 14:16], STAT[:, 12:14], AF.Exp, scale=-0.5),
                        waits=[("S", sln)], inc="S")
                    s3_cur = s3
                xv = xv_cur
                v6 = em("vector", lambda v, sl=sl: v.tensor_scalar(
                    XNT[:, sl, :], XST[:, sl, :], STAT[:, 4 + sl:5 + sl],
                    STAT[:, 14 + sl:15 + sl], ALU.subtract, ALU.mult),
                    waits=[("S", s3_cur), last["XNT"][sl]], inc="V")
                if sl == 1:
                    last["XST"][0] = ("V", v6)
                if debug and b == 0 and tt == 0:
                    em("gpsimd", lambda g: g.dma_start(d_stat, STAT[:]),
                       waits=[("V", v6), ("S", s3)], inc="q2", inc_n=16)
                    em("gpsimd", lambda g: g.dma_start(d_xnt, XNT[:, 0, :]),
                       waits=[("V", v6)], inc="q2", inc_n=16)
                tv = None
                for ki in range(KI):
                    tv = em("tensor", lambda t, ki=ki, sl=sl: t.transpose(
                        pT[:, ki * 128:(ki + 1) * 128],
                        XNT[:, sl, ki * 128:(ki + 1) * 128], IDB[:]),
                        waits=[("V", v6), ("q2", q2["ident"]), pw(pT)] if ki == 0 else (),
                        inc="T" if ki == KI - 1 else None)
                ptr[id(pT)].grp = tv
                last["XNT"][sl] = ("T", tv)
                ev = em("vector", lambda v, tt4=tt4: v.tensor_copy(
                    XNJ[:, :, tt4 * 128:(tt4 + 1) * 128],
                    pT[:].rearrange("p (ki f) -> p ki f", f=128)),
                    waits=[("T", tv)] + ([last["XNJ"]] if tt4 == 0 else []), inc="V")
                ptr[id(pT)].evac = ("V", ev)
                xnj_ready = ("V", ev)
                if ktr_pend is not None:
                    f = ktr_pend
                    ktr_pend = None
                    f()

                def emit_kv(dst, col0, bias_off, tt4, base_w):
                    last_mm = None
                    for hh in range(2):
                        for ki in range(KI):
                            def f(t, ki=ki, hh=hh, dst=dst, col0=col0, tt4=tt4):
                                return t.matmul(
                                    dst[:, hh * 512:(hh + 1) * 512],
                                    XNJ[:, ki, tt4 * 128:(tt4 + 1) * 128],
                                    WKV[:, ki, col0 + hh * 512:col0 + (hh + 1) * 512],
                                    start=(ki == 0), stop=False, skip_group_check=True)
                            w = list(base_w) if (ki == 0 and hh == 0) else []
                            if hh == 0:
                                w.append(wkv_k[ki])
                            last_mm = em("tensor", f, waits=w)
                        def f(t, hh=hh, dst=dst, bias_off=bias_off):
                            return t.matmul(
                                dst[:, hh * 512:(hh + 1) * 512], ONES[:],
                                BKVROW[:, bias_off + hh * 512:bias_off + (hh + 1) * 512],
                                start=False, stop=True, skip_group_check=True)
                        last_mm = em("tensor", f, inc="T" if hh == 1 else None)
                    return last_mm
                tK = emit_kv(pA, 0, 0, tt4,
                             [xnj_ready, ("q2", q2["bkv"]), ("G", g_done)] + (
                                 [pw(pA)] if pw(pA) else []))
                ptr[id(pA)].grp = tK
                tV = emit_kv(pB, C, C, tt4,
                             [xnj_ready] + ([pw(pB)] if pw(pB) else []))
                ptr[id(pB)].grp = tV
                gm = []
                for ki in range(KI):
                    def f(t, ki=ki, tt4=tt4):
                        return t.matmul(pS[:, 0:8], XNJ[:, ki, tt4 * 128:(tt4 + 1) * 128],
                                        GWB[:, ki, :], start=(ki == 0), stop=False,
                                        skip_group_check=True)
                    gm.append(f)
                gm.append(lambda t: t.matmul(pS[:, 0:8], ONES[:], GBROW[:],
                                             start=False, stop=True, skip_group_check=True))
                tG = mm_group(pS, gm, extra_waits=[xnj_ready, ("q2", q2["gwb"]),
                                                   ("q2", q2["gb"])])

                evK = em("vector", lambda v: v.tensor_copy(KTM[:], pA[:]),
                         waits=[("T", tK), last["KTM"]], inc="V")
                ptr[id(pA)].evac = ("V", evK)
                evV = em("vector", lambda v, tt=tt: v.tensor_copy(
                    VB[:, tt, :, 0:HD], pB[:].rearrange("p (h d) -> p h d", d=HD)),
                    waits=[("T", tV), ("V", vb_ones)], inc="V")
                ptr[id(pB)].evac = ("V", evV)
                evG = em("scalar", lambda s, tt=tt: s.activation(
                    WBUF[:, tt, :], pS[:, 0:8], AF.Copy, scale=RTEMP[:, 0:1]),
                    waits=[("T", tG), ("V", v_rt2)], inc="S")
                ptr[id(pS)].evac = ("S", evG)
                def emit_ktr(tt=tt, evK=evK):
                    ktv = None
                    for fc in range(8):
                        ktv = em("tensor", lambda t, fc=fc: t.transpose(
                            pT[:, fc * 128:(fc + 1) * 128],
                            KTM[:, fc * 128:(fc + 1) * 128], IDB[:]),
                            waits=[("V", evK), pw(pT)] if fc == 0 else (),
                            inc="T" if fc == 7 else None)
                    ptr[id(pT)].grp = ktv
                    last["KTM"] = ("T", ktv)
                    evKT = em("vector", lambda v, tt=tt: v.tensor_copy(
                        KT[:, :, tt * 128:(tt + 1) * 128],
                        pT[:].rearrange("p (fc f) -> p fc f", f=128)),
                        waits=[("T", ktv)] + ([last["KVseg"]] if tt == 0 else []),
                        inc="V")
                    ptr[id(pT)].evac = ("V", evKT)
                ktr_pend = emit_ktr
            if j == NJ - 1 and ktr_pend is not None:
                f = ktr_pend
                ktr_pend = None
                f()
            spv = em("scalar", lambda s, j=j: s.dma_start(sxns[j], XNJ[:]),
                     waits=[xnj_ready], inc="q1", inc_n=16)
            last["XNJ"] = ("q1", spv)
        p1_T = r.cnt["T"]
        p1_V = r.cnt["V"]
        p1_S = r.cnt["S"]
        for tt in range(NTT):
            em("vector", lambda v, tt=tt: v.reduce_max(
                STAT[:, 8:9], WBUF[:, tt, :], AX.X), waits=[("S", p1_S)], inc="V")
            em("vector", lambda v, tt=tt: v.tensor_scalar(
                GTMP[:, 8:16], WBUF[:, tt, :], STAT[:, 8:9], None, ALU.is_ge), inc="V")
            em("vector", lambda v, tt=tt: v.scalar_tensor_tensor(
                GTMP[:, 16:24], GTMP[:, 8:16], -1e30, WBUF[:, tt, :],
                ALU.mult, ALU.add), inc="V")
            em("vector", lambda v: v.reduce_max(STAT[:, 9:10], GTMP[:, 16:24], AX.X),
               inc="V")
            g5 = em("vector", lambda v, tt=tt: v.reduce_max(
                STAT[:, 10:11], WBUF[:, tt, :], AX.X, negate=True), inc="V")
            sE = em("scalar", lambda s, tt=tt: s.activation(
                GTMP[:, 24:32], WBUF[:, tt, :], AF.Exp, bias=STAT[:, 10:11]),
                waits=[("V", g5)], inc="S")
            em("vector", lambda v, tt=tt: v.tensor_scalar(
                GTMP[:, 8:16], WBUF[:, tt, :], STAT[:, 9:10], None, ALU.is_ge),
                waits=[("S", sE)], inc="V")
            em("vector", lambda v: v.tensor_mul(
                GTMP[:, 24:32], GTMP[:, 24:32], GTMP[:, 8:16]), inc="V")
            em("vector", lambda v: v.reduce_sum(STAT[:, 11:12], GTMP[:, 24:32], AX.X),
               inc="V")
            em("vector", lambda v: v.reciprocal(STAT[:, 12:13], STAT[:, 11:12]), inc="V")
            em("vector", lambda v, tt=tt: v.tensor_scalar(
                WBUF[:, tt, :], GTMP[:, 24:32], STAT[:, 12:13], None, ALU.mult),
                inc="V")

        qq = em("sync", lambda s: s.dma_start(
            WQQ[:], swqkv.rearrange("(ki p) c -> p ki c", p=128)[:, :, 0:C]),
            waits=[("T", p1_T), ("q2", q2["wqkv"])], inc="q0", inc_n=16)
        qp = em("sync", lambda s: s.dma_start(
            WPJ[:], swproj.rearrange("(ki p) c -> p ki c", p=128)),
            waits=[("q2", q2["wproj"])], inc="q0", inc_n=16)

        vY_last = None
        for j in range(NJ):
            xr = em("scalar", lambda s, j=j: s.dma_start(XNJ2[:], sxns[j]),
                    waits=[("q1", spv), last["XNJ2"]], inc="q1", inc_n=16)
            evQ = None
            for fc in range(8):
                pq = pS if fc % 2 == 0 else pE
                mms = []
                for ki in range(KI):
                    def f(t, ki=ki, fc=fc, pq=pq):
                        return t.matmul(pq[:, 0:512], WQQ[:, ki, fc * 128:(fc + 1) * 128],
                                        XNJ2[:, ki, :], start=(ki == 0), stop=(ki == KI - 1),
                                        skip_group_check=True)
                    mms.append(f)
                tQ = mm_group(pq, mms,
                              extra_waits=[("q1", xr), ("q0", qq), ("V", v_bq)])
                evQ = em("vector", lambda v, fc=fc, pq=pq: v.tensor_scalar(
                    QJ[:, fc, :], pq[:, 0:512], BQKVT[:, fc:fc + 1], None, ALU.add),
                    waits=[("T", tQ)] + ([last["QJ"]] if fc == 0 else []), inc="V")
                ptr[id(pq)].evac = ("V", evQ)
            qj_ready = ("V", evQ)

            nkc = 4 * j + 4
            norm_pend = []

            def emit_norm_a(h, avl):
                ys = h % 2
                lncell = ACC[0:1, 4 * j, 0:512]
                em("scalar", lambda s, ys=ys, lncell=lncell: s.activation(
                    lncell, pA[HD:HD + 1, ys * 512:ys * 512 + 512], AF.Ln),
                    waits=[("T", avl), last["ACCga"]], inc="S")
                vr2 = em("scalar", lambda s, ys=ys, lncell=lncell: s.activation(
                    RDEN[0:1, ys, :], lncell, AF.Exp, scale=-1.0), inc="S")
                return (vr2,)

            def emit_norm_b(h, vr2):
                ys = h % 2
                p0 = (h % 2) * 64
                fc = h // 2
                tB = em("tensor", lambda t, ys=ys: t.matmul(
                    pE[0:HD, :], ONES[0:1, 0:HD], RDEN[0:1, ys, :], start=True, stop=True,
                    skip_group_check=True), waits=[pw(pE), ("S", vr2)], inc="T")
                ptr[id(pE)].grp = tB
                sY = em("scalar", lambda s, ys=ys: s.copy(
                    YTMP[:, ys, :], pA[0:HD, ys * 512:ys * 512 + 512]), inc="S")
                last["yun"][ys] = ("S", sY)
                vY = em("vector", lambda v, p0=p0, fc=fc, ys=ys: v.tensor_mul(
                    YJ[p0:p0 + HD, fc, :], YTMP[:, ys, :], pE[0:HD, :]),
                    waits=[("T", tB), ("S", sY)] + ([last["YJ"]] if h < 2 else []),
                    inc="V")
                ptr[id(pE)].evac = ("V", vY)
                return vY

            for fc in range(NH // 2):
                pS_of = [pS, pB]

                def emit_s2(kc, first_wait):
                    px = pS_of[kc % 2]
                    w = [first_wait, ("V", p1_V)] + ([pw(px)] if pw(px) else [])
                    for half in range(2):
                        def f(t, kc=kc, half=half, px=px, fc=fc):
                            p0 = half * 64
                            return t.matmul(
                                px[:, half * 512:(half + 1) * 512],
                                KT[p0:p0 + 64, fc, kc * 128:(kc + 1) * 128],
                                QJ[p0:p0 + 64, fc, :], start=True, stop=True,
                                skip_group_check=True)
                        tS = em("tensor", f, waits=w if half == 0 else (),
                                inc="T" if half == 1 else None)
                    ptr[id(px)].grp = tS
                    return tS

                def emit_exp2(kc, tS):
                    px = pS_of[kc % 2]
                    es = kc % 2
                    sEx = em("scalar", lambda s, es=es, px=px: s.activation(
                        EB[:, es, :, :].rearrange("p a q -> p (a q)"), px[:],
                        AF.Exp, bias=EXP_SHIFT, scale=0.125),
                        waits=[("T", tS), last["EB"][es], ("V", p1_V), ("G", g_done)],
                        inc="S")
                    ptr[id(px)].evac = ("S", sEx)
                    out = ("S", sEx)
                    d = kc - 4 * j
                    if d >= 0:
                        for half in range(2):
                            mv = em("vector", lambda v, es=es, half=half, d=d: v.tensor_mul(
                                EB[:, es, half, :], EB[:, es, half, :], MASK[:, d, :]),
                                waits=[out, ("q2", q2["mask"])], inc="V")
                            out = ("V", mv)
                    return out

                def emit_av2(kc, eready, fc=fc):
                    es = kc % 2
                    lastmm = None
                    for half in range(2):
                        h = 2 * fc + half
                        wv = [eready]
                        if kc == 0:
                            wv.append(last["yun"][half])
                        def f(t, kc=kc, es=es, half=half, h=h):
                            return t.matmul(
                                pA[0:HD + 1, half * 512:half * 512 + 512],
                                VB[:, kc, h, :], EB[:, es, half, :],
                                start=(kc == 0), stop=(kc == nkc - 1),
                                skip_group_check=True)
                        lastmm = em("tensor", f, waits=wv, inc="T")
                    last["EB"][es] = ("T", lastmm)
                    return lastmm

                er_prev = None
                avl = None
                for kc in range(nkc + 1):
                    if kc < nkc:
                        tS = emit_s2(kc, qj_ready)
                        er = emit_exp2(kc, tS)
                    if kc == 0 and norm_pend:
                        for hprev, na in norm_pend:
                            vY_last = emit_norm_b(hprev, na)
                        norm_pend = []
                    if kc > 0:
                        avl = emit_av2(kc - 1, er_prev)
                    if kc < nkc:
                        er_prev = er
                for half in range(2):
                    h = 2 * fc + half
                    na = emit_norm_a(h, avl)[0]
                    norm_pend.append((h, na))
            for hprev, na in norm_pend:
                vY_last = emit_norm_b(hprev, na)
            norm_pend = []
            yj_ready = ("V", r.cnt["V"])

            for tt4 in range(4):
                tt = j * 4 + tt4
                vc = em("vector", lambda v, tt=tt: v.tensor_copy(WBB[:], WBUF[:, tt, :]),
                        waits=[last["WBB"]], inc="V")
                tW = em("tensor", lambda t: t.transpose(pT[0:8, 0:128], WBB[:], IDB[:]),
                        waits=[("V", vc), pw(pT)], inc="T")
                ptr[id(pT)].grp = tW
                last["WBB"] = ("T", tW)
                vW = em("vector", lambda v, tt4=tt4: v.tensor_copy(
                    WTJ[0:8, tt4 * 128:(tt4 + 1) * 128], pT[0:8, 0:128]),
                    waits=[("T", tW), last["WTJ"]], inc="V")
                ptr[id(pT)].evac = ("V", vW)
                mms = []
                for hh in range(2):
                    for ki in range(KI):
                        def f(t, ki=ki, hh=hh, tt4=tt4):
                            return t.matmul(pB[:, hh * 512:(hh + 1) * 512],
                                            YJ[:, ki, tt4 * 128:(tt4 + 1) * 128],
                                            WPJ[:, ki, hh * 512:(hh + 1) * 512],
                                            start=(ki == 0), stop=False,
                                            skip_group_check=True)
                        mms.append(f)
                    def f(t, hh=hh):
                        return t.matmul(pB[:, hh * 512:(hh + 1) * 512], ONES[:],
                                        BPROW[:, hh * 512:(hh + 1) * 512],
                                        start=False, stop=False, skip_group_check=True)
                    mms.append(f)
                    def f(t, hh=hh, tt4=tt4):
                        return t.matmul(pB[:, hh * 512:(hh + 1) * 512],
                                        WTJ[0:8, tt4 * 128:(tt4 + 1) * 128],
                                        B2B[:, hh * 512:(hh + 1) * 512],
                                        start=False, stop=(hh == 1), skip_group_check=True)
                    mms.append(f)
                tP = mm_group(pB, mms, extra_waits=[yj_ready, ("V", vW), ("q0", qp),
                                                    ("q2", q2["bp"]), ("q2", q2["b2"])])
                evP = em("vector", lambda v, tt=tt: v.tensor_copy(ACC[:, tt, :], pB[:]),
                         waits=[("T", tP), last["ACCga"]], inc="V")
                ptr[id(pB)].evac = ("V", evP)
            last["QJ"] = ("T", r.cnt["T"])
            last["XNJ2"] = ("T", r.cnt["T"])
            last["YJ"] = ("T", r.cnt["T"])
            last["WTJ"] = ("T", r.cnt["T"])
        p2_T = r.cnt["T"]
        if debug and b == 0:
            em("gpsimd", lambda g: g.dma_start(d_wbuf, WBUF[:]),
               waits=[("V", r.cnt["V"])], inc="q2", inc_n=16)
            em("gpsimd", lambda g: g.dma_start(d_acc1, ACC[:]),
               waits=[("V", r.cnt["V"])], inc="q2", inc_n=16)
            em("gpsimd", lambda g: g.dma_start(d_xn, sxns),
               waits=[("q1", r.cnt["q1"])], inc="q2", inc_n=16)
            em("gpsimd", lambda g: g.dma_start(d_kt, KT[:]),
               waits=[("V", r.cnt["V"])], inc="q2", inc_n=16)
            em("gpsimd", lambda g: g.dma_start(d_vb, VB[:]),
               waits=[("V", r.cnt["V"])], inc="q2", inc_n=16)
            em("gpsimd", lambda g: g.dma_start(d_qj, QJ[:]),
               waits=[("V", r.cnt["V"])], inc="q2", inc_n=16)

        for e in range(NE):
            xr3_pend = {}
            xr3_pend[0] = em("gpsimd", lambda g: g.dma_start(XNJ3[:, 0], sxns[0]),
                             waits=[("q1", spv), last["XNJ3"][0], ("T", p2_T)],
                             inc="q2", inc_n=16)
            w2l = None
            for c8 in range(8):
                w2l = em("gpsimd", lambda g, e=e, c8=c8: g.dma_start(
                    W2E[:, c8 * 4:(c8 + 1) * 4, :],
                    sw2[e].rearrange("(hc p) c -> p hc c", p=128)[:, c8 * 4:(c8 + 1) * 4, :]),
                    waits=[("q2", q2_w2[e]), ("T", p2_T if e == 0 else last_w2_T)]
                    if c8 == 0 else (), inc="q2", inc_n=16)
            w2_ready = w2l
            for j in range(NJ):
                sl3 = j % 2
                xr3 = xr3_pend.pop(j)
                tH = None
                sG = None
                sG_of = {}
                for sb in range(16):
                    ws = sb % 2
                    q1s = em("sync", lambda s, e=e, sb=sb, ws=ws: s.dma_start(
                        W1S[:, ws], sw1[e, sb]),
                        waits=[("q2", q2_w1[e]), last["W1S"][ws], ("T", p2_T)],
                        inc="q0", inc_n=16)
                    for half in range(2):
                        hc = sb * 2 + half
                        ph = pS if hc % 2 == 0 else pE
                        mms = []
                        for ki in range(KI):
                            def f(t, ki=ki, half=half, ws=ws, sl3=sl3, ph=ph):
                                return t.matmul(
                                    ph[:, 0:512], W1S[:, ws, ki, half * 128:(half + 1) * 128],
                                    XNJ3[:, sl3, ki, :], start=(ki == 0),
                                    stop=(ki == KI - 1), skip_group_check=True)
                            mms.append(f)
                        tH = mm_group(ph, mms, extra_waits=[("q0", q1s), ("q2", xr3)])
                        sG = em("scalar", lambda s, e=e, hc=hc, ph=ph: s.activation(
                            HBUF[:, hc, :], ph[:, 0:512], AF.Gelu,
                            bias=B1T[:, e, hc:hc + 1]),
                            waits=[("T", tH), ("V", v_b1_done)], inc="S")
                        sG_of[hc] = sG
                        ptr[id(ph)].evac = ("S", sG)
                    last["W1S"][ws] = ("T", tH)
                    if sb == 0 and j + 1 < NJ:
                        xr3_pend[j + 1] = em(
                            "gpsimd", lambda g, j=j: g.dma_start(
                                XNJ3[:, (j + 1) % 2], sxns[j + 1]),
                            waits=[("q1", spv), last["XNJ3"][(j + 1) % 2]],
                            inc="q2", inc_n=16)
                last["XNJ3"][sl3] = ("T", tH)
                if e == NE - 1 and j == NJ - 1:
                    w1_last_T = tH
                h_ready = ("S", sG)
                if debug and b == 0 and e == 0 and j == 0:
                    em("gpsimd", lambda g: g.dma_start(d_hb, HBUF[:]),
                       waits=[h_ready], inc="q2", inc_n=16)
                for tt4 in range(4):
                    tt = j * 4 + tt4
                    pw_ = pA if tt4 % 2 == 0 else pB
                    base_w = [("q2", w2_ready)] + [w for w in [pw(pw_)] if w]
                    tW2 = None
                    for hh in range(2):
                        for hc in range(32):
                            def f(t, hc=hc, hh=hh, tt4=tt4, pw_=pw_):
                                return t.matmul(
                                    pw_[:, hh * 512:(hh + 1) * 512],
                                    HBUF[:, hc, tt4 * 128:(tt4 + 1) * 128],
                                    W2E[:, hc, hh * 512:(hh + 1) * 512],
                                    start=(hc == 0), stop=(hc == 31 and hh == 1),
                                    skip_group_check=True)
                            w = list(base_w) if hc == 0 and hh == 0 else []
                            if hh == 0:
                                w.append(("S", sG_of[hc]))
                            tW2 = em("tensor", f, waits=w,
                                     inc="T" if (hc == 31 and hh == 1) else None)
                    ptr[id(pw_)].grp = tW2
                    last_w2_T = tW2
                    evF = em("vector", lambda v, tt=tt, e=e, pw_=pw_: v.scalar_tensor_tensor(
                        ACC[:, tt, :], pw_[:], WBUF[:, tt, e:e + 1], ACC[:, tt, :],
                        ALU.mult, ALU.add), waits=[("T", tW2)], inc="V")
                    ptr[id(pw_)].evac = ("V", evF)
                    if e == NE - 1 and tt % 2 == 1:
                        xa = em("gpsimd", lambda g, b=b, tt=tt: g.dma_start(
                            ACC[:, tt - 1:tt + 1, :],
                            dx[b].rearrange("(n p) c -> p n c", p=128)[:, tt - 1:tt + 1, :],
                            accum_op=ALU.add), waits=[("V", evF)], inc="q2", inc_n=16)
                        ov = em("sync", lambda s, b=b, tt=tt: s.dma_start(
                            dout[b].rearrange("(n p) c -> p n c", p=128)[:, tt - 1:tt + 1, :],
                            ACC[:, tt - 1:tt + 1, :]),
                            waits=[("q2", xa)], inc="q0", inc_n=16)
        last["KVseg"] = ("T", last_w2_T)
        last["Bseg"] = ("T", last_w2_T)
        p3_T = r.cnt["T"]

        qf = ("q0", ov)
        tf = ("T", p3_T)
        wf = ("T", w1_last_T)
        last["ACCga"] = qf
        last["XST"] = [wf, wf]
        last["XNT"] = [wf, wf]
        last["XNJ"] = ("q1", spv)
        last["KTM"] = wf
        last["LNSC"] = wf
        last["XNJ2"] = tf
        last["QJ"] = tf
        last["YJ"] = tf
        last["XNJ3"] = [wf, wf]
        last["W1S"] = [wf, wf]

    em("gpsimd", lambda g: g.engine_nop(),
       waits=[("q1", r.cnt["q1"]), ("q0", r.cnt["q0"]), ("q2", r.cnt["q2"])], inc="G")

    from contextlib import ExitStack
    ctx = ExitStack()
    sems = {s: ctx.enter_context(nc.semaphore(s)) for s in
            ("T", "V", "S", "q0", "q1", "q2", "G")}
    with nc.Block() as block:
        @block.tensor
        def _(t):
            r.replay("tensor", t, sems)

        @block.vector
        def _(v):
            r.replay("vector", v, sems)

        @block.scalar
        def _(s):
            r.replay("scalar", s, sems)

        @block.sync
        def _(s):
            r.replay("sync", s, sems)

        @block.gpsimd
        def _(g):
            r.replay("gpsimd", g, sems)
    ctx.close()
    return nc


def make_masks():
    a = np.arange(128)[:, None, None]
    d = np.arange(4)[None, :, None]
    bb = np.arange(512)[None, None, :]
    return (a + 128 * d <= bb).astype(np.float32)


_NC_CACHE = {}


def _get_nc(B):
    if B not in _NC_CACHE:
        _NC_CACHE[B] = build_core_program(B)
    return _NC_CACHE[B]


def kernel(L0_attn_x, L0_attn_w_qkv, L0_attn_b_qkv, L0_attn_w_proj, L0_attn_b_proj,
           L0_ffn_W1, L0_ffn_b1, L0_ffn_W2, L0_ffn_b2, L0_ffn_gate_w, L0_ffn_gate_b,
           L0_ffn_temperature):
    x = np.ascontiguousarray(np.asarray(L0_attn_x, dtype=np.float32))
    nb = x.shape[0]
    B = nb // NCORES
    nc = _get_nc(B)
    common = dict(
        wqkv=np.ascontiguousarray(L0_attn_w_qkv, np.float32),
        bqkv=np.ascontiguousarray(L0_attn_b_qkv, np.float32).reshape(1, 3 * C),
        wproj=np.ascontiguousarray(L0_attn_w_proj, np.float32),
        bproj=np.ascontiguousarray(L0_attn_b_proj, np.float32).reshape(1, C),
        w1=np.ascontiguousarray(L0_ffn_W1, np.float32),
        b1=np.ascontiguousarray(L0_ffn_b1, np.float32),
        w2=np.ascontiguousarray(L0_ffn_W2, np.float32),
        b2=np.ascontiguousarray(L0_ffn_b2, np.float32),
        gw=np.ascontiguousarray(L0_ffn_gate_w, np.float32),
        gb=np.ascontiguousarray(L0_ffn_gate_b, np.float32).reshape(1, NE),
        temp=np.ascontiguousarray(L0_ffn_temperature, np.float32).reshape(1, 1),
        cmask=make_masks(),
        ident=np.eye(128, dtype=np.float32),
    )
    in_maps = []
    for c in range(NCORES):
        m = dict(common)
        m["x"] = x[c * B:(c + 1) * B]
        in_maps.append(m)
    res = run_bass_kernel_spmd(nc, in_maps, core_ids=list(range(NCORES)))
    return np.concatenate([r["out"] for r in res.results], axis=0)


# version 21
# speedup vs baseline: 1.1065x; 1.1065x over previous
import numpy as np
import concourse.bass as bass
import concourse.mybir as mybir
from concourse.bass_utils import run_bass_kernel_spmd

F32 = mybir.dt.float32
BF16 = mybir.dt.bfloat16
U8 = mybir.dt.uint8
AF = mybir.ActivationFunctionType
ALU = mybir.AluOpType
AX = mybir.AxisListType

NCORES = 8
T, C, NH, HD, NE, FF = 2048, 1024, 16, 64, 8, 4096
KI = C // 128
NTT = T // 128
NJ = T // 512
EXP_SHIFT = -12.0


class Rec:
    ENGS = ("tensor", "vector", "scalar", "sync", "gpsimd")

    def __init__(self):
        self.ops = {e: [] for e in self.ENGS}
        self.cnt = {s: 0 for s in ("T", "V", "S", "q0", "q1", "q2", "G")}

    SELF_SEM = {"vector": "V", "scalar": "S"}

    def emit(self, eng, fn, waits=(), inc=None, inc_n=1):
        waits = [w for w in waits if w is not None]
        for s, v in waits:
            assert v <= self.cnt[s], (s, v, self.cnt[s])
        own = self.SELF_SEM.get(eng)
        if own is not None:
            waits.append((own, self.cnt[own]))
            if inc is None:
                inc = own
                inc_n = 1
        post = None
        if inc is not None:
            self.cnt[inc] += inc_n
            post = self.cnt[inc]
        self.ops[eng].append((fn, tuple(waits), inc, inc_n))
        return post

    def replay(self, eng_name, eng, sems):
        waited = {}
        for fn, waits, inc, inc_n in self.ops[eng_name]:
            for s, v in waits:
                if waited.get(s, -1) < v:
                    eng.wait_ge(sems[s], v)
                    waited[s] = v
            ins = fn(eng)
            if inc is not None:
                if isinstance(ins, list):
                    ins = ins[-1]
                ins.then_inc(sems[inc], inc_n)


def build_core_program(B, debug=False):
    nc = bass.Bass("TRN2", target_bir_lowering=False)

    dx = nc.dram_tensor("x", [B, T, C], F32, kind="ExternalInput").ap()
    dwqkv = nc.dram_tensor("wqkv", [C, 3 * C], F32, kind="ExternalInput").ap()
    dbqkv = nc.dram_tensor("bqkv", [1, 3 * C], F32, kind="ExternalInput").ap()
    dwproj = nc.dram_tensor("wproj", [C, C], F32, kind="ExternalInput").ap()
    dbproj = nc.dram_tensor("bproj", [1, C], F32, kind="ExternalInput").ap()
    dw1 = nc.dram_tensor("w1", [NE, C, FF], F32, kind="ExternalInput").ap()
    db1 = nc.dram_tensor("b1", [NE, FF], F32, kind="ExternalInput").ap()
    dw2 = nc.dram_tensor("w2", [NE, FF, C], F32, kind="ExternalInput").ap()
    db2 = nc.dram_tensor("b2", [NE, C], F32, kind="ExternalInput").ap()
    dgw = nc.dram_tensor("gw", [C, NE], F32, kind="ExternalInput").ap()
    dgb = nc.dram_tensor("gb", [1, NE], F32, kind="ExternalInput").ap()
    dtemp = nc.dram_tensor("temp", [1, 1], F32, kind="ExternalInput").ap()
    dmask = nc.dram_tensor("cmask", [128, 4, 512], F32, kind="ExternalInput").ap()
    dident = nc.dram_tensor("ident", [128, 128], F32, kind="ExternalInput").ap()
    dout = nc.dram_tensor("out", [B, T, C], F32, kind="ExternalOutput").ap()
    swqkv = nc.dram_tensor("swqkv", [C, 3 * C], BF16, kind="Internal").ap()
    swproj = nc.dram_tensor("swproj", [C, C], BF16, kind="Internal").ap()
    sw1 = nc.dram_tensor("sw1", [NE, 16, 128, KI, 256], BF16, kind="Internal").ap()
    sw2 = nc.dram_tensor("sw2", [NE, FF, C], BF16, kind="Internal").ap()
    sxns = nc.dram_tensor("sxns", [NJ, 128, KI, 512], BF16, kind="Internal").ap()
    if debug:
        d_wbuf = nc.dram_tensor("d_wbuf", [128, NTT, NE], F32, kind="ExternalOutput").ap()
        d_acc1 = nc.dram_tensor("d_acc1", [128, NTT, C], F32, kind="ExternalOutput").ap()
        d_xn = nc.dram_tensor("d_xn", [NJ, 128, KI, 512], BF16, kind="ExternalOutput").ap()
        d_hb = nc.dram_tensor("d_hb", [128, 32, 512], BF16, kind="ExternalOutput").ap()
        d_kt = nc.dram_tensor("d_kt", [128, 8, T], BF16, kind="ExternalOutput").ap()
        d_vb = nc.dram_tensor("d_vb", [128, NTT, NH, HD + 1], BF16, kind="ExternalOutput").ap()
        d_qj = nc.dram_tensor("d_qj", [128, 8, 512], BF16, kind="ExternalOutput").ap()
        d_stat = nc.dram_tensor("d_stat", [128, 16], F32, kind="ExternalOutput").ap()
        d_xnt = nc.dram_tensor("d_xnt", [128, C], BF16, kind="ExternalOutput").ap()

    TOTAL = 211456
    MAPBASE = 16896
    nc.alloc_sbuf_tensor("PAD", [128, MAPBASE - 16512], U8)
    nc.alloc_sbuf_tensor("ARENA", [128, TOTAL], U8)

    off = [MAPBASE]

    class _Placed:
        __slots__ = ("t", "_at", "_end")

        def __init__(self, t, at, end):
            self.t = t
            self._at = at
            self._end = end

        def __getitem__(self, idx):
            return self.t[idx]

        def ap(self):
            return self.t.ap()

    def place(name, shape, dtype, at=None):
        esz = 2 if dtype == BF16 else 4
        width = int(np.prod(shape[1:])) * esz
        if at is None:
            at = (off[0] + 63) & ~63
            off[0] = at + width
            assert off[0] <= MAPBASE + TOTAL, (name, off[0])
        t = nc.alloc_sbuf_tensor_at(name, list(shape), dtype, offset=at)
        return _Placed(t, at, at + width)

    ACC = place("ACC", [128, NTT, C], F32)
    MASK = place("MASKB", [128, 4, 512], BF16)
    IDB = place("IDB", [128, 128], BF16)
    IDF = place("IDF", [128, 128], F32)
    CONST = place("CONSTS", [128, 8], F32)
    BQKVT = place("BQKVT", [128, 8], F32)
    B1T = place("B1T", [128, NE, 32], F32)
    GWB = place("GWB", [128, KI, NE], BF16)
    GBROW = place("GBROW", [1, NE], BF16)
    BKVROW = place("BKVROW", [1, 2 * C], BF16)
    BPROW = place("BPROW", [1, C], BF16)
    B2B = place("B2B", [NE, C], BF16)
    ONES = place("ONESB", [1, 128], BF16)
    RTEMP = place("RTEMP", [128, 1], F32)
    TTMP = place("TTMP", [1, 1], F32)
    WBUF = place("WBUF", [128, NTT, NE], F32)
    WBB = place("WBB", [128, 8], BF16)
    STAT = place("STATS", [128, 16], F32)
    GTMP = place("GTMP", [128, 32], F32)
    KT = place("KT", [128, 8, T], BF16)
    VB = place("VBV", [128, NTT, NH, HD + 1], BF16)
    W2E = place("W2E", [128, 32, C], BF16, at=KT._at)
    BQR = place("BQR", [1, C], F32, at=KT._at)
    B1R = place("B1R", [NE, FF], F32, at=KT._at)
    assert W2E._end <= VB._end
    WKV = place("WKV", [128, KI, 2 * C], BF16)
    WQQ = place("WQQ", [128, KI, C], BF16, at=WKV._at)
    WPJ = place("WPJ", [128, KI, C], BF16, at=WKV._at + KI * C * 2)
    HBUF = place("HBUF", [128, 32, 512], BF16, at=WKV._at)
    XST = place("XST", [128, 2, C], F32)
    XNT = place("XNT", [128, 2, C], BF16)
    XNJ = place("XNJ", [128, KI, 512], BF16)
    KTM = place("KTM", [128, C], BF16)
    LNSC = place("LNSC", [128, C], BF16)
    XNJ2 = place("XNJ2", [128, KI, 512], BF16, at=XST._at)
    EB = place("EB", [128, 2, 2, 512], BF16, at=XST._at)
    WTJ = place("WTJ", [8, 512], BF16, at=XST._at)
    YTMP = place("YTMP", [64, 2, 512], BF16, at=XST._at + 4096)
    RDEN = place("RDEN", [1, 2, 512], BF16, at=XST._at + 6144)
    QJ = place("QJ", [128, 8, 512], BF16, at=XNT._at)
    YJ = place("YJ", [128, 8, 512], BF16, at=XNT._at + 8192)
    XNJ3 = place("XNJ3", [128, 2, KI, 512], BF16, at=XNT._at)
    W1S = place("W1S", [128, 2, KI, 256], BF16, at=XST._at)
    XRS = place("XRS", [128, 2, C], F32, at=XNT._at)
    ORS = place("ORS", [128, 2, C], F32, at=XNT._at + 8192)
    assert max(LNSC._end, YJ._end, W1S._end, ORS._end) <= MAPBASE + TOTAL

    pA = nc.alloc_psum_tensor("pA", [128, 1024], F32)
    pB = nc.alloc_psum_tensor("pB", [128, 1024], F32)
    pS = nc.alloc_psum_tensor("pS", [128, 1024], F32)
    pE = nc.alloc_psum_tensor("pE", [128, 512], F32)
    pT = nc.alloc_psum_tensor("pT", [128, 1024], BF16)

    r = Rec()

    def em(eng, fn, waits=(), inc=None, inc_n=1):
        return r.emit(eng, fn, waits, inc, inc_n)

    class Tr:
        def __init__(self):
            self.evac = None
            self.grp = 0
    ptr = {id(p): Tr() for p in (pA, pB, pS, pE, pT)}

    def pw(p):
        return ptr[id(p)].evac

    def mm_group(p, mms, extra_waits=(), track=True):
        waits = [w for w in extra_waits if w is not None]
        if track:
            waits.append(pw(p))
        last = None
        for i, fn in enumerate(mms):
            last = em("tensor", fn, waits=waits if i == 0 else (),
                      inc="T" if i == len(mms) - 1 else None)
        if track:
            ptr[id(p)].grp = last
        return last

    em("gpsimd", lambda g: g.memset(ONES[:], 1.0), inc="G")
    em("gpsimd", lambda g: g.memset(CONST[:, 2:3], EXP_SHIFT), inc="G")
    em("gpsimd", lambda g: g.memset(CONST[:, 3:4], 0.125), inc="G")
    em("gpsimd", lambda g: g.memset(CONST[:, 4:5], -0.5), inc="G")
    em("gpsimd", lambda g: g.memset(CONST[:, 5:6], -1.0), inc="G")
    g_done = r.cnt["G"]
    q2 = {}
    q2["ident"] = em("gpsimd", lambda g: g.dma_start(IDB[:], dident), inc="q2", inc_n=16)
    q2["identf"] = em("gpsimd", lambda g: g.dma_start(IDF[:], dident), inc="q2", inc_n=16)
    q2["mask"] = em("gpsimd", lambda g: g.dma_start(MASK[:], dmask), inc="q2", inc_n=16)
    q2["gwb"] = em("gpsimd", lambda g: g.dma_start(
        GWB[:], dgw.rearrange("(ki p) e -> p ki e", p=128)), inc="q2", inc_n=16)
    q2["gb"] = em("gpsimd", lambda g: g.dma_start(GBROW[:], dgb), inc="q2", inc_n=16)
    q2["bkv"] = em("gpsimd", lambda g: g.dma_start(BKVROW[:], dbqkv[:, C:3 * C]),
                   inc="q2", inc_n=16)
    q2["bp"] = em("gpsimd", lambda g: g.dma_start(BPROW[:], dbproj), inc="q2", inc_n=16)
    q2["b2"] = em("gpsimd", lambda g: g.dma_start(B2B[:], db2), inc="q2", inc_n=16)
    q2["temp"] = em("gpsimd", lambda g: g.dma_start(TTMP[:], dtemp), inc="q2", inc_n=16)
    q2["bqr"] = em("gpsimd", lambda g: g.dma_start(BQR[:], dbqkv[:, 0:C]), inc="q2", inc_n=16)
    v_rt = em("vector", lambda v: v.reciprocal(STAT[0:1, 15:16], TTMP[:]),
              waits=[("q2", q2["temp"])], inc="V")
    em("vector", lambda v: v.tensor_copy(WBB[0:1, 0:1], STAT[0:1, 15:16]), inc="V")
    t_rt = em("tensor", lambda t: t.matmul(pE[:, 0:1], ONES[:], WBB[0:1, 0:1],
                                           start=True, stop=True, skip_group_check=True),
              waits=[("V", r.cnt["V"]), ("G", g_done)], inc="T")
    v_rt2 = em("vector", lambda v: v.tensor_copy(RTEMP[:], pE[:, 0:1]),
               waits=[("T", t_rt)], inc="V")
    ptr[id(pE)].evac = ("V", v_rt2)
    tb = None
    for fc in range(8):
        tb = em("tensor", lambda t, fc=fc: t.transpose(
            pE[:, 8 + fc:9 + fc], BQR[0:1, fc * 128:(fc + 1) * 128], IDF[0:1, 0:1]),
            waits=[("q2", q2["bqr"]), ("q2", q2["identf"]), pw(pE)] if fc == 0 else (),
            inc="T" if fc == 7 else None)
    ptr[id(pE)].grp = tb
    v_bq = em("vector", lambda v: v.tensor_copy(BQKVT[:], pE[:, 8:16]),
              waits=[("T", tb)], inc="V")
    ptr[id(pE)].evac = ("V", v_bq)
    q2["b1r"] = em("gpsimd", lambda g: g.dma_start(B1R[:], db1),
                   waits=[("T", tb)], inc="q2", inc_n=16)
    for hc in range(32):
        tb1 = em("tensor", lambda t, hc=hc: t.transpose(
            pE[:, 0:8], B1R[:, hc * 128:(hc + 1) * 128], IDF[0:8, 0:8]),
            waits=[("q2", q2["b1r"]), pw(pE)], inc="T")
        ptr[id(pE)].grp = tb1
        vb1 = em("vector", lambda v, hc=hc: v.tensor_copy(B1T[:, :, hc], pE[:, 0:8]),
                 waits=[("T", tb1)], inc="V")
        ptr[id(pE)].evac = ("V", vb1)
    b1t_done_T = r.cnt["T"]
    v_b1_done = r.cnt["V"]
    q2["wqkv"] = em("gpsimd", lambda g: g.dma_start(swqkv, dwqkv), inc="q2", inc_n=16)
    q2["wproj"] = em("gpsimd", lambda g: g.dma_start(swproj, dwproj), inc="q2", inc_n=16)
    q2_w2, q2_w1 = {}, {}
    for e in range(NE):
        q2_w2[e] = em("gpsimd", lambda g, e=e: g.dma_start(sw2[e], dw2[e]),
                      inc="q2", inc_n=16)
        for sb in range(16):
            q2_w1[e] = em("gpsimd", lambda g, e=e, sb=sb: g.dma_start(
                sw1[e, sb],
                dw1[e].rearrange("(ki p) (sb f) -> sb p ki f", p=128, f=256)[sb]),
                inc="q2", inc_n=16)

    nc.const_aps.register(F32, EXP_SHIFT, CONST[:, 2:3])
    nc.const_aps.register(F32, 0.125, CONST[:, 3:4])
    nc.const_aps.register(F32, -0.5, CONST[:, 4:5])
    nc.const_aps.register(F32, -1.0, CONST[:, 5:6])

    NONE2 = [None, None]
    last = {
        "XST": list(NONE2), "XNT": list(NONE2), "XNJ": None, "KTM": None,
        "LNSC": None, "EB": list(NONE2), "XNJ3": list(NONE2), "W1S": list(NONE2),
        "XRS": list(NONE2), "ORS": list(NONE2), "XNJ2": None, "QJ": None, "ACCga": None,
        "YJ": None, "WTJ": None, "WBB": ("T", t_rt), "KVseg": ("T", b1t_done_T),
        "Bseg": None, "yun": list(NONE2),
    }
    last_w2_T = b1t_done_T
    gW_final = None

    for b in range(B):
        vb_ones = em("vector", lambda v: v.memset(VB[:, :, :, HD:HD + 1], 1.0),
                     waits=[last["KVseg"]], inc="V")
        wkv_k = []
        for ki in range(KI):
            wkv_k.append(("q1", em("scalar", lambda s, ki=ki: s.dma_start(
                WKV[:, ki, :],
                swqkv.rearrange("(ki p) c -> p ki c", p=128)[:, ki, C:3 * C]),
                waits=[("q2", q2["wqkv"]), last["Bseg"]] if ki == 0 else (),
                inc="q1", inc_n=16)))
        wkv_ready = wkv_k[-1]
        xnj_ready = None
        spv = None
        ktr_pend = None
        for j in range(NJ):
            for tt4 in range(4):
                tt = j * 4 + tt4
                sl = tt % 2
                if sl == 0:
                    xv = em("scalar", lambda s, b=b, tt=tt: s.dma_start(
                        XST[:], dx[b].rearrange("(n p) c -> p n c", p=128)[:, tt:tt + 2, :]),
                        waits=[last["XST"][0]], inc="q1", inc_n=16)
                    xv_cur = ("q1", xv)
                    em("vector", lambda v: v.reduce_sum(
                        STAT[:, 0:2].rearrange("p c -> p c 1") if False else STAT[:, 0:2],
                        XST[:], AX.X), waits=[xv_cur], inc="V")
                    s_sq0 = em("scalar", lambda s: s.activation(
                        LNSC[:], XST[:, 0, :], AF.Square, accum_out=STAT[:, 2:3]),
                        waits=[xv_cur, last["LNSC"], ("G", g_done)], inc="S")
                    s_sq = em("scalar", lambda s: s.activation(
                        LNSC[:], XST[:, 1, :], AF.Square, accum_out=STAT[:, 3:4]),
                        waits=[("S", s_sq0)], inc="S")
                    em("vector", lambda v: v.tensor_scalar(
                        STAT[:, 4:6], STAT[:, 0:2], 1.0 / C, None, ALU.mult), inc="V")
                    em("vector", lambda v: v.tensor_scalar(
                        STAT[:, 6:8], STAT[:, 2:4], 1.0 / C, 1e-5, ALU.mult, ALU.add),
                        waits=[("S", s_sq)], inc="V")
                    em("vector", lambda v: v.tensor_mul(
                        STAT[:, 8:10], STAT[:, 4:6], STAT[:, 4:6]), inc="V")
                    v5 = em("vector", lambda v: v.tensor_sub(
                        STAT[:, 10:12], STAT[:, 6:8], STAT[:, 8:10]), inc="V")
                    sln = em("scalar", lambda s: s.activation(
                        STAT[:, 12:14], STAT[:, 10:12], AF.Ln),
                        waits=[("V", v5), ("G", g_done)], inc="S")
                    s3 = em("scalar", lambda s: s.activation(
                        STAT[:, 14:16], STAT[:, 12:14], AF.Exp, scale=-0.5),
                        waits=[("S", sln)], inc="S")
                    s3_cur = s3
                xv = xv_cur
                v6 = em("vector", lambda v, sl=sl: v.tensor_scalar(
                    XNT[:, sl, :], XST[:, sl, :], STAT[:, 4 + sl:5 + sl],
                    STAT[:, 14 + sl:15 + sl], ALU.subtract, ALU.mult),
                    waits=[("S", s3_cur), last["XNT"][sl]], inc="V")
                if sl == 1:
                    last["XST"][0] = ("V", v6)
                if debug and b == 0 and tt == 0:
                    em("gpsimd", lambda g: g.dma_start(d_stat, STAT[:]),
                       waits=[("V", v6), ("S", s3)], inc="q2", inc_n=16)
                    em("gpsimd", lambda g: g.dma_start(d_xnt, XNT[:, 0, :]),
                       waits=[("V", v6)], inc="q2", inc_n=16)
                tv = None
                for ki in range(KI):
                    tv = em("tensor", lambda t, ki=ki, sl=sl: t.transpose(
                        pT[:, ki * 128:(ki + 1) * 128],
                        XNT[:, sl, ki * 128:(ki + 1) * 128], IDB[:]),
                        waits=[("V", v6), ("q2", q2["ident"]), pw(pT)] if ki == 0 else (),
                        inc="T" if ki == KI - 1 else None)
                ptr[id(pT)].grp = tv
                last["XNT"][sl] = ("T", tv)
                ev = em("vector", lambda v, tt4=tt4: v.tensor_copy(
                    XNJ[:, :, tt4 * 128:(tt4 + 1) * 128],
                    pT[:].rearrange("p (ki f) -> p ki f", f=128)),
                    waits=[("T", tv)] + ([last["XNJ"]] if tt4 == 0 else []), inc="V")
                ptr[id(pT)].evac = ("V", ev)
                xnj_ready = ("V", ev)
                if ktr_pend is not None:
                    f = ktr_pend
                    ktr_pend = None
                    f()

                def emit_kv(dst, col0, bias_off, tt4, base_w):
                    last_mm = None
                    for hh in range(2):
                        for ki in range(KI):
                            def f(t, ki=ki, hh=hh, dst=dst, col0=col0, tt4=tt4):
                                return t.matmul(
                                    dst[:, hh * 512:(hh + 1) * 512],
                                    XNJ[:, ki, tt4 * 128:(tt4 + 1) * 128],
                                    WKV[:, ki, col0 + hh * 512:col0 + (hh + 1) * 512],
                                    start=(ki == 0), stop=False, skip_group_check=True)
                            w = list(base_w) if (ki == 0 and hh == 0) else []
                            if hh == 0:
                                w.append(wkv_k[ki])
                            last_mm = em("tensor", f, waits=w)
                        def f(t, hh=hh, dst=dst, bias_off=bias_off):
                            return t.matmul(
                                dst[:, hh * 512:(hh + 1) * 512], ONES[:],
                                BKVROW[:, bias_off + hh * 512:bias_off + (hh + 1) * 512],
                                start=False, stop=True, skip_group_check=True)
                        last_mm = em("tensor", f, inc="T" if hh == 1 else None)
                    return last_mm
                tK = emit_kv(pA, 0, 0, tt4,
                             [xnj_ready, ("q2", q2["bkv"]), ("G", g_done)] + (
                                 [pw(pA)] if pw(pA) else []))
                ptr[id(pA)].grp = tK
                tV = emit_kv(pB, C, C, tt4,
                             [xnj_ready] + ([pw(pB)] if pw(pB) else []))
                ptr[id(pB)].grp = tV
                gm = []
                for ki in range(KI):
                    def f(t, ki=ki, tt4=tt4):
                        return t.matmul(pS[:, 0:8], XNJ[:, ki, tt4 * 128:(tt4 + 1) * 128],
                                        GWB[:, ki, :], start=(ki == 0), stop=False,
                                        skip_group_check=True)
                    gm.append(f)
                gm.append(lambda t: t.matmul(pS[:, 0:8], ONES[:], GBROW[:],
                                             start=False, stop=True, skip_group_check=True))
                tG = mm_group(pS, gm, extra_waits=[xnj_ready, ("q2", q2["gwb"]),
                                                   ("q2", q2["gb"])])

                evK = em("vector", lambda v: v.tensor_copy(KTM[:], pA[:]),
                         waits=[("T", tK), last["KTM"]], inc="V")
                ptr[id(pA)].evac = ("V", evK)
                evV = em("vector", lambda v, tt=tt: v.tensor_copy(
                    VB[:, tt, :, 0:HD], pB[:].rearrange("p (h d) -> p h d", d=HD)),
                    waits=[("T", tV), ("V", vb_ones)], inc="V")
                ptr[id(pB)].evac = ("V", evV)
                evG = em("scalar", lambda s, tt=tt: s.activation(
                    WBUF[:, tt, :], pS[:, 0:8], AF.Copy, scale=RTEMP[:, 0:1]),
                    waits=[("T", tG), ("V", v_rt2)], inc="S")
                ptr[id(pS)].evac = ("S", evG)
                def emit_ktr(tt=tt, evK=evK):
                    ktv = None
                    for fc in range(8):
                        ktv = em("tensor", lambda t, fc=fc: t.transpose(
                            pT[:, fc * 128:(fc + 1) * 128],
                            KTM[:, fc * 128:(fc + 1) * 128], IDB[:]),
                            waits=[("V", evK), pw(pT)] if fc == 0 else (),
                            inc="T" if fc == 7 else None)
                    ptr[id(pT)].grp = ktv
                    last["KTM"] = ("T", ktv)
                    evKT = em("vector", lambda v, tt=tt: v.tensor_copy(
                        KT[:, :, tt * 128:(tt + 1) * 128],
                        pT[:].rearrange("p (fc f) -> p fc f", f=128)),
                        waits=[("T", ktv)] + ([last["KVseg"]] if tt == 0 else []),
                        inc="V")
                    ptr[id(pT)].evac = ("V", evKT)
                ktr_pend = emit_ktr
            if j == NJ - 1 and ktr_pend is not None:
                f = ktr_pend
                ktr_pend = None
                f()
            spv = em("scalar", lambda s, j=j: s.dma_start(sxns[j], XNJ[:]),
                     waits=[xnj_ready], inc="q1", inc_n=16)
            last["XNJ"] = ("q1", spv)
        p1_T = r.cnt["T"]
        p1_V = r.cnt["V"]
        p1_S = r.cnt["S"]
        for tt in range(NTT):
            em("vector", lambda v, tt=tt: v.reduce_max(
                STAT[:, 8:9], WBUF[:, tt, :], AX.X), waits=[("S", p1_S)], inc="V")
            em("vector", lambda v, tt=tt: v.tensor_scalar(
                GTMP[:, 8:16], WBUF[:, tt, :], STAT[:, 8:9], None, ALU.is_ge), inc="V")
            em("vector", lambda v, tt=tt: v.scalar_tensor_tensor(
                GTMP[:, 16:24], GTMP[:, 8:16], -1e30, WBUF[:, tt, :],
                ALU.mult, ALU.add), inc="V")
            em("vector", lambda v: v.reduce_max(STAT[:, 9:10], GTMP[:, 16:24], AX.X),
               inc="V")
            g5 = em("vector", lambda v, tt=tt: v.reduce_max(
                STAT[:, 10:11], WBUF[:, tt, :], AX.X, negate=True), inc="V")
            sE = em("scalar", lambda s, tt=tt: s.activation(
                GTMP[:, 24:32], WBUF[:, tt, :], AF.Exp, bias=STAT[:, 10:11]),
                waits=[("V", g5)], inc="S")
            em("vector", lambda v, tt=tt: v.tensor_scalar(
                GTMP[:, 8:16], WBUF[:, tt, :], STAT[:, 9:10], None, ALU.is_ge),
                waits=[("S", sE)], inc="V")
            em("vector", lambda v: v.tensor_mul(
                GTMP[:, 24:32], GTMP[:, 24:32], GTMP[:, 8:16]), inc="V")
            em("vector", lambda v: v.reduce_sum(STAT[:, 11:12], GTMP[:, 24:32], AX.X),
               inc="V")
            em("vector", lambda v: v.reciprocal(STAT[:, 12:13], STAT[:, 11:12]), inc="V")
            em("vector", lambda v, tt=tt: v.tensor_scalar(
                WBUF[:, tt, :], GTMP[:, 24:32], STAT[:, 12:13], None, ALU.mult),
                inc="V")

        qq = em("sync", lambda s: s.dma_start(
            WQQ[:], swqkv.rearrange("(ki p) c -> p ki c", p=128)[:, :, 0:C]),
            waits=[("T", p1_T), ("q2", q2["wqkv"])], inc="q0", inc_n=16)
        qp = em("sync", lambda s: s.dma_start(
            WPJ[:], swproj.rearrange("(ki p) c -> p ki c", p=128)),
            waits=[("q2", q2["wproj"])], inc="q0", inc_n=16)

        vY_last = None
        for j in range(NJ):
            xr = em("scalar", lambda s, j=j: s.dma_start(XNJ2[:], sxns[j]),
                    waits=[("q1", spv), last["XNJ2"]], inc="q1", inc_n=16)
            evQ = None
            for fc in range(8):
                pq = pS if fc % 2 == 0 else pE
                mms = []
                for ki in range(KI):
                    def f(t, ki=ki, fc=fc, pq=pq):
                        return t.matmul(pq[:, 0:512], WQQ[:, ki, fc * 128:(fc + 1) * 128],
                                        XNJ2[:, ki, :], start=(ki == 0), stop=(ki == KI - 1),
                                        skip_group_check=True)
                    mms.append(f)
                tQ = mm_group(pq, mms,
                              extra_waits=[("q1", xr), ("q0", qq), ("V", v_bq)])
                evQ = em("vector", lambda v, fc=fc, pq=pq: v.tensor_scalar(
                    QJ[:, fc, :], pq[:, 0:512], BQKVT[:, fc:fc + 1], None, ALU.add),
                    waits=[("T", tQ)] + ([last["QJ"]] if fc == 0 else []), inc="V")
                ptr[id(pq)].evac = ("V", evQ)
            qj_ready = ("V", evQ)

            nkc = 4 * j + 4
            norm_pend = []

            def emit_norm_a(h, avl):
                ys = h % 2
                lncell = ACC[0:1, 4 * j, 0:512]
                em("scalar", lambda s, ys=ys, lncell=lncell: s.activation(
                    lncell, pA[HD:HD + 1, ys * 512:ys * 512 + 512], AF.Ln),
                    waits=[("T", avl), last["ACCga"]], inc="S")
                vr2 = em("scalar", lambda s, ys=ys, lncell=lncell: s.activation(
                    RDEN[0:1, ys, :], lncell, AF.Exp, scale=-1.0), inc="S")
                return (vr2,)

            def emit_norm_b2(pends):
                tB = None
                w0 = [pw(pB)] + [("S", vr2) for _, vr2 in pends]
                for (h, vr2) in pends:
                    ys = h % 2
                    tB = em("tensor", lambda t, ys=ys: t.matmul(
                        pB[0:HD, ys * 512:ys * 512 + 512], ONES[0:1, 0:HD],
                        RDEN[0:1, ys, :], start=True, stop=True,
                        skip_group_check=True),
                        waits=w0 if ys == 0 else (), inc="T" if ys == 1 else None)
                ptr[id(pB)].grp = tB
                vY = None
                for (h, vr2) in pends:
                    ys = h % 2
                    p0 = ys * 64
                    fc = h // 2
                    sY = em("scalar", lambda s, ys=ys: s.copy(
                        YTMP[:, ys, :], pA[0:HD, ys * 512:ys * 512 + 512]), inc="S")
                    last["yun"][ys] = ("S", sY)
                    vY = em("vector", lambda v, p0=p0, fc=fc, ys=ys: v.tensor_mul(
                        YJ[p0:p0 + HD, fc, :], YTMP[:, ys, :],
                        pB[0:HD, ys * 512:ys * 512 + 512]),
                        waits=[("T", tB), ("S", sY)] + ([last["YJ"]] if h < 2 else []),
                        inc="V")
                ptr[id(pB)].evac = ("V", vY)
                return vY

            for fc in range(NH // 2):
                pS_of = [pS, pB]

                def emit_s2(kc, first_wait):
                    px = pS_of[kc % 2]
                    w = [first_wait, ("V", p1_V)] + ([pw(px)] if pw(px) else [])
                    for half in range(2):
                        def f(t, kc=kc, half=half, px=px, fc=fc):
                            p0 = half * 64
                            return t.matmul(
                                px[:, half * 512:(half + 1) * 512],
                                KT[p0:p0 + 64, fc, kc * 128:(kc + 1) * 128],
                                QJ[p0:p0 + 64, fc, :], start=True, stop=True,
                                skip_group_check=True)
                        tS = em("tensor", f, waits=w if half == 0 else (),
                                inc="T" if half == 1 else None)
                    ptr[id(px)].grp = tS
                    return tS

                def emit_exp2(kc, tS):
                    px = pS_of[kc % 2]
                    es = kc % 2
                    sEx = em("scalar", lambda s, es=es, px=px: s.activation(
                        EB[:, es, :, :].rearrange("p a q -> p (a q)"), px[:],
                        AF.Exp, bias=EXP_SHIFT, scale=0.125),
                        waits=[("T", tS), last["EB"][es], ("V", p1_V), ("G", g_done)],
                        inc="S")
                    ptr[id(px)].evac = ("S", sEx)
                    out = ("S", sEx)
                    d = kc - 4 * j
                    if d >= 0:
                        for half in range(2):
                            mv = em("vector", lambda v, es=es, half=half, d=d: v.tensor_mul(
                                EB[:, es, half, :], EB[:, es, half, :], MASK[:, d, :]),
                                waits=[out, ("q2", q2["mask"])], inc="V")
                            out = ("V", mv)
                    return out

                def emit_av2(kc, eready, fc=fc):
                    es = kc % 2
                    lastmm = None
                    for half in range(2):
                        h = 2 * fc + half
                        wv = [eready]
                        if kc == 0:
                            wv.append(last["yun"][half])
                        def f(t, kc=kc, es=es, half=half, h=h):
                            return t.matmul(
                                pA[0:HD + 1, half * 512:half * 512 + 512],
                                VB[:, kc, h, :], EB[:, es, half, :],
                                start=(kc == 0), stop=(kc == nkc - 1),
                                skip_group_check=True)
                        lastmm = em("tensor", f, waits=wv, inc="T")
                    last["EB"][es] = ("T", lastmm)
                    return lastmm

                er_prev = None
                avl = None
                for kc in range(nkc + 1):
                    if kc < nkc:
                        tS = emit_s2(kc, qj_ready)
                        er = emit_exp2(kc, tS)
                    if kc == 0 and norm_pend:
                        vY_last = emit_norm_b2(norm_pend)
                        norm_pend = []
                    if kc > 0:
                        avl = emit_av2(kc - 1, er_prev)
                    if kc < nkc:
                        er_prev = er
                for half in range(2):
                    h = 2 * fc + half
                    na = emit_norm_a(h, avl)[0]
                    norm_pend.append((h, na))
            if norm_pend:
                vY_last = emit_norm_b2(norm_pend)
            norm_pend = []
            yj_ready = ("V", r.cnt["V"])

            for tt4 in range(4):
                tt = j * 4 + tt4
                vc = em("vector", lambda v, tt=tt: v.tensor_copy(WBB[:], WBUF[:, tt, :]),
                        waits=[last["WBB"]], inc="V")
                tW = em("tensor", lambda t: t.transpose(pT[0:8, 0:128], WBB[:], IDB[:]),
                        waits=[("V", vc), pw(pT)], inc="T")
                ptr[id(pT)].grp = tW
                last["WBB"] = ("T", tW)
                vW = em("vector", lambda v, tt4=tt4: v.tensor_copy(
                    WTJ[0:8, tt4 * 128:(tt4 + 1) * 128], pT[0:8, 0:128]),
                    waits=[("T", tW), last["WTJ"]], inc="V")
                ptr[id(pT)].evac = ("V", vW)
                mms = []
                for hh in range(2):
                    for ki in range(KI):
                        def f(t, ki=ki, hh=hh, tt4=tt4):
                            return t.matmul(pB[:, hh * 512:(hh + 1) * 512],
                                            YJ[:, ki, tt4 * 128:(tt4 + 1) * 128],
                                            WPJ[:, ki, hh * 512:(hh + 1) * 512],
                                            start=(ki == 0), stop=False,
                                            skip_group_check=True)
                        mms.append(f)
                    def f(t, hh=hh):
                        return t.matmul(pB[:, hh * 512:(hh + 1) * 512], ONES[:],
                                        BPROW[:, hh * 512:(hh + 1) * 512],
                                        start=False, stop=False, skip_group_check=True)
                    mms.append(f)
                    def f(t, hh=hh, tt4=tt4):
                        return t.matmul(pB[:, hh * 512:(hh + 1) * 512],
                                        WTJ[0:8, tt4 * 128:(tt4 + 1) * 128],
                                        B2B[:, hh * 512:(hh + 1) * 512],
                                        start=False, stop=(hh == 1), skip_group_check=True)
                    mms.append(f)
                tP = mm_group(pB, mms, extra_waits=[yj_ready, ("V", vW), ("q0", qp),
                                                    ("q2", q2["bp"]), ("q2", q2["b2"])])
                evP = em("vector", lambda v, tt=tt: v.tensor_copy(ACC[:, tt, :], pB[:]),
                         waits=[("T", tP), last["ACCga"]], inc="V")
                ptr[id(pB)].evac = ("V", evP)
            last["QJ"] = ("T", r.cnt["T"])
            last["XNJ2"] = ("T", r.cnt["T"])
            last["YJ"] = ("T", r.cnt["T"])
            last["WTJ"] = ("T", r.cnt["T"])
        p2_T = r.cnt["T"]
        if debug and b == 0:
            em("gpsimd", lambda g: g.dma_start(d_wbuf, WBUF[:]),
               waits=[("V", r.cnt["V"])], inc="q2", inc_n=16)
            em("gpsimd", lambda g: g.dma_start(d_acc1, ACC[:]),
               waits=[("V", r.cnt["V"])], inc="q2", inc_n=16)
            em("gpsimd", lambda g: g.dma_start(d_xn, sxns),
               waits=[("q1", r.cnt["q1"])], inc="q2", inc_n=16)
            em("gpsimd", lambda g: g.dma_start(d_kt, KT[:]),
               waits=[("V", r.cnt["V"])], inc="q2", inc_n=16)
            em("gpsimd", lambda g: g.dma_start(d_vb, VB[:]),
               waits=[("V", r.cnt["V"])], inc="q2", inc_n=16)
            em("gpsimd", lambda g: g.dma_start(d_qj, QJ[:]),
               waits=[("V", r.cnt["V"])], inc="q2", inc_n=16)

        for e in range(NE):
            xr3_pend = {}
            xr3_pend[0] = em("gpsimd", lambda g: g.dma_start(XNJ3[:, 0], sxns[0]),
                             waits=[("q1", spv), last["XNJ3"][0], ("T", p2_T)],
                             inc="q2", inc_n=16)
            w2l = None
            for c8 in range(8):
                w2l = em("gpsimd", lambda g, e=e, c8=c8: g.dma_start(
                    W2E[:, c8 * 4:(c8 + 1) * 4, :],
                    sw2[e].rearrange("(hc p) c -> p hc c", p=128)[:, c8 * 4:(c8 + 1) * 4, :]),
                    waits=[("q2", q2_w2[e]), ("T", p2_T if e == 0 else last_w2_T)]
                    if c8 == 0 else (), inc="q2", inc_n=16)
            w2_ready = w2l
            for j in range(NJ):
                sl3 = j % 2
                xr3 = xr3_pend.pop(j)
                tH = None
                sG = None
                sG_of = {}
                for sb in range(16):
                    ws = sb % 2
                    q1s = em("sync", lambda s, e=e, sb=sb, ws=ws: s.dma_start(
                        W1S[:, ws], sw1[e, sb]),
                        waits=[("q2", q2_w1[e]), last["W1S"][ws], ("T", p2_T)],
                        inc="q0", inc_n=16)
                    for half in range(2):
                        hc = sb * 2 + half
                        ph = pS if hc % 2 == 0 else pE
                        mms = []
                        for ki in range(KI):
                            def f(t, ki=ki, half=half, ws=ws, sl3=sl3, ph=ph):
                                return t.matmul(
                                    ph[:, 0:512], W1S[:, ws, ki, half * 128:(half + 1) * 128],
                                    XNJ3[:, sl3, ki, :], start=(ki == 0),
                                    stop=(ki == KI - 1), skip_group_check=True)
                            mms.append(f)
                        tH = mm_group(ph, mms, extra_waits=[("q0", q1s), ("q2", xr3)])
                        sG = em("scalar", lambda s, e=e, hc=hc, ph=ph: s.activation(
                            HBUF[:, hc, :], ph[:, 0:512], AF.Gelu,
                            bias=B1T[:, e, hc:hc + 1]),
                            waits=[("T", tH), ("V", v_b1_done)], inc="S")
                        sG_of[hc] = sG
                        ptr[id(ph)].evac = ("S", sG)
                    last["W1S"][ws] = ("T", tH)
                    if sb == 0 and j + 1 < NJ:
                        xr3_pend[j + 1] = em(
                            "gpsimd", lambda g, j=j: g.dma_start(
                                XNJ3[:, (j + 1) % 2], sxns[j + 1]),
                            waits=[("q1", spv), last["XNJ3"][(j + 1) % 2]],
                            inc="q2", inc_n=16)
                last["XNJ3"][sl3] = ("T", tH)
                if e == NE - 1 and j == NJ - 1:
                    w1_last_T = tH
                h_ready = ("S", sG)
                if debug and b == 0 and e == 0 and j == 0:
                    em("gpsimd", lambda g: g.dma_start(d_hb, HBUF[:]),
                       waits=[h_ready], inc="q2", inc_n=16)
                for tt4 in range(4):
                    tt = j * 4 + tt4
                    pw_ = pA if tt4 % 2 == 0 else pB
                    base_w = [("q2", w2_ready)] + [w for w in [pw(pw_)] if w]
                    tW2 = None
                    for hh in range(2):
                        for hc in range(32):
                            def f(t, hc=hc, hh=hh, tt4=tt4, pw_=pw_):
                                return t.matmul(
                                    pw_[:, hh * 512:(hh + 1) * 512],
                                    HBUF[:, hc, tt4 * 128:(tt4 + 1) * 128],
                                    W2E[:, hc, hh * 512:(hh + 1) * 512],
                                    start=(hc == 0), stop=(hc == 31 and hh == 1),
                                    skip_group_check=True)
                            w = list(base_w) if hc == 0 and hh == 0 else []
                            if hh == 0:
                                w.append(("S", sG_of[hc]))
                            tW2 = em("tensor", f, waits=w,
                                     inc="T" if (hc == 31 and hh == 1) else None)
                    ptr[id(pw_)].grp = tW2
                    last_w2_T = tW2
                    evF = em("vector", lambda v, tt=tt, e=e, pw_=pw_: v.scalar_tensor_tensor(
                        ACC[:, tt, :], pw_[:], WBUF[:, tt, e:e + 1], ACC[:, tt, :],
                        ALU.mult, ALU.add), waits=[("T", tW2)], inc="V")
                    ptr[id(pw_)].evac = ("V", evF)
                    if e == NE - 1 and tt % 2 == 1:
                        xa = em("gpsimd", lambda g, b=b, tt=tt: g.dma_start(
                            ACC[:, tt - 1:tt + 1, :],
                            dx[b].rearrange("(n p) c -> p n c", p=128)[:, tt - 1:tt + 1, :],
                            accum_op=ALU.add), waits=[("V", evF)], inc="q2", inc_n=16)
                        ov = em("sync", lambda s, b=b, tt=tt: s.dma_start(
                            dout[b].rearrange("(n p) c -> p n c", p=128)[:, tt - 1:tt + 1, :],
                            ACC[:, tt - 1:tt + 1, :]),
                            waits=[("q2", xa)], inc="q0", inc_n=16)
        last["KVseg"] = ("T", last_w2_T)
        last["Bseg"] = ("T", last_w2_T)
        p3_T = r.cnt["T"]

        qf = ("q0", ov)
        tf = ("T", p3_T)
        wf = ("T", w1_last_T)
        last["ACCga"] = qf
        last["XST"] = [wf, wf]
        last["XNT"] = [wf, wf]
        last["XNJ"] = ("q1", spv)
        last["KTM"] = wf
        last["LNSC"] = wf
        last["XNJ2"] = tf
        last["QJ"] = tf
        last["YJ"] = tf
        last["XNJ3"] = [wf, wf]
        last["W1S"] = [wf, wf]

    em("gpsimd", lambda g: g.engine_nop(),
       waits=[("q1", r.cnt["q1"]), ("q0", r.cnt["q0"]), ("q2", r.cnt["q2"])], inc="G")

    from contextlib import ExitStack
    ctx = ExitStack()
    sems = {s: ctx.enter_context(nc.semaphore(s)) for s in
            ("T", "V", "S", "q0", "q1", "q2", "G")}
    with nc.Block() as block:
        @block.tensor
        def _(t):
            r.replay("tensor", t, sems)

        @block.vector
        def _(v):
            r.replay("vector", v, sems)

        @block.scalar
        def _(s):
            r.replay("scalar", s, sems)

        @block.sync
        def _(s):
            r.replay("sync", s, sems)

        @block.gpsimd
        def _(g):
            r.replay("gpsimd", g, sems)
    ctx.close()
    return nc


def make_masks():
    a = np.arange(128)[:, None, None]
    d = np.arange(4)[None, :, None]
    bb = np.arange(512)[None, None, :]
    return (a + 128 * d <= bb).astype(np.float32)


_NC_CACHE = {}


def _get_nc(B):
    if B not in _NC_CACHE:
        _NC_CACHE[B] = build_core_program(B)
    return _NC_CACHE[B]


def kernel(L0_attn_x, L0_attn_w_qkv, L0_attn_b_qkv, L0_attn_w_proj, L0_attn_b_proj,
           L0_ffn_W1, L0_ffn_b1, L0_ffn_W2, L0_ffn_b2, L0_ffn_gate_w, L0_ffn_gate_b,
           L0_ffn_temperature):
    x = np.ascontiguousarray(np.asarray(L0_attn_x, dtype=np.float32))
    nb = x.shape[0]
    B = nb // NCORES
    nc = _get_nc(B)
    common = dict(
        wqkv=np.ascontiguousarray(L0_attn_w_qkv, np.float32),
        bqkv=np.ascontiguousarray(L0_attn_b_qkv, np.float32).reshape(1, 3 * C),
        wproj=np.ascontiguousarray(L0_attn_w_proj, np.float32),
        bproj=np.ascontiguousarray(L0_attn_b_proj, np.float32).reshape(1, C),
        w1=np.ascontiguousarray(L0_ffn_W1, np.float32),
        b1=np.ascontiguousarray(L0_ffn_b1, np.float32),
        w2=np.ascontiguousarray(L0_ffn_W2, np.float32),
        b2=np.ascontiguousarray(L0_ffn_b2, np.float32),
        gw=np.ascontiguousarray(L0_ffn_gate_w, np.float32),
        gb=np.ascontiguousarray(L0_ffn_gate_b, np.float32).reshape(1, NE),
        temp=np.ascontiguousarray(L0_ffn_temperature, np.float32).reshape(1, 1),
        cmask=make_masks(),
        ident=np.eye(128, dtype=np.float32),
    )
    in_maps = []
    for c in range(NCORES):
        m = dict(common)
        m["x"] = x[c * B:(c + 1) * B]
        in_maps.append(m)
    res = run_bass_kernel_spmd(nc, in_maps, core_ids=list(range(NCORES)))
    return np.concatenate([r["out"] for r in res.results], axis=0)


# version 23
# speedup vs baseline: 1.1191x; 1.0114x over previous
import numpy as np
import concourse.bass as bass
import concourse.mybir as mybir
from concourse.bass_utils import run_bass_kernel_spmd

F32 = mybir.dt.float32
BF16 = mybir.dt.bfloat16
U8 = mybir.dt.uint8
AF = mybir.ActivationFunctionType
ALU = mybir.AluOpType
AX = mybir.AxisListType

NCORES = 8
T, C, NH, HD, NE, FF = 2048, 1024, 16, 64, 8, 4096
KI = C // 128
NTT = T // 128
NJ = T // 512
EXP_SHIFT = -12.0


class Rec:
    ENGS = ("tensor", "vector", "scalar", "sync", "gpsimd")

    def __init__(self):
        self.ops = {e: [] for e in self.ENGS}
        self.cnt = {s: 0 for s in ("T", "V", "S", "q0", "q1", "q2", "G")}

    SELF_SEM = {"vector": "V", "scalar": "S"}

    def emit(self, eng, fn, waits=(), inc=None, inc_n=1):
        waits = [w for w in waits if w is not None]
        for s, v in waits:
            assert v <= self.cnt[s], (s, v, self.cnt[s])
        own = self.SELF_SEM.get(eng)
        if own is not None:
            waits.append((own, self.cnt[own]))
            if inc is None:
                inc = own
                inc_n = 1
        post = None
        if inc is not None:
            self.cnt[inc] += inc_n
            post = self.cnt[inc]
        self.ops[eng].append((fn, tuple(waits), inc, inc_n))
        return post

    def replay(self, eng_name, eng, sems):
        waited = {}
        for fn, waits, inc, inc_n in self.ops[eng_name]:
            for s, v in waits:
                if waited.get(s, -1) < v:
                    eng.wait_ge(sems[s], v)
                    waited[s] = v
            ins = fn(eng)
            if inc is not None:
                if isinstance(ins, list):
                    ins = ins[-1]
                ins.then_inc(sems[inc], inc_n)


def build_core_program(B, debug=False):
    nc = bass.Bass("TRN2", target_bir_lowering=False)

    dx = nc.dram_tensor("x", [B, T, C], F32, kind="ExternalInput").ap()
    dwqkv = nc.dram_tensor("wqkv", [C, 3 * C], F32, kind="ExternalInput").ap()
    dbqkv = nc.dram_tensor("bqkv", [1, 3 * C], F32, kind="ExternalInput").ap()
    dwproj = nc.dram_tensor("wproj", [C, C], F32, kind="ExternalInput").ap()
    dbproj = nc.dram_tensor("bproj", [1, C], F32, kind="ExternalInput").ap()
    dw1 = nc.dram_tensor("w1", [NE, C, FF], F32, kind="ExternalInput").ap()
    db1 = nc.dram_tensor("b1", [NE, FF], F32, kind="ExternalInput").ap()
    dw2 = nc.dram_tensor("w2", [NE, FF, C], F32, kind="ExternalInput").ap()
    db2 = nc.dram_tensor("b2", [NE, C], F32, kind="ExternalInput").ap()
    dgw = nc.dram_tensor("gw", [C, NE], F32, kind="ExternalInput").ap()
    dgb = nc.dram_tensor("gb", [1, NE], F32, kind="ExternalInput").ap()
    dtemp = nc.dram_tensor("temp", [1, 1], F32, kind="ExternalInput").ap()
    dmask = nc.dram_tensor("cmask", [128, 4, 512], F32, kind="ExternalInput").ap()
    dident = nc.dram_tensor("ident", [128, 128], F32, kind="ExternalInput").ap()
    dout = nc.dram_tensor("out", [B, T, C], F32, kind="ExternalOutput").ap()
    swqkv = nc.dram_tensor("swqkv", [C, 3 * C], BF16, kind="Internal").ap()
    swproj = nc.dram_tensor("swproj", [C, C], BF16, kind="Internal").ap()
    sw1 = nc.dram_tensor("sw1", [NE, 16, 128, KI, 256], BF16, kind="Internal").ap()
    sw2 = nc.dram_tensor("sw2", [NE, FF, C], BF16, kind="Internal").ap()
    sxns = nc.dram_tensor("sxns", [NJ, 128, KI, 512], BF16, kind="Internal").ap()
    if debug:
        d_wbuf = nc.dram_tensor("d_wbuf", [128, NTT, NE], F32, kind="ExternalOutput").ap()
        d_acc1 = nc.dram_tensor("d_acc1", [128, NTT, C], F32, kind="ExternalOutput").ap()
        d_xn = nc.dram_tensor("d_xn", [NJ, 128, KI, 512], BF16, kind="ExternalOutput").ap()
        d_hb = nc.dram_tensor("d_hb", [128, 32, 512], BF16, kind="ExternalOutput").ap()
        d_kt = nc.dram_tensor("d_kt", [128, 8, T], BF16, kind="ExternalOutput").ap()
        d_vb = nc.dram_tensor("d_vb", [128, NTT, NH, HD + 1], BF16, kind="ExternalOutput").ap()
        d_qj = nc.dram_tensor("d_qj", [128, 8, 512], BF16, kind="ExternalOutput").ap()
        d_stat = nc.dram_tensor("d_stat", [128, 16], F32, kind="ExternalOutput").ap()
        d_xnt = nc.dram_tensor("d_xnt", [128, C], BF16, kind="ExternalOutput").ap()

    TOTAL = 211456
    MAPBASE = 16896
    nc.alloc_sbuf_tensor("PAD", [128, MAPBASE - 16512], U8)
    nc.alloc_sbuf_tensor("ARENA", [128, TOTAL], U8)

    off = [MAPBASE]

    class _Placed:
        __slots__ = ("t", "_at", "_end")

        def __init__(self, t, at, end):
            self.t = t
            self._at = at
            self._end = end

        def __getitem__(self, idx):
            return self.t[idx]

        def ap(self):
            return self.t.ap()

    def place(name, shape, dtype, at=None):
        esz = 2 if dtype == BF16 else 4
        width = int(np.prod(shape[1:])) * esz
        if at is None:
            at = (off[0] + 63) & ~63
            off[0] = at + width
            assert off[0] <= MAPBASE + TOTAL, (name, off[0])
        t = nc.alloc_sbuf_tensor_at(name, list(shape), dtype, offset=at)
        return _Placed(t, at, at + width)

    ACC = place("ACC", [128, NTT, C], F32)
    MASK = place("MASKB", [128, 4, 512], BF16)
    IDB = place("IDB", [128, 128], BF16)
    IDF = place("IDF", [128, 128], F32)
    CONST = place("CONSTS", [128, 8], F32)
    BQKVT = place("BQKVT", [128, 8], F32)
    B1T = place("B1T", [128, NE, 32], F32)
    GWB = place("GWB", [128, KI, NE], BF16)
    GBROW = place("GBROW", [1, NE], BF16)
    BKVROW = place("BKVROW", [1, 2 * C], BF16)
    BPROW = place("BPROW", [1, C], BF16)
    B2B = place("B2B", [NE, C], BF16)
    ONES = place("ONESB", [1, 128], BF16)
    RTEMP = place("RTEMP", [128, 1], F32)
    TTMP = place("TTMP", [1, 1], F32)
    WBUF = place("WBUF", [128, NTT, NE], F32)
    WBB = place("WBB", [128, 8], BF16)
    STAT = place("STATS", [128, 16], F32)
    GTMP = place("GTMP", [128, 32], F32)
    KT = place("KT", [128, 8, T], BF16)
    VB = place("VBV", [128, NTT, NH, HD + 1], BF16)
    W2E = place("W2E", [128, 32, C], BF16, at=KT._at)
    BQR = place("BQR", [1, C], F32, at=KT._at)
    B1R = place("B1R", [NE, FF], F32, at=KT._at)
    assert W2E._end <= VB._end
    WKV = place("WKV", [128, KI, 2 * C], BF16)
    WQQ = place("WQQ", [128, KI, C], BF16, at=WKV._at)
    WPJ = place("WPJ", [128, KI, C], BF16, at=WKV._at + KI * C * 2)
    HBUF = place("HBUF", [128, 32, 512], BF16, at=WKV._at)
    XST = place("XST", [128, 2, C], F32)
    XNT = place("XNT", [128, 2, C], BF16)
    XNJ = place("XNJ", [128, KI, 512], BF16)
    KTM = place("KTM", [128, C], BF16)
    LNSC = place("LNSC", [128, C], BF16)
    XNJ2 = place("XNJ2", [128, KI, 512], BF16, at=XST._at)
    EB = place("EB", [128, 2, 2, 512], BF16, at=XST._at)
    WTJ = place("WTJ", [8, 512], BF16, at=XST._at)
    YTMP = place("YTMP", [64, 2, 512], BF16, at=XST._at + 4096)
    RDEN = place("RDEN", [1, 2, 512], BF16, at=XST._at + 6144)
    QJ = place("QJ", [128, 8, 512], BF16, at=XNT._at)
    YJ = place("YJ", [128, 8, 512], BF16, at=XNT._at + 8192)
    XNJ3 = place("XNJ3", [128, 2, KI, 512], BF16, at=XNT._at)
    W1S = place("W1S", [128, 2, KI, 256], BF16, at=XST._at)
    XRS = place("XRS", [128, 2, C], F32, at=XNT._at)
    ORS = place("ORS", [128, 2, C], F32, at=XNT._at + 8192)
    assert max(LNSC._end, YJ._end, W1S._end, ORS._end) <= MAPBASE + TOTAL

    pA = nc.alloc_psum_tensor("pA", [128, 1024], F32)
    pB = nc.alloc_psum_tensor("pB", [128, 1024], F32)
    pS = nc.alloc_psum_tensor("pS", [128, 1024], F32)
    pE = nc.alloc_psum_tensor("pE", [128, 512], F32)
    pT = nc.alloc_psum_tensor("pT", [128, 1024], BF16)

    r = Rec()

    def em(eng, fn, waits=(), inc=None, inc_n=1):
        return r.emit(eng, fn, waits, inc, inc_n)

    class Tr:
        def __init__(self):
            self.evac = None
            self.grp = 0
    ptr = {id(p): Tr() for p in (pA, pB, pS, pE, pT)}

    def pw(p):
        return ptr[id(p)].evac

    def mm_group(p, mms, extra_waits=(), track=True):
        waits = [w for w in extra_waits if w is not None]
        if track:
            waits.append(pw(p))
        last = None
        for i, fn in enumerate(mms):
            last = em("tensor", fn, waits=waits if i == 0 else (),
                      inc="T" if i == len(mms) - 1 else None)
        if track:
            ptr[id(p)].grp = last
        return last

    em("gpsimd", lambda g: g.memset(ONES[:], 1.0), inc="G")
    em("gpsimd", lambda g: g.memset(CONST[:, 2:3], EXP_SHIFT), inc="G")
    em("gpsimd", lambda g: g.memset(CONST[:, 3:4], 0.125), inc="G")
    em("gpsimd", lambda g: g.memset(CONST[:, 4:5], -0.5), inc="G")
    em("gpsimd", lambda g: g.memset(CONST[:, 5:6], -1.0), inc="G")
    g_done = r.cnt["G"]
    q2 = {}
    q2["ident"] = em("gpsimd", lambda g: g.dma_start(IDB[:], dident), inc="q2", inc_n=16)
    q2["identf"] = em("gpsimd", lambda g: g.dma_start(IDF[:], dident), inc="q2", inc_n=16)
    q2["mask"] = em("gpsimd", lambda g: g.dma_start(MASK[:], dmask), inc="q2", inc_n=16)
    q2["gwb"] = em("gpsimd", lambda g: g.dma_start(
        GWB[:], dgw.rearrange("(ki p) e -> p ki e", p=128)), inc="q2", inc_n=16)
    q2["gb"] = em("gpsimd", lambda g: g.dma_start(GBROW[:], dgb), inc="q2", inc_n=16)
    q2["bkv"] = em("gpsimd", lambda g: g.dma_start(BKVROW[:], dbqkv[:, C:3 * C]),
                   inc="q2", inc_n=16)
    q2["bp"] = em("gpsimd", lambda g: g.dma_start(BPROW[:], dbproj), inc="q2", inc_n=16)
    q2["b2"] = em("gpsimd", lambda g: g.dma_start(B2B[:], db2), inc="q2", inc_n=16)
    q2["temp"] = em("gpsimd", lambda g: g.dma_start(TTMP[:], dtemp), inc="q2", inc_n=16)
    q2["bqr"] = em("gpsimd", lambda g: g.dma_start(BQR[:], dbqkv[:, 0:C]), inc="q2", inc_n=16)
    v_rt = em("vector", lambda v: v.reciprocal(STAT[0:1, 15:16], TTMP[:]),
              waits=[("q2", q2["temp"])], inc="V")
    em("vector", lambda v: v.tensor_copy(WBB[0:1, 0:1], STAT[0:1, 15:16]), inc="V")
    t_rt = em("tensor", lambda t: t.matmul(pE[:, 0:1], ONES[:], WBB[0:1, 0:1],
                                           start=True, stop=True, skip_group_check=True),
              waits=[("V", r.cnt["V"]), ("G", g_done)], inc="T")
    v_rt2 = em("vector", lambda v: v.tensor_copy(RTEMP[:], pE[:, 0:1]),
               waits=[("T", t_rt)], inc="V")
    ptr[id(pE)].evac = ("V", v_rt2)
    tb = None
    for fc in range(8):
        tb = em("tensor", lambda t, fc=fc: t.transpose(
            pE[:, 8 + fc:9 + fc], BQR[0:1, fc * 128:(fc + 1) * 128], IDF[0:1, 0:1]),
            waits=[("q2", q2["bqr"]), ("q2", q2["identf"]), pw(pE)] if fc == 0 else (),
            inc="T" if fc == 7 else None)
    ptr[id(pE)].grp = tb
    v_bq = em("vector", lambda v: v.tensor_copy(BQKVT[:], pE[:, 8:16]),
              waits=[("T", tb)], inc="V")
    ptr[id(pE)].evac = ("V", v_bq)
    q2["b1r"] = em("gpsimd", lambda g: g.dma_start(B1R[:], db1),
                   waits=[("T", tb)], inc="q2", inc_n=16)
    for hc in range(32):
        tb1 = em("tensor", lambda t, hc=hc: t.transpose(
            pE[:, 0:8], B1R[:, hc * 128:(hc + 1) * 128], IDF[0:8, 0:8]),
            waits=[("q2", q2["b1r"]), pw(pE)], inc="T")
        ptr[id(pE)].grp = tb1
        vb1 = em("vector", lambda v, hc=hc: v.tensor_copy(B1T[:, :, hc], pE[:, 0:8]),
                 waits=[("T", tb1)], inc="V")
        ptr[id(pE)].evac = ("V", vb1)
    b1t_done_T = r.cnt["T"]
    v_b1_done = r.cnt["V"]
    q2["wqkv"] = em("gpsimd", lambda g: g.dma_start(swqkv, dwqkv), inc="q2", inc_n=16)
    q2["wproj"] = em("gpsimd", lambda g: g.dma_start(swproj, dwproj), inc="q2", inc_n=16)
    q2_w2, q2_w1 = {}, {}

    def emit_weight_casts(e):
        q2_w2[e] = em("gpsimd", lambda g, e=e: g.dma_start(sw2[e], dw2[e]),
                      inc="q2", inc_n=16)
        for sb in range(16):
            q2_w1[e] = em("gpsimd", lambda g, e=e, sb=sb: g.dma_start(
                sw1[e, sb],
                dw1[e].rearrange("(ki p) (sb f) -> sb p ki f", p=128, f=256)[sb]),
                inc="q2", inc_n=16)

    for e in range(NE):
        emit_weight_casts(e)

    nc.const_aps.register(F32, EXP_SHIFT, CONST[:, 2:3])
    nc.const_aps.register(F32, 0.125, CONST[:, 3:4])
    nc.const_aps.register(F32, -0.5, CONST[:, 4:5])
    nc.const_aps.register(F32, -1.0, CONST[:, 5:6])

    NONE2 = [None, None]
    last = {
        "XST": list(NONE2), "XNT": list(NONE2), "XNJ": None, "KTM": None,
        "LNSC": None, "EB": list(NONE2), "XNJ3": list(NONE2), "W1S": list(NONE2),
        "XRS": list(NONE2), "ORS": list(NONE2), "XNJ2": None, "QJ": None, "ACCga": None,
        "YJ": None, "WTJ": None, "WBB": ("T", t_rt), "KVseg": ("T", b1t_done_T),
        "Bseg": None, "yun": list(NONE2),
    }
    last_w2_T = b1t_done_T
    gW_final = None

    for b in range(B):
        vb_ones = em("vector", lambda v: v.memset(VB[:, :, :, HD:HD + 1], 1.0),
                     waits=[last["KVseg"]], inc="V")
        xv_first = em("scalar", lambda s, b=b: s.dma_start(
            XST[:], dx[b].rearrange("(n p) c -> p n c", p=128)[:, 0:2, :]),
            waits=[last["XST"][0]], inc="q1", inc_n=16)
        wkv_k = []
        for ki in range(KI):
            wkv_k.append(("q1", em("scalar", lambda s, ki=ki: s.dma_start(
                WKV[:, ki, :],
                swqkv.rearrange("(ki p) c -> p ki c", p=128)[:, ki, C:3 * C]),
                waits=[("q2", q2["wqkv"]), last["Bseg"]] if ki == 0 else (),
                inc="q1", inc_n=16)))
        wkv_ready = wkv_k[-1]
        xnj_ready = None
        spv = None
        ktr_pend = None
        for j in range(NJ):
            for tt4 in range(4):
                tt = j * 4 + tt4
                sl = tt % 2
                if sl == 0:
                    if tt == 0:
                        xv = xv_first
                    else:
                        xv = em("scalar", lambda s, b=b, tt=tt: s.dma_start(
                            XST[:], dx[b].rearrange("(n p) c -> p n c", p=128)[:, tt:tt + 2, :]),
                            waits=[last["XST"][0]], inc="q1", inc_n=16)
                    xv_cur = ("q1", xv)
                    em("vector", lambda v: v.reduce_sum(
                        STAT[:, 0:2].rearrange("p c -> p c 1") if False else STAT[:, 0:2],
                        XST[:], AX.X), waits=[xv_cur], inc="V")
                    s_sq0 = em("scalar", lambda s: s.activation(
                        LNSC[:], XST[:, 0, :], AF.Square, accum_out=STAT[:, 2:3]),
                        waits=[xv_cur, last["LNSC"], ("G", g_done)], inc="S")
                    s_sq = em("scalar", lambda s: s.activation(
                        LNSC[:], XST[:, 1, :], AF.Square, accum_out=STAT[:, 3:4]),
                        waits=[("S", s_sq0)], inc="S")
                    em("vector", lambda v: v.tensor_scalar(
                        STAT[:, 4:6], STAT[:, 0:2], 1.0 / C, None, ALU.mult), inc="V")
                    em("vector", lambda v: v.tensor_scalar(
                        STAT[:, 6:8], STAT[:, 2:4], 1.0 / C, 1e-5, ALU.mult, ALU.add),
                        waits=[("S", s_sq)], inc="V")
                    em("vector", lambda v: v.tensor_mul(
                        STAT[:, 8:10], STAT[:, 4:6], STAT[:, 4:6]), inc="V")
                    v5 = em("vector", lambda v: v.tensor_sub(
                        STAT[:, 10:12], STAT[:, 6:8], STAT[:, 8:10]), inc="V")
                    sln = em("scalar", lambda s: s.activation(
                        STAT[:, 12:14], STAT[:, 10:12], AF.Ln),
                        waits=[("V", v5), ("G", g_done)], inc="S")
                    s3 = em("scalar", lambda s: s.activation(
                        STAT[:, 14:16], STAT[:, 12:14], AF.Exp, scale=-0.5),
                        waits=[("S", sln)], inc="S")
                    s3_cur = s3
                xv = xv_cur
                v6 = em("vector", lambda v, sl=sl: v.tensor_scalar(
                    XNT[:, sl, :], XST[:, sl, :], STAT[:, 4 + sl:5 + sl],
                    STAT[:, 14 + sl:15 + sl], ALU.subtract, ALU.mult),
                    waits=[("S", s3_cur), last["XNT"][sl]], inc="V")
                if sl == 1:
                    last["XST"][0] = ("V", v6)
                if debug and b == 0 and tt == 0:
                    em("gpsimd", lambda g: g.dma_start(d_stat, STAT[:]),
                       waits=[("V", v6), ("S", s3)], inc="q2", inc_n=16)
                    em("gpsimd", lambda g: g.dma_start(d_xnt, XNT[:, 0, :]),
                       waits=[("V", v6)], inc="q2", inc_n=16)
                tv = None
                for ki in range(KI):
                    tv = em("tensor", lambda t, ki=ki, sl=sl: t.transpose(
                        pT[:, ki * 128:(ki + 1) * 128],
                        XNT[:, sl, ki * 128:(ki + 1) * 128], IDB[:]),
                        waits=[("V", v6), ("q2", q2["ident"]), pw(pT)] if ki == 0 else (),
                        inc="T" if ki == KI - 1 else None)
                ptr[id(pT)].grp = tv
                last["XNT"][sl] = ("T", tv)
                ev = em("vector", lambda v, tt4=tt4: v.tensor_copy(
                    XNJ[:, :, tt4 * 128:(tt4 + 1) * 128],
                    pT[:].rearrange("p (ki f) -> p ki f", f=128)),
                    waits=[("T", tv)] + ([last["XNJ"]] if tt4 == 0 else []), inc="V")
                ptr[id(pT)].evac = ("V", ev)
                xnj_ready = ("V", ev)
                if ktr_pend is not None:
                    f = ktr_pend
                    ktr_pend = None
                    f()

                def emit_kv(dst, col0, bias_off, tt4, base_w):
                    last_mm = None
                    for hh in range(2):
                        for ki in range(KI):
                            def f(t, ki=ki, hh=hh, dst=dst, col0=col0, tt4=tt4):
                                return t.matmul(
                                    dst[:, hh * 512:(hh + 1) * 512],
                                    XNJ[:, ki, tt4 * 128:(tt4 + 1) * 128],
                                    WKV[:, ki, col0 + hh * 512:col0 + (hh + 1) * 512],
                                    start=(ki == 0), stop=False, skip_group_check=True)
                            w = list(base_w) if (ki == 0 and hh == 0) else []
                            if hh == 0:
                                w.append(wkv_k[ki])
                            last_mm = em("tensor", f, waits=w)
                        def f(t, hh=hh, dst=dst, bias_off=bias_off):
                            return t.matmul(
                                dst[:, hh * 512:(hh + 1) * 512], ONES[:],
                                BKVROW[:, bias_off + hh * 512:bias_off + (hh + 1) * 512],
                                start=False, stop=True, skip_group_check=True)
                        last_mm = em("tensor", f, inc="T" if hh == 1 else None)
                    return last_mm
                tK = emit_kv(pA, 0, 0, tt4,
                             [xnj_ready, ("q2", q2["bkv"]), ("G", g_done)] + (
                                 [pw(pA)] if pw(pA) else []))
                ptr[id(pA)].grp = tK
                tV = emit_kv(pB, C, C, tt4,
                             [xnj_ready] + ([pw(pB)] if pw(pB) else []))
                ptr[id(pB)].grp = tV
                gm = []
                for ki in range(KI):
                    def f(t, ki=ki, tt4=tt4):
                        return t.matmul(pS[:, 0:8], XNJ[:, ki, tt4 * 128:(tt4 + 1) * 128],
                                        GWB[:, ki, :], start=(ki == 0), stop=False,
                                        skip_group_check=True)
                    gm.append(f)
                gm.append(lambda t: t.matmul(pS[:, 0:8], ONES[:], GBROW[:],
                                             start=False, stop=True, skip_group_check=True))
                tG = mm_group(pS, gm, extra_waits=[xnj_ready, ("q2", q2["gwb"]),
                                                   ("q2", q2["gb"])])

                evK = em("vector", lambda v: v.tensor_copy(KTM[:], pA[:]),
                         waits=[("T", tK), last["KTM"]], inc="V")
                ptr[id(pA)].evac = ("V", evK)
                evV = em("vector", lambda v, tt=tt: v.tensor_copy(
                    VB[:, tt, :, 0:HD], pB[:].rearrange("p (h d) -> p h d", d=HD)),
                    waits=[("T", tV), ("V", vb_ones)], inc="V")
                ptr[id(pB)].evac = ("V", evV)
                evG = em("scalar", lambda s, tt=tt: s.activation(
                    WBUF[:, tt, :], pS[:, 0:8], AF.Copy, scale=RTEMP[:, 0:1]),
                    waits=[("T", tG), ("V", v_rt2)], inc="S")
                ptr[id(pS)].evac = ("S", evG)
                def emit_ktr(tt=tt, evK=evK):
                    ktv = None
                    for fc in range(8):
                        ktv = em("tensor", lambda t, fc=fc: t.transpose(
                            pT[:, fc * 128:(fc + 1) * 128],
                            KTM[:, fc * 128:(fc + 1) * 128], IDB[:]),
                            waits=[("V", evK), pw(pT)] if fc == 0 else (),
                            inc="T" if fc == 7 else None)
                    ptr[id(pT)].grp = ktv
                    last["KTM"] = ("T", ktv)
                    evKT = em("vector", lambda v, tt=tt: v.tensor_copy(
                        KT[:, :, tt * 128:(tt + 1) * 128],
                        pT[:].rearrange("p (fc f) -> p fc f", f=128)),
                        waits=[("T", ktv)] + ([last["KVseg"]] if tt == 0 else []),
                        inc="V")
                    ptr[id(pT)].evac = ("V", evKT)
                ktr_pend = emit_ktr
            if j == NJ - 1 and ktr_pend is not None:
                f = ktr_pend
                ktr_pend = None
                f()
            spv = em("scalar", lambda s, j=j: s.dma_start(sxns[j], XNJ[:]),
                     waits=[xnj_ready], inc="q1", inc_n=16)
            last["XNJ"] = ("q1", spv)
        p1_T = r.cnt["T"]
        p1_V = r.cnt["V"]
        p1_S = r.cnt["S"]
        for tt in range(NTT):
            em("vector", lambda v, tt=tt: v.reduce_max(
                STAT[:, 8:9], WBUF[:, tt, :], AX.X), waits=[("S", p1_S)], inc="V")
            em("vector", lambda v, tt=tt: v.tensor_scalar(
                GTMP[:, 8:16], WBUF[:, tt, :], STAT[:, 8:9], None, ALU.is_ge), inc="V")
            em("vector", lambda v, tt=tt: v.scalar_tensor_tensor(
                GTMP[:, 16:24], GTMP[:, 8:16], -1e30, WBUF[:, tt, :],
                ALU.mult, ALU.add), inc="V")
            em("vector", lambda v: v.reduce_max(STAT[:, 9:10], GTMP[:, 16:24], AX.X),
               inc="V")
            g5 = em("vector", lambda v, tt=tt: v.reduce_max(
                STAT[:, 10:11], WBUF[:, tt, :], AX.X, negate=True), inc="V")
            sE = em("scalar", lambda s, tt=tt: s.activation(
                GTMP[:, 24:32], WBUF[:, tt, :], AF.Exp, bias=STAT[:, 10:11]),
                waits=[("V", g5)], inc="S")
            em("vector", lambda v, tt=tt: v.tensor_scalar(
                GTMP[:, 8:16], WBUF[:, tt, :], STAT[:, 9:10], None, ALU.is_ge),
                waits=[("S", sE)], inc="V")
            em("vector", lambda v: v.tensor_mul(
                GTMP[:, 24:32], GTMP[:, 24:32], GTMP[:, 8:16]), inc="V")
            em("vector", lambda v: v.reduce_sum(STAT[:, 11:12], GTMP[:, 24:32], AX.X),
               inc="V")
            em("vector", lambda v: v.reciprocal(STAT[:, 12:13], STAT[:, 11:12]), inc="V")
            em("vector", lambda v, tt=tt: v.tensor_scalar(
                WBUF[:, tt, :], GTMP[:, 24:32], STAT[:, 12:13], None, ALU.mult),
                inc="V")

        qq = em("sync", lambda s: s.dma_start(
            WQQ[:], swqkv.rearrange("(ki p) c -> p ki c", p=128)[:, :, 0:C]),
            waits=[("T", p1_T), ("q2", q2["wqkv"])], inc="q0", inc_n=16)
        qp = em("sync", lambda s: s.dma_start(
            WPJ[:], swproj.rearrange("(ki p) c -> p ki c", p=128)),
            waits=[("q2", q2["wproj"])], inc="q0", inc_n=16)

        vY_last = None
        for j in range(NJ):
            xr = em("scalar", lambda s, j=j: s.dma_start(XNJ2[:], sxns[j]),
                    waits=[("q1", spv), last["XNJ2"]], inc="q1", inc_n=16)
            evQ = None
            for fc in range(8):
                pq = pS if fc % 2 == 0 else pE
                mms = []
                for ki in range(KI):
                    def f(t, ki=ki, fc=fc, pq=pq):
                        return t.matmul(pq[:, 0:512], WQQ[:, ki, fc * 128:(fc + 1) * 128],
                                        XNJ2[:, ki, :], start=(ki == 0), stop=(ki == KI - 1),
                                        skip_group_check=True)
                    mms.append(f)
                tQ = mm_group(pq, mms,
                              extra_waits=[("q1", xr), ("q0", qq), ("V", v_bq)])
                evQ = em("vector", lambda v, fc=fc, pq=pq: v.tensor_scalar(
                    QJ[:, fc, :], pq[:, 0:512], BQKVT[:, fc:fc + 1], None, ALU.add),
                    waits=[("T", tQ)] + ([last["QJ"]] if fc == 0 else []), inc="V")
                ptr[id(pq)].evac = ("V", evQ)
            qj_ready = ("V", evQ)

            nkc = 4 * j + 4
            norm_pend = []

            def emit_norm_a(h, avl):
                ys = h % 2
                lncell = ACC[0:1, 4 * j, 0:512]
                em("scalar", lambda s, ys=ys, lncell=lncell: s.activation(
                    lncell, pA[HD:HD + 1, ys * 512:ys * 512 + 512], AF.Ln),
                    waits=[("T", avl), last["ACCga"]], inc="S")
                vr2 = em("scalar", lambda s, ys=ys, lncell=lncell: s.activation(
                    RDEN[0:1, ys, :], lncell, AF.Exp, scale=-1.0), inc="S")
                return (vr2,)

            def emit_norm_b2(pends):
                tB = None
                w0 = [pw(pB)] + [("S", vr2) for _, vr2 in pends]
                for (h, vr2) in pends:
                    ys = h % 2
                    tB = em("tensor", lambda t, ys=ys: t.matmul(
                        pB[0:HD, ys * 512:ys * 512 + 512], ONES[0:1, 0:HD],
                        RDEN[0:1, ys, :], start=True, stop=True,
                        skip_group_check=True),
                        waits=w0 if ys == 0 else (), inc="T" if ys == 1 else None)
                ptr[id(pB)].grp = tB
                vY = None
                for (h, vr2) in pends:
                    ys = h % 2
                    p0 = ys * 64
                    fc = h // 2
                    sY = em("scalar", lambda s, ys=ys: s.copy(
                        YTMP[:, ys, :], pA[0:HD, ys * 512:ys * 512 + 512]), inc="S")
                    last["yun"][ys] = ("S", sY)
                    vY = em("vector", lambda v, p0=p0, fc=fc, ys=ys: v.tensor_mul(
                        YJ[p0:p0 + HD, fc, :], YTMP[:, ys, :],
                        pB[0:HD, ys * 512:ys * 512 + 512]),
                        waits=[("T", tB), ("S", sY)] + ([last["YJ"]] if h < 2 else []),
                        inc="V")
                ptr[id(pB)].evac = ("V", vY)
                return vY

            for fc in range(NH // 2):
                pS_of = [pS, pB]

                def emit_s2(kc, first_wait):
                    px = pS_of[kc % 2]
                    w = [first_wait, ("V", p1_V)] + ([pw(px)] if pw(px) else [])
                    for half in range(2):
                        def f(t, kc=kc, half=half, px=px, fc=fc):
                            p0 = half * 64
                            return t.matmul(
                                px[:, half * 512:(half + 1) * 512],
                                KT[p0:p0 + 64, fc, kc * 128:(kc + 1) * 128],
                                QJ[p0:p0 + 64, fc, :], start=True, stop=True,
                                skip_group_check=True)
                        tS = em("tensor", f, waits=w if half == 0 else (),
                                inc="T" if half == 1 else None)
                    ptr[id(px)].grp = tS
                    return tS

                def emit_exp2(kc, tS):
                    px = pS_of[kc % 2]
                    es = kc % 2
                    sEx = em("scalar", lambda s, es=es, px=px: s.activation(
                        EB[:, es, :, :].rearrange("p a q -> p (a q)"), px[:],
                        AF.Exp, bias=EXP_SHIFT, scale=0.125),
                        waits=[("T", tS), last["EB"][es], ("V", p1_V), ("G", g_done)],
                        inc="S")
                    ptr[id(px)].evac = ("S", sEx)
                    out = ("S", sEx)
                    d = kc - 4 * j
                    if d >= 0:
                        for half in range(2):
                            mv = em("vector", lambda v, es=es, half=half, d=d: v.tensor_mul(
                                EB[:, es, half, :], EB[:, es, half, :], MASK[:, d, :]),
                                waits=[out, ("q2", q2["mask"])], inc="V")
                            out = ("V", mv)
                    return out

                def emit_av2(kc, eready, fc=fc):
                    es = kc % 2
                    lastmm = None
                    for half in range(2):
                        h = 2 * fc + half
                        wv = [eready]
                        if kc == 0:
                            wv.append(last["yun"][half])
                        def f(t, kc=kc, es=es, half=half, h=h):
                            return t.matmul(
                                pA[0:HD + 1, half * 512:half * 512 + 512],
                                VB[:, kc, h, :], EB[:, es, half, :],
                                start=(kc == 0), stop=(kc == nkc - 1),
                                skip_group_check=True)
                        lastmm = em("tensor", f, waits=wv, inc="T")
                    last["EB"][es] = ("T", lastmm)
                    return lastmm

                er_prev = None
                avl = None
                for kc in range(nkc + 1):
                    if kc < nkc:
                        tS = emit_s2(kc, qj_ready)
                        er = emit_exp2(kc, tS)
                    if kc == 0 and norm_pend:
                        vY_last = emit_norm_b2(norm_pend)
                        norm_pend = []
                    if kc > 0:
                        avl = emit_av2(kc - 1, er_prev)
                    if kc < nkc:
                        er_prev = er
                for half in range(2):
                    h = 2 * fc + half
                    na = emit_norm_a(h, avl)[0]
                    norm_pend.append((h, na))
            if norm_pend:
                vY_last = emit_norm_b2(norm_pend)
            norm_pend = []
            yj_ready = ("V", r.cnt["V"])

            for tt4 in range(4):
                tt = j * 4 + tt4
                vc = em("vector", lambda v, tt=tt: v.tensor_copy(WBB[:], WBUF[:, tt, :]),
                        waits=[last["WBB"]], inc="V")
                tW = em("tensor", lambda t: t.transpose(pT[0:8, 0:128], WBB[:], IDB[:]),
                        waits=[("V", vc), pw(pT)], inc="T")
                ptr[id(pT)].grp = tW
                last["WBB"] = ("T", tW)
                vW = em("vector", lambda v, tt4=tt4: v.tensor_copy(
                    WTJ[0:8, tt4 * 128:(tt4 + 1) * 128], pT[0:8, 0:128]),
                    waits=[("T", tW), last["WTJ"]], inc="V")
                ptr[id(pT)].evac = ("V", vW)
                mms = []
                for hh in range(2):
                    for ki in range(KI):
                        def f(t, ki=ki, hh=hh, tt4=tt4):
                            return t.matmul(pB[:, hh * 512:(hh + 1) * 512],
                                            YJ[:, ki, tt4 * 128:(tt4 + 1) * 128],
                                            WPJ[:, ki, hh * 512:(hh + 1) * 512],
                                            start=(ki == 0), stop=False,
                                            skip_group_check=True)
                        mms.append(f)
                    def f(t, hh=hh):
                        return t.matmul(pB[:, hh * 512:(hh + 1) * 512], ONES[:],
                                        BPROW[:, hh * 512:(hh + 1) * 512],
                                        start=False, stop=False, skip_group_check=True)
                    mms.append(f)
                    def f(t, hh=hh, tt4=tt4):
                        return t.matmul(pB[:, hh * 512:(hh + 1) * 512],
                                        WTJ[0:8, tt4 * 128:(tt4 + 1) * 128],
                                        B2B[:, hh * 512:(hh + 1) * 512],
                                        start=False, stop=(hh == 1), skip_group_check=True)
                    mms.append(f)
                tP = mm_group(pB, mms, extra_waits=[yj_ready, ("V", vW), ("q0", qp),
                                                    ("q2", q2["bp"]), ("q2", q2["b2"])])
                evP = em("vector", lambda v, tt=tt: v.tensor_copy(ACC[:, tt, :], pB[:]),
                         waits=[("T", tP), last["ACCga"]], inc="V")
                ptr[id(pB)].evac = ("V", evP)
            last["QJ"] = ("T", r.cnt["T"])
            last["XNJ2"] = ("T", r.cnt["T"])
            last["YJ"] = ("T", r.cnt["T"])
            last["WTJ"] = ("T", r.cnt["T"])
        p2_T = r.cnt["T"]
        if debug and b == 0:
            em("gpsimd", lambda g: g.dma_start(d_wbuf, WBUF[:]),
               waits=[("V", r.cnt["V"])], inc="q2", inc_n=16)
            em("gpsimd", lambda g: g.dma_start(d_acc1, ACC[:]),
               waits=[("V", r.cnt["V"])], inc="q2", inc_n=16)
            em("gpsimd", lambda g: g.dma_start(d_xn, sxns),
               waits=[("q1", r.cnt["q1"])], inc="q2", inc_n=16)
            em("gpsimd", lambda g: g.dma_start(d_kt, KT[:]),
               waits=[("V", r.cnt["V"])], inc="q2", inc_n=16)
            em("gpsimd", lambda g: g.dma_start(d_vb, VB[:]),
               waits=[("V", r.cnt["V"])], inc="q2", inc_n=16)
            em("gpsimd", lambda g: g.dma_start(d_qj, QJ[:]),
               waits=[("V", r.cnt["V"])], inc="q2", inc_n=16)

        for e in range(NE):
            xr3_pend = {}
            xr3_pend[0] = em("gpsimd", lambda g: g.dma_start(XNJ3[:, 0], sxns[0]),
                             waits=[("q1", spv), last["XNJ3"][0], ("T", p2_T)],
                             inc="q2", inc_n=16)
            w2l = None
            for c8 in range(8):
                w2l = em("gpsimd", lambda g, e=e, c8=c8: g.dma_start(
                    W2E[:, c8 * 4:(c8 + 1) * 4, :],
                    sw2[e].rearrange("(hc p) c -> p hc c", p=128)[:, c8 * 4:(c8 + 1) * 4, :]),
                    waits=[("q2", q2_w2[e]), ("T", p2_T if e == 0 else last_w2_T)]
                    if c8 == 0 else (), inc="q2", inc_n=16)
            w2_ready = w2l
            for j in range(NJ):
                sl3 = j % 2
                xr3 = xr3_pend.pop(j)
                tH = None
                sG = None
                sG_of = {}
                for sb in range(16):
                    ws = sb % 2
                    q1s = em("sync", lambda s, e=e, sb=sb, ws=ws: s.dma_start(
                        W1S[:, ws], sw1[e, sb]),
                        waits=[("q2", q2_w1[e]), last["W1S"][ws], ("T", p2_T)],
                        inc="q0", inc_n=16)
                    for half in range(2):
                        hc = sb * 2 + half
                        ph = pS if hc % 2 == 0 else pE
                        mms = []
                        for ki in range(KI):
                            def f(t, ki=ki, half=half, ws=ws, sl3=sl3, ph=ph):
                                return t.matmul(
                                    ph[:, 0:512], W1S[:, ws, ki, half * 128:(half + 1) * 128],
                                    XNJ3[:, sl3, ki, :], start=(ki == 0),
                                    stop=(ki == KI - 1), skip_group_check=True)
                            mms.append(f)
                        tH = mm_group(ph, mms, extra_waits=[("q0", q1s), ("q2", xr3)])
                        sG = em("scalar", lambda s, e=e, hc=hc, ph=ph: s.activation(
                            HBUF[:, hc, :], ph[:, 0:512], AF.Gelu,
                            bias=B1T[:, e, hc:hc + 1]),
                            waits=[("T", tH), ("V", v_b1_done)], inc="S")
                        sG_of[hc] = sG
                        ptr[id(ph)].evac = ("S", sG)
                    last["W1S"][ws] = ("T", tH)
                    if sb == 0 and j + 1 < NJ:
                        xr3_pend[j + 1] = em(
                            "gpsimd", lambda g, j=j: g.dma_start(
                                XNJ3[:, (j + 1) % 2], sxns[j + 1]),
                            waits=[("q1", spv), last["XNJ3"][(j + 1) % 2]],
                            inc="q2", inc_n=16)
                last["XNJ3"][sl3] = ("T", tH)
                if e == NE - 1 and j == NJ - 1:
                    w1_last_T = tH
                h_ready = ("S", sG)
                if debug and b == 0 and e == 0 and j == 0:
                    em("gpsimd", lambda g: g.dma_start(d_hb, HBUF[:]),
                       waits=[h_ready], inc="q2", inc_n=16)
                for tt4 in range(4):
                    tt = j * 4 + tt4
                    pw_ = pA if tt4 % 2 == 0 else pB
                    base_w = [("q2", w2_ready)] + [w for w in [pw(pw_)] if w]
                    tW2 = None
                    for hh in range(2):
                        for hc in range(32):
                            def f(t, hc=hc, hh=hh, tt4=tt4, pw_=pw_):
                                return t.matmul(
                                    pw_[:, hh * 512:(hh + 1) * 512],
                                    HBUF[:, hc, tt4 * 128:(tt4 + 1) * 128],
                                    W2E[:, hc, hh * 512:(hh + 1) * 512],
                                    start=(hc == 0), stop=(hc == 31 and hh == 1),
                                    skip_group_check=True)
                            w = list(base_w) if hc == 0 and hh == 0 else []
                            if hh == 0:
                                w.append(("S", sG_of[hc]))
                            tW2 = em("tensor", f, waits=w,
                                     inc="T" if (hc == 31 and hh == 1) else None)
                    ptr[id(pw_)].grp = tW2
                    last_w2_T = tW2
                    evF = em("vector", lambda v, tt=tt, e=e, pw_=pw_: v.scalar_tensor_tensor(
                        ACC[:, tt, :], pw_[:], WBUF[:, tt, e:e + 1], ACC[:, tt, :],
                        ALU.mult, ALU.add), waits=[("T", tW2)], inc="V")
                    ptr[id(pw_)].evac = ("V", evF)
                    if e == NE - 1 and tt % 2 == 1:
                        xa = em("gpsimd", lambda g, b=b, tt=tt: g.dma_start(
                            ACC[:, tt - 1:tt + 1, :],
                            dx[b].rearrange("(n p) c -> p n c", p=128)[:, tt - 1:tt + 1, :],
                            accum_op=ALU.add), waits=[("V", evF)], inc="q2", inc_n=16)
                        ov = em("sync", lambda s, b=b, tt=tt: s.dma_start(
                            dout[b].rearrange("(n p) c -> p n c", p=128)[:, tt - 1:tt + 1, :],
                            ACC[:, tt - 1:tt + 1, :]),
                            waits=[("q2", xa)], inc="q0", inc_n=16)
        last["KVseg"] = ("T", last_w2_T)
        last["Bseg"] = ("T", last_w2_T)
        p3_T = r.cnt["T"]

        qf = ("q0", ov)
        tf = ("T", p3_T)
        wf = ("T", w1_last_T)
        last["ACCga"] = qf
        last["XST"] = [wf, wf]
        last["XNT"] = [wf, wf]
        last["XNJ"] = ("q1", spv)
        last["KTM"] = wf
        last["LNSC"] = wf
        last["XNJ2"] = tf
        last["QJ"] = tf
        last["YJ"] = tf
        last["XNJ3"] = [wf, wf]
        last["W1S"] = [wf, wf]

    em("gpsimd", lambda g: g.engine_nop(),
       waits=[("q1", r.cnt["q1"]), ("q0", r.cnt["q0"]), ("q2", r.cnt["q2"])], inc="G")

    from contextlib import ExitStack
    ctx = ExitStack()
    sems = {s: ctx.enter_context(nc.semaphore(s)) for s in
            ("T", "V", "S", "q0", "q1", "q2", "G")}
    with nc.Block() as block:
        @block.tensor
        def _(t):
            r.replay("tensor", t, sems)

        @block.vector
        def _(v):
            r.replay("vector", v, sems)

        @block.scalar
        def _(s):
            r.replay("scalar", s, sems)

        @block.sync
        def _(s):
            r.replay("sync", s, sems)

        @block.gpsimd
        def _(g):
            r.replay("gpsimd", g, sems)
    ctx.close()
    return nc


def make_masks():
    a = np.arange(128)[:, None, None]
    d = np.arange(4)[None, :, None]
    bb = np.arange(512)[None, None, :]
    return (a + 128 * d <= bb).astype(np.float32)


_NC_CACHE = {}


def _get_nc(B):
    if B not in _NC_CACHE:
        _NC_CACHE[B] = build_core_program(B)
    return _NC_CACHE[B]


def kernel(L0_attn_x, L0_attn_w_qkv, L0_attn_b_qkv, L0_attn_w_proj, L0_attn_b_proj,
           L0_ffn_W1, L0_ffn_b1, L0_ffn_W2, L0_ffn_b2, L0_ffn_gate_w, L0_ffn_gate_b,
           L0_ffn_temperature):
    x = np.ascontiguousarray(np.asarray(L0_attn_x, dtype=np.float32))
    nb = x.shape[0]
    B = nb // NCORES
    nc = _get_nc(B)
    common = dict(
        wqkv=np.ascontiguousarray(L0_attn_w_qkv, np.float32),
        bqkv=np.ascontiguousarray(L0_attn_b_qkv, np.float32).reshape(1, 3 * C),
        wproj=np.ascontiguousarray(L0_attn_w_proj, np.float32),
        bproj=np.ascontiguousarray(L0_attn_b_proj, np.float32).reshape(1, C),
        w1=np.ascontiguousarray(L0_ffn_W1, np.float32),
        b1=np.ascontiguousarray(L0_ffn_b1, np.float32),
        w2=np.ascontiguousarray(L0_ffn_W2, np.float32),
        b2=np.ascontiguousarray(L0_ffn_b2, np.float32),
        gw=np.ascontiguousarray(L0_ffn_gate_w, np.float32),
        gb=np.ascontiguousarray(L0_ffn_gate_b, np.float32).reshape(1, NE),
        temp=np.ascontiguousarray(L0_ffn_temperature, np.float32).reshape(1, 1),
        cmask=make_masks(),
        ident=np.eye(128, dtype=np.float32),
    )
    in_maps = []
    for c in range(NCORES):
        m = dict(common)
        m["x"] = x[c * B:(c + 1) * B]
        in_maps.append(m)
    res = run_bass_kernel_spmd(nc, in_maps, core_ids=list(range(NCORES)))
    return np.concatenate([r["out"] for r in res.results], axis=0)


# version 26
# speedup vs baseline: 1.1514x; 1.0289x over previous
import numpy as np
import concourse.bass as bass
import concourse.mybir as mybir
from concourse.bass_utils import run_bass_kernel_spmd

F32 = mybir.dt.float32
BF16 = mybir.dt.bfloat16
U8 = mybir.dt.uint8
AF = mybir.ActivationFunctionType
ALU = mybir.AluOpType
AX = mybir.AxisListType

NCORES = 8
T, C, NH, HD, NE, FF = 2048, 1024, 16, 64, 8, 4096
KI = C // 128
NTT = T // 128
NJ = T // 512
EXP_SHIFT = -12.0


class Rec:
    ENGS = ("tensor", "vector", "scalar", "sync", "gpsimd")

    def __init__(self):
        self.ops = {e: [] for e in self.ENGS}
        self.cnt = {s: 0 for s in ("T", "V", "S", "q0", "q1", "q2", "q3", "G")}

    SELF_SEM = {"vector": "V", "scalar": "S"}

    def emit(self, eng, fn, waits=(), inc=None, inc_n=1):
        waits = [w for w in waits if w is not None]
        for s, v in waits:
            assert v <= self.cnt[s], (s, v, self.cnt[s])
        own = self.SELF_SEM.get(eng)
        if own is not None:
            waits.append((own, self.cnt[own]))
            if inc is None:
                inc = own
                inc_n = 1
        post = None
        if inc is not None:
            self.cnt[inc] += inc_n
            post = self.cnt[inc]
        self.ops[eng].append((fn, tuple(waits), inc, inc_n))
        return post

    def replay(self, eng_name, eng, sems):
        waited = {}
        for fn, waits, inc, inc_n in self.ops[eng_name]:
            for s, v in waits:
                if waited.get(s, -1) < v:
                    eng.wait_ge(sems[s], v)
                    waited[s] = v
            ins = fn(eng)
            if inc is not None:
                if isinstance(ins, list):
                    ins = ins[-1]
                ins.then_inc(sems[inc], inc_n)


def build_core_program(B, debug=False):
    nc = bass.Bass("TRN2", target_bir_lowering=False)

    dx = nc.dram_tensor("x", [B, T, C], F32, kind="ExternalInput").ap()
    dwqkv = nc.dram_tensor("wqkv", [C, 3 * C], F32, kind="ExternalInput").ap()
    dbqkv = nc.dram_tensor("bqkv", [1, 3 * C], F32, kind="ExternalInput").ap()
    dwproj = nc.dram_tensor("wproj", [C, C], F32, kind="ExternalInput").ap()
    dbproj = nc.dram_tensor("bproj", [1, C], F32, kind="ExternalInput").ap()
    dw1 = nc.dram_tensor("w1", [NE, C, FF], F32, kind="ExternalInput").ap()
    db1 = nc.dram_tensor("b1", [NE, FF], F32, kind="ExternalInput").ap()
    dw2 = nc.dram_tensor("w2", [NE, FF, C], F32, kind="ExternalInput").ap()
    db2 = nc.dram_tensor("b2", [NE, C], F32, kind="ExternalInput").ap()
    dgw = nc.dram_tensor("gw", [C, NE], F32, kind="ExternalInput").ap()
    dgb = nc.dram_tensor("gb", [1, NE], F32, kind="ExternalInput").ap()
    dtemp = nc.dram_tensor("temp", [1, 1], F32, kind="ExternalInput").ap()
    dmask = nc.dram_tensor("cmask", [128, 4, 512], F32, kind="ExternalInput").ap()
    dident = nc.dram_tensor("ident", [128, 128], F32, kind="ExternalInput").ap()
    dout = nc.dram_tensor("out", [B, T, C], F32, kind="ExternalOutput").ap()
    swkvr = nc.dram_tensor("swkvr", [128, KI, 2 * C], BF16, kind="Internal").ap()
    swqr = nc.dram_tensor("swqr", [128, KI, C], BF16, kind="Internal").ap()
    swpr = nc.dram_tensor("swpr", [128, KI, C], BF16, kind="Internal").ap()
    sw1 = nc.dram_tensor("sw1", [NE, 16, 128, KI, 256], BF16, kind="Internal").ap()
    sw2r = nc.dram_tensor("sw2r", [NE, 128, 32, C], BF16, kind="Internal").ap()
    sxns = nc.dram_tensor("sxns", [NJ, 128, KI, 512], BF16, kind="Internal").ap()
    if debug:
        d_wbuf = nc.dram_tensor("d_wbuf", [128, NTT, NE], F32, kind="ExternalOutput").ap()
        d_acc1 = nc.dram_tensor("d_acc1", [128, NTT, C], F32, kind="ExternalOutput").ap()
        d_xn = nc.dram_tensor("d_xn", [NJ, 128, KI, 512], BF16, kind="ExternalOutput").ap()
        d_hb = nc.dram_tensor("d_hb", [128, 32, 512], BF16, kind="ExternalOutput").ap()
        d_kt = nc.dram_tensor("d_kt", [128, 8, T], BF16, kind="ExternalOutput").ap()
        d_vb = nc.dram_tensor("d_vb", [128, NTT, NH, HD + 1], BF16, kind="ExternalOutput").ap()
        d_qj = nc.dram_tensor("d_qj", [128, 8, 512], BF16, kind="ExternalOutput").ap()
        d_stat = nc.dram_tensor("d_stat", [128, 16], F32, kind="ExternalOutput").ap()
        d_xnt = nc.dram_tensor("d_xnt", [128, C], BF16, kind="ExternalOutput").ap()

    TOTAL = 211456
    MAPBASE = 16896
    nc.alloc_sbuf_tensor("PAD", [128, MAPBASE - 16512], U8)
    nc.alloc_sbuf_tensor("ARENA", [128, TOTAL], U8)

    off = [MAPBASE]

    class _Placed:
        __slots__ = ("t", "_at", "_end")

        def __init__(self, t, at, end):
            self.t = t
            self._at = at
            self._end = end

        def __getitem__(self, idx):
            return self.t[idx]

        def ap(self):
            return self.t.ap()

    def place(name, shape, dtype, at=None):
        esz = 2 if dtype == BF16 else 4
        width = int(np.prod(shape[1:])) * esz
        if at is None:
            at = (off[0] + 63) & ~63
            off[0] = at + width
            assert off[0] <= MAPBASE + TOTAL, (name, off[0])
        t = nc.alloc_sbuf_tensor_at(name, list(shape), dtype, offset=at)
        return _Placed(t, at, at + width)

    ACC = place("ACC", [128, NTT, C], F32)
    MASK = place("MASKB", [128, 4, 512], BF16)
    IDB = place("IDB", [128, 128], BF16)
    IDF = place("IDF", [128, 128], F32)
    CONST = place("CONSTS", [128, 8], F32)
    BQKVT = place("BQKVT", [128, 8], F32)
    B1T = place("B1T", [128, NE, 32], F32)
    GWB = place("GWB", [128, KI, NE], BF16)
    GBROW = place("GBROW", [1, NE], BF16)
    BKVROW = place("BKVROW", [1, 2 * C], BF16)
    BPROW = place("BPROW", [1, C], BF16)
    B2B = place("B2B", [NE, C], BF16)
    ONES = place("ONESB", [1, 128], BF16)
    RTEMP = place("RTEMP", [128, 1], F32)
    TTMP = place("TTMP", [1, 1], F32)
    WBUF = place("WBUF", [128, NTT, NE], F32)
    WBB = place("WBB", [128, 8], BF16)
    STAT = place("STATS", [128, 16], F32)
    GTMP = place("GTMP", [128, 32], F32)
    KT = place("KT", [128, 8, T], BF16)
    VB = place("VBV", [128, NTT, NH, HD + 1], BF16)
    W2E = place("W2E", [128, 32, C], BF16, at=KT._at)
    BQR = place("BQR", [1, C], F32, at=KT._at)
    B1R = place("B1R", [NE, FF], F32, at=KT._at)
    assert W2E._end <= VB._end
    WKV = place("WKV", [128, KI, 2 * C], BF16)
    WQQ = place("WQQ", [128, KI, C], BF16, at=WKV._at)
    WPJ = place("WPJ", [128, KI, C], BF16, at=WKV._at + KI * C * 2)
    HBUF = place("HBUF", [128, 32, 512], BF16, at=WKV._at)
    XST = place("XST", [128, 2, C], F32)
    XNT = place("XNT", [128, 2, C], BF16)
    XNJ = place("XNJ", [128, KI, 512], BF16)
    KTM = place("KTM", [128, C], BF16)
    LNSC = place("LNSC", [128, C], BF16)
    XNJ2 = place("XNJ2", [128, KI, 512], BF16, at=XST._at)
    EB = place("EB", [128, 2, 2, 512], BF16, at=XST._at)
    WTJ = place("WTJ", [8, 512], BF16, at=XST._at)
    YTMP = place("YTMP", [64, 2, 512], BF16, at=XST._at + 4096)
    RDEN = place("RDEN", [1, 2, 512], BF16, at=XST._at + 6144)
    QJ = place("QJ", [128, 8, 512], BF16, at=XNT._at)
    YJ = place("YJ", [128, 8, 512], BF16, at=XNT._at + 8192)
    XNJ3 = place("XNJ3", [128, 2, KI, 512], BF16, at=XNT._at)
    W1S = place("W1S", [128, 2, KI, 256], BF16, at=XST._at)
    W1SX = place("W1SX", [128, KI, 256], BF16, at=221376)
    assert W1SX._end <= MAPBASE + TOTAL
    XRS = place("XRS", [128, 2, C], F32, at=XNT._at)
    ORS = place("ORS", [128, 2, C], F32, at=XNT._at + 8192)
    assert max(LNSC._end, YJ._end, W1S._end, ORS._end) <= MAPBASE + TOTAL

    pA = nc.alloc_psum_tensor("pA", [128, 1024], F32)
    pB = nc.alloc_psum_tensor("pB", [128, 1024], F32)
    pS = nc.alloc_psum_tensor("pS", [128, 1024], F32)
    pE = nc.alloc_psum_tensor("pE", [128, 512], F32)
    pT = nc.alloc_psum_tensor("pT", [128, 1024], BF16)

    r = Rec()

    def em(eng, fn, waits=(), inc=None, inc_n=1):
        return r.emit(eng, fn, waits, inc, inc_n)

    class Tr:
        def __init__(self):
            self.evac = None
            self.grp = 0
    ptr = {id(p): Tr() for p in (pA, pB, pS, pE, pT)}

    def pw(p):
        return ptr[id(p)].evac

    def mm_group(p, mms, extra_waits=(), track=True):
        waits = [w for w in extra_waits if w is not None]
        if track:
            waits.append(pw(p))
        last = None
        for i, fn in enumerate(mms):
            last = em("tensor", fn, waits=waits if i == 0 else (),
                      inc="T" if i == len(mms) - 1 else None)
        if track:
            ptr[id(p)].grp = last
        return last

    em("gpsimd", lambda g: g.memset(ONES[:], 1.0), inc="G")
    em("gpsimd", lambda g: g.memset(CONST[:, 2:3], EXP_SHIFT), inc="G")
    em("gpsimd", lambda g: g.memset(CONST[:, 3:4], 0.125), inc="G")
    em("gpsimd", lambda g: g.memset(CONST[:, 4:5], -0.5), inc="G")
    em("gpsimd", lambda g: g.memset(CONST[:, 5:6], -1.0), inc="G")
    g_done = r.cnt["G"]
    q2 = {}
    q2["ident"] = em("gpsimd", lambda g: g.dma_start(IDB[:], dident), inc="q2", inc_n=16)
    q2["identf"] = em("gpsimd", lambda g: g.dma_start(IDF[:], dident), inc="q2", inc_n=16)
    q2["mask"] = em("gpsimd", lambda g: g.dma_start(MASK[:], dmask), inc="q2", inc_n=16)
    q2["gwb"] = em("gpsimd", lambda g: g.dma_start(
        GWB[:], dgw.rearrange("(ki p) e -> p ki e", p=128)), inc="q2", inc_n=16)
    q2["gb"] = em("gpsimd", lambda g: g.dma_start(GBROW[:], dgb), inc="q2", inc_n=16)
    q2["bkv"] = em("gpsimd", lambda g: g.dma_start(BKVROW[:], dbqkv[:, C:3 * C]),
                   inc="q2", inc_n=16)
    q2["bp"] = em("gpsimd", lambda g: g.dma_start(BPROW[:], dbproj), inc="q2", inc_n=16)
    q2["b2"] = em("gpsimd", lambda g: g.dma_start(B2B[:], db2), inc="q2", inc_n=16)
    q2["temp"] = em("gpsimd", lambda g: g.dma_start(TTMP[:], dtemp), inc="q2", inc_n=16)
    q2["bqr"] = em("gpsimd", lambda g: g.dma_start(BQR[:], dbqkv[:, 0:C]), inc="q2", inc_n=16)
    v_rt = em("vector", lambda v: v.reciprocal(STAT[0:1, 15:16], TTMP[:]),
              waits=[("q2", q2["temp"])], inc="V")
    em("vector", lambda v: v.tensor_copy(WBB[0:1, 0:1], STAT[0:1, 15:16]), inc="V")
    t_rt = em("tensor", lambda t: t.matmul(pE[:, 0:1], ONES[:], WBB[0:1, 0:1],
                                           start=True, stop=True, skip_group_check=True),
              waits=[("V", r.cnt["V"]), ("G", g_done)], inc="T")
    v_rt2 = em("vector", lambda v: v.tensor_copy(RTEMP[:], pE[:, 0:1]),
               waits=[("T", t_rt)], inc="V")
    ptr[id(pE)].evac = ("V", v_rt2)
    tb = None
    for fc in range(8):
        tb = em("tensor", lambda t, fc=fc: t.transpose(
            pE[:, 8 + fc:9 + fc], BQR[0:1, fc * 128:(fc + 1) * 128], IDF[0:1, 0:1]),
            waits=[("q2", q2["bqr"]), ("q2", q2["identf"]), pw(pE)] if fc == 0 else (),
            inc="T" if fc == 7 else None)
    ptr[id(pE)].grp = tb
    v_bq = em("vector", lambda v: v.tensor_copy(BQKVT[:], pE[:, 8:16]),
              waits=[("T", tb)], inc="V")
    ptr[id(pE)].evac = ("V", v_bq)
    q2["b1r"] = em("gpsimd", lambda g: g.dma_start(B1R[:], db1),
                   waits=[("T", tb)], inc="q2", inc_n=16)
    for hc in range(32):
        tb1 = em("tensor", lambda t, hc=hc: t.transpose(
            pE[:, 0:8], B1R[:, hc * 128:(hc + 1) * 128], IDF[0:8, 0:8]),
            waits=[("q2", q2["b1r"]), pw(pE)], inc="T")
        ptr[id(pE)].grp = tb1
        vb1 = em("vector", lambda v, hc=hc: v.tensor_copy(B1T[:, :, hc], pE[:, 0:8]),
                 waits=[("T", tb1)], inc="V")
        ptr[id(pE)].evac = ("V", vb1)
    b1t_done_T = r.cnt["T"]
    v_b1_done = r.cnt["V"]
    q2["wkvr"] = em("gpsimd", lambda g: g.dma_start(
        swkvr, dwqkv.rearrange("(ki p) c -> p ki c", p=128)[:, :, C:3 * C]),
        inc="q2", inc_n=16)
    q2["wqr"] = em("gpsimd", lambda g: g.dma_start(
        swqr, dwqkv.rearrange("(ki p) c -> p ki c", p=128)[:, :, 0:C]),
        inc="q2", inc_n=16)
    q2["wpr"] = em("gpsimd", lambda g: g.dma_start(
        swpr, dwproj.rearrange("(ki p) c -> p ki c", p=128)), inc="q2", inc_n=16)
    q2_w2, q2_w1 = {}, {}

    def emit_weight_casts(e):
        q2_w2[e] = em("gpsimd", lambda g, e=e: g.dma_start(
            sw2r[e], dw2[e].rearrange("(hc p) c -> p hc c", p=128)),
            inc="q2", inc_n=16)
        for sb in range(16):
            q2_w1[e] = em("gpsimd", lambda g, e=e, sb=sb: g.dma_start(
                sw1[e, sb],
                dw1[e].rearrange("(ki p) (sb f) -> sb p ki f", p=128, f=256)[sb]),
                inc="q2", inc_n=16)

    for e in range(NE):
        emit_weight_casts(e)

    nc.const_aps.register(F32, EXP_SHIFT, CONST[:, 2:3])
    nc.const_aps.register(F32, 0.125, CONST[:, 3:4])
    nc.const_aps.register(F32, -0.5, CONST[:, 4:5])
    nc.const_aps.register(F32, -1.0, CONST[:, 5:6])

    NONE2 = [None, None]
    last = {
        "XST": list(NONE2), "XNT": list(NONE2), "XNJ": None, "KTM": None,
        "LNSC": None, "EB": list(NONE2), "XNJ3": list(NONE2),
        "W1S": [None, None, None],
        "XRS": list(NONE2), "ORS": list(NONE2), "XNJ2": None, "QJ": None, "ACCga": None,
        "YJ": None, "WTJ": None, "WBB": ("T", t_rt), "KVseg": ("T", b1t_done_T),
        "Bseg": None, "yun": list(NONE2),
    }
    last_w2_T = b1t_done_T
    w1c = [0]
    gW_final = None

    for b in range(B):
        vb_ones = em("vector", lambda v: v.memset(VB[:, :, :, HD:HD + 1], 1.0),
                     waits=[last["KVseg"]], inc="V")
        tt0 = 0
        xv_first = em("scalar", lambda s, b=b: s.dma_start(
            XST[:], dx[b].rearrange("(n p) c -> p n c", p=128)[:, 0:2, :]),
            waits=[last["XST"][0]], inc="q1", inc_n=16)
        wkv_k4 = []
        for kk in range(4):
            w3 = em("sync", lambda s, kk=kk: s.dma_start(
                WKV[:, 2 * kk:2 * kk + 2, :], swkvr[:, 2 * kk:2 * kk + 2, :]),
                waits=[("q2", q2["wkvr"]), last["Bseg"]] if kk == 0 else (),
                inc="q3", inc_n=16)
            wkv_k4.append(("q3", w3))
        wkv_k = [wkv_k4[ki // 2] for ki in range(KI)]
        wkv_ready = wkv_k[-1]
        xnj_ready = None
        spv = None
        ktr_pend = None
        for j in range(NJ):
            for tt4 in range(4):
                tt = j * 4 + tt4
                sl = tt % 2
                if sl == 0:
                    if tt == tt0:
                        xv = xv_first
                    else:
                        xv = em("scalar", lambda s, b=b, tt=tt: s.dma_start(
                            XST[:], dx[b].rearrange("(n p) c -> p n c", p=128)[:, tt:tt + 2, :]),
                            waits=[last["XST"][0]], inc="q1", inc_n=16)
                    xv_cur = ("q1", xv)
                    em("vector", lambda v: v.reduce_sum(
                        STAT[:, 0:2].rearrange("p c -> p c 1") if False else STAT[:, 0:2],
                        XST[:], AX.X), waits=[xv_cur], inc="V")
                    s_sq0 = em("scalar", lambda s: s.activation(
                        LNSC[:], XST[:, 0, :], AF.Square, accum_out=STAT[:, 2:3]),
                        waits=[xv_cur, last["LNSC"], ("G", g_done)], inc="S")
                    s_sq = em("scalar", lambda s: s.activation(
                        LNSC[:], XST[:, 1, :], AF.Square, accum_out=STAT[:, 3:4]),
                        waits=[("S", s_sq0)], inc="S")
                    em("vector", lambda v: v.tensor_scalar(
                        STAT[:, 4:6], STAT[:, 0:2], 1.0 / C, None, ALU.mult), inc="V")
                    em("vector", lambda v: v.tensor_scalar(
                        STAT[:, 6:8], STAT[:, 2:4], 1.0 / C, 1e-5, ALU.mult, ALU.add),
                        waits=[("S", s_sq)], inc="V")
                    em("vector", lambda v: v.tensor_mul(
                        STAT[:, 8:10], STAT[:, 4:6], STAT[:, 4:6]), inc="V")
                    v5 = em("vector", lambda v: v.tensor_sub(
                        STAT[:, 10:12], STAT[:, 6:8], STAT[:, 8:10]), inc="V")
                    sln = em("scalar", lambda s: s.activation(
                        STAT[:, 12:14], STAT[:, 10:12], AF.Ln),
                        waits=[("V", v5), ("G", g_done)], inc="S")
                    s3 = em("scalar", lambda s: s.activation(
                        STAT[:, 14:16], STAT[:, 12:14], AF.Exp, scale=-0.5),
                        waits=[("S", sln)], inc="S")
                    s3_cur = s3
                xv = xv_cur
                v6 = em("vector", lambda v, sl=sl: v.tensor_scalar(
                    XNT[:, sl, :], XST[:, sl, :], STAT[:, 4 + sl:5 + sl],
                    STAT[:, 14 + sl:15 + sl], ALU.subtract, ALU.mult),
                    waits=[("S", s3_cur), last["XNT"][sl]], inc="V")
                if sl == 1:
                    last["XST"][0] = ("V", v6)
                if debug and b == 0 and tt == 0:
                    em("gpsimd", lambda g: g.dma_start(d_stat, STAT[:]),
                       waits=[("V", v6), ("S", s3)], inc="q2", inc_n=16)
                    em("gpsimd", lambda g: g.dma_start(d_xnt, XNT[:, 0, :]),
                       waits=[("V", v6)], inc="q2", inc_n=16)
                tv = None
                for ki in range(KI):
                    tv = em("tensor", lambda t, ki=ki, sl=sl: t.transpose(
                        pT[:, ki * 128:(ki + 1) * 128],
                        XNT[:, sl, ki * 128:(ki + 1) * 128], IDB[:]),
                        waits=[("V", v6), ("q2", q2["ident"]), pw(pT)] if ki == 0 else (),
                        inc="T" if ki == KI - 1 else None)
                ptr[id(pT)].grp = tv
                last["XNT"][sl] = ("T", tv)
                ev = em("vector", lambda v, tt4=tt4: v.tensor_copy(
                    XNJ[:, :, tt4 * 128:(tt4 + 1) * 128],
                    pT[:].rearrange("p (ki f) -> p ki f", f=128)),
                    waits=[("T", tv)] + ([last["XNJ"]] if tt4 == 0 else []), inc="V")
                ptr[id(pT)].evac = ("V", ev)
                xnj_ready = ("V", ev)
                if ktr_pend is not None:
                    f = ktr_pend
                    ktr_pend = None
                    f()

                def emit_kv(dst, col0, bias_off, tt4, base_w):
                    last_mm = None
                    for hh in range(2):
                        for ki in range(KI):
                            def f(t, ki=ki, hh=hh, dst=dst, col0=col0, tt4=tt4):
                                return t.matmul(
                                    dst[:, hh * 512:(hh + 1) * 512],
                                    XNJ[:, ki, tt4 * 128:(tt4 + 1) * 128],
                                    WKV[:, ki, col0 + hh * 512:col0 + (hh + 1) * 512],
                                    start=(ki == 0), stop=False, skip_group_check=True)
                            w = list(base_w) if (ki == 0 and hh == 0) else []
                            if hh == 0:
                                w.append(wkv_k[ki])
                            last_mm = em("tensor", f, waits=w)
                        def f(t, hh=hh, dst=dst, bias_off=bias_off):
                            return t.matmul(
                                dst[:, hh * 512:(hh + 1) * 512], ONES[:],
                                BKVROW[:, bias_off + hh * 512:bias_off + (hh + 1) * 512],
                                start=False, stop=True, skip_group_check=True)
                        last_mm = em("tensor", f, inc="T" if hh == 1 else None)
                    return last_mm
                tK = emit_kv(pA, 0, 0, tt4,
                             [xnj_ready, ("q2", q2["bkv"]), ("G", g_done)] + (
                                 [pw(pA)] if pw(pA) else []))
                ptr[id(pA)].grp = tK
                tV = emit_kv(pB, C, C, tt4,
                             [xnj_ready] + ([pw(pB)] if pw(pB) else []))
                ptr[id(pB)].grp = tV
                gm = []
                for ki in range(KI):
                    def f(t, ki=ki, tt4=tt4):
                        return t.matmul(pS[:, 0:8], XNJ[:, ki, tt4 * 128:(tt4 + 1) * 128],
                                        GWB[:, ki, :], start=(ki == 0), stop=False,
                                        skip_group_check=True)
                    gm.append(f)
                gm.append(lambda t: t.matmul(pS[:, 0:8], ONES[:], GBROW[:],
                                             start=False, stop=True, skip_group_check=True))
                tG = mm_group(pS, gm, extra_waits=[xnj_ready, ("q2", q2["gwb"]),
                                                   ("q2", q2["gb"])])

                evK = em("vector", lambda v: v.tensor_copy(KTM[:], pA[:]),
                         waits=[("T", tK), last["KTM"]], inc="V")
                ptr[id(pA)].evac = ("V", evK)
                evV = em("vector", lambda v, tt=tt: v.tensor_copy(
                    VB[:, tt, :, 0:HD], pB[:].rearrange("p (h d) -> p h d", d=HD)),
                    waits=[("T", tV), ("V", vb_ones)], inc="V")
                ptr[id(pB)].evac = ("V", evV)
                evG = em("scalar", lambda s, tt=tt: s.activation(
                    WBUF[:, tt, :], pS[:, 0:8], AF.Copy, scale=RTEMP[:, 0:1]),
                    waits=[("T", tG), ("V", v_rt2)], inc="S")
                ptr[id(pS)].evac = ("S", evG)
                def emit_ktr(tt=tt, evK=evK):
                    ktv = None
                    for fc in range(8):
                        ktv = em("tensor", lambda t, fc=fc: t.transpose(
                            pT[:, fc * 128:(fc + 1) * 128],
                            KTM[:, fc * 128:(fc + 1) * 128], IDB[:]),
                            waits=[("V", evK), pw(pT)] if fc == 0 else (),
                            inc="T" if fc == 7 else None)
                    ptr[id(pT)].grp = ktv
                    last["KTM"] = ("T", ktv)
                    evKT = em("vector", lambda v, tt=tt: v.tensor_copy(
                        KT[:, :, tt * 128:(tt + 1) * 128],
                        pT[:].rearrange("p (fc f) -> p fc f", f=128)),
                        waits=[("T", ktv)] + ([last["KVseg"]] if tt == 0 else []),
                        inc="V")
                    ptr[id(pT)].evac = ("V", evKT)
                ktr_pend = emit_ktr
            if j == NJ - 1 and ktr_pend is not None:
                f = ktr_pend
                ktr_pend = None
                f()
            spv = em("scalar", lambda s, j=j: s.dma_start(sxns[j], XNJ[:]),
                     waits=[xnj_ready], inc="q1", inc_n=16)
            last["XNJ"] = ("q1", spv)
        p1_T = r.cnt["T"]
        p1_V = r.cnt["V"]
        p1_S = r.cnt["S"]
        for tt in range(NTT):
            em("vector", lambda v, tt=tt: v.reduce_max(
                STAT[:, 8:9], WBUF[:, tt, :], AX.X), waits=[("S", p1_S)], inc="V")
            em("vector", lambda v, tt=tt: v.tensor_scalar(
                GTMP[:, 8:16], WBUF[:, tt, :], STAT[:, 8:9], None, ALU.is_ge), inc="V")
            em("vector", lambda v, tt=tt: v.scalar_tensor_tensor(
                GTMP[:, 16:24], GTMP[:, 8:16], -1e30, WBUF[:, tt, :],
                ALU.mult, ALU.add), inc="V")
            em("vector", lambda v: v.reduce_max(STAT[:, 9:10], GTMP[:, 16:24], AX.X),
               inc="V")
            g5 = em("vector", lambda v, tt=tt: v.reduce_max(
                STAT[:, 10:11], WBUF[:, tt, :], AX.X, negate=True), inc="V")
            sE = em("scalar", lambda s, tt=tt: s.activation(
                GTMP[:, 24:32], WBUF[:, tt, :], AF.Exp, bias=STAT[:, 10:11]),
                waits=[("V", g5)], inc="S")
            em("vector", lambda v, tt=tt: v.tensor_scalar(
                GTMP[:, 8:16], WBUF[:, tt, :], STAT[:, 9:10], None, ALU.is_ge),
                waits=[("S", sE)], inc="V")
            em("vector", lambda v: v.tensor_mul(
                GTMP[:, 24:32], GTMP[:, 24:32], GTMP[:, 8:16]), inc="V")
            em("vector", lambda v: v.reduce_sum(STAT[:, 11:12], GTMP[:, 24:32], AX.X),
               inc="V")
            em("vector", lambda v: v.reciprocal(STAT[:, 12:13], STAT[:, 11:12]), inc="V")
            em("vector", lambda v, tt=tt: v.tensor_scalar(
                WBUF[:, tt, :], GTMP[:, 24:32], STAT[:, 12:13], None, ALU.mult),
                inc="V")

        qq = em("sync", lambda s: s.dma_start(WQQ[:], swqr),
                waits=[("T", p1_T), ("q2", q2["wqr"])], inc="q0", inc_n=16)
        qp = em("sync", lambda s: s.dma_start(WPJ[:], swpr),
                waits=[("q2", q2["wpr"])], inc="q0", inc_n=16)

        vY_last = None
        for j in range(NJ):
            xr = em("scalar", lambda s, j=j: s.dma_start(XNJ2[:], sxns[j]),
                    waits=[("q1", spv), last["XNJ2"]], inc="q1", inc_n=16)
            evQ = None
            for fc in range(8):
                pq = pS if fc % 2 == 0 else pE
                mms = []
                for ki in range(KI):
                    def f(t, ki=ki, fc=fc, pq=pq):
                        return t.matmul(pq[:, 0:512], WQQ[:, ki, fc * 128:(fc + 1) * 128],
                                        XNJ2[:, ki, :], start=(ki == 0), stop=(ki == KI - 1),
                                        skip_group_check=True)
                    mms.append(f)
                tQ = mm_group(pq, mms,
                              extra_waits=[("q1", xr), ("q0", qq), ("V", v_bq)])
                evQ = em("vector", lambda v, fc=fc, pq=pq: v.tensor_scalar(
                    QJ[:, fc, :], pq[:, 0:512], BQKVT[:, fc:fc + 1], None, ALU.add),
                    waits=[("T", tQ)] + ([last["QJ"]] if fc == 0 else []), inc="V")
                ptr[id(pq)].evac = ("V", evQ)
            qj_ready = ("V", evQ)

            nkc = 4 * j + 4
            norm_pend = []

            def emit_norm_a(h, avl):
                ys = h % 2
                lncell = ACC[0:1, 4 * j, 0:512]
                em("scalar", lambda s, ys=ys, lncell=lncell: s.activation(
                    lncell, pA[HD:HD + 1, ys * 512:ys * 512 + 512], AF.Ln),
                    waits=[("T", avl), last["ACCga"]], inc="S")
                vr2 = em("scalar", lambda s, ys=ys, lncell=lncell: s.activation(
                    RDEN[0:1, ys, :], lncell, AF.Exp, scale=-1.0), inc="S")
                return (vr2,)

            def emit_norm_b2(pends):
                tB = None
                w0 = [pw(pB)] + [("S", vr2) for _, vr2 in pends]
                for (h, vr2) in pends:
                    ys = h % 2
                    tB = em("tensor", lambda t, ys=ys: t.matmul(
                        pB[0:HD, ys * 512:ys * 512 + 512], ONES[0:1, 0:HD],
                        RDEN[0:1, ys, :], start=True, stop=True,
                        skip_group_check=True),
                        waits=w0 if ys == 0 else (), inc="T" if ys == 1 else None)
                ptr[id(pB)].grp = tB
                vY = None
                for (h, vr2) in pends:
                    ys = h % 2
                    p0 = ys * 64
                    fc = h // 2
                    sY = em("scalar", lambda s, ys=ys: s.copy(
                        YTMP[:, ys, :], pA[0:HD, ys * 512:ys * 512 + 512]), inc="S")
                    last["yun"][ys] = ("S", sY)
                    vY = em("vector", lambda v, p0=p0, fc=fc, ys=ys: v.tensor_mul(
                        YJ[p0:p0 + HD, fc, :], YTMP[:, ys, :],
                        pB[0:HD, ys * 512:ys * 512 + 512]),
                        waits=[("T", tB), ("S", sY)] + ([last["YJ"]] if h < 2 else []),
                        inc="V")
                ptr[id(pB)].evac = ("V", vY)
                return vY

            for fc in range(NH // 2):
                pS_of = [pS, pB]

                def emit_s2(kc, first_wait):
                    px = pS_of[kc % 2]
                    w = [first_wait, ("V", p1_V)] + ([pw(px)] if pw(px) else [])
                    for half in range(2):
                        def f(t, kc=kc, half=half, px=px, fc=fc):
                            p0 = half * 64
                            return t.matmul(
                                px[:, half * 512:(half + 1) * 512],
                                KT[p0:p0 + 64, fc, kc * 128:(kc + 1) * 128],
                                QJ[p0:p0 + 64, fc, :], start=True, stop=True,
                                skip_group_check=True)
                        tS = em("tensor", f, waits=w if half == 0 else (),
                                inc="T" if half == 1 else None)
                    ptr[id(px)].grp = tS
                    return tS

                def emit_exp2(kc, tS):
                    px = pS_of[kc % 2]
                    es = kc % 2
                    sEx = em("scalar", lambda s, es=es, px=px: s.activation(
                        EB[:, es, :, :].rearrange("p a q -> p (a q)"), px[:],
                        AF.Exp, bias=EXP_SHIFT, scale=0.125),
                        waits=[("T", tS), last["EB"][es], ("V", p1_V), ("G", g_done)],
                        inc="S")
                    ptr[id(px)].evac = ("S", sEx)
                    out = ("S", sEx)
                    d = kc - 4 * j
                    if d >= 0:
                        for half in range(2):
                            mv = em("vector", lambda v, es=es, half=half, d=d: v.tensor_mul(
                                EB[:, es, half, :], EB[:, es, half, :], MASK[:, d, :]),
                                waits=[out, ("q2", q2["mask"])], inc="V")
                            out = ("V", mv)
                    return out

                def emit_av2(kc, eready, fc=fc):
                    es = kc % 2
                    lastmm = None
                    for half in range(2):
                        h = 2 * fc + half
                        wv = [eready]
                        if kc == 0:
                            wv.append(last["yun"][half])
                        def f(t, kc=kc, es=es, half=half, h=h):
                            return t.matmul(
                                pA[0:HD + 1, half * 512:half * 512 + 512],
                                VB[:, kc, h, :], EB[:, es, half, :],
                                start=(kc == 0), stop=(kc == nkc - 1),
                                skip_group_check=True)
                        lastmm = em("tensor", f, waits=wv, inc="T")
                    last["EB"][es] = ("T", lastmm)
                    return lastmm

                er_prev = None
                avl = None
                for kc in range(nkc + 1):
                    if kc < nkc:
                        tS = emit_s2(kc, qj_ready)
                        er = emit_exp2(kc, tS)
                    if kc == 0 and norm_pend:
                        vY_last = emit_norm_b2(norm_pend)
                        norm_pend = []
                    if kc > 0:
                        avl = emit_av2(kc - 1, er_prev)
                    if kc < nkc:
                        er_prev = er
                for half in range(2):
                    h = 2 * fc + half
                    na = emit_norm_a(h, avl)[0]
                    norm_pend.append((h, na))
            if norm_pend:
                vY_last = emit_norm_b2(norm_pend)
            norm_pend = []
            yj_ready = ("V", r.cnt["V"])

            for tt4 in range(4):
                tt = j * 4 + tt4
                vc = em("vector", lambda v, tt=tt: v.tensor_copy(WBB[:], WBUF[:, tt, :]),
                        waits=[last["WBB"]], inc="V")
                tW = em("tensor", lambda t: t.transpose(pT[0:8, 0:128], WBB[:], IDB[:]),
                        waits=[("V", vc), pw(pT)], inc="T")
                ptr[id(pT)].grp = tW
                last["WBB"] = ("T", tW)
                vW = em("vector", lambda v, tt4=tt4: v.tensor_copy(
                    WTJ[0:8, tt4 * 128:(tt4 + 1) * 128], pT[0:8, 0:128]),
                    waits=[("T", tW), last["WTJ"]], inc="V")
                ptr[id(pT)].evac = ("V", vW)
                mms = []
                for hh in range(2):
                    for ki in range(KI):
                        def f(t, ki=ki, hh=hh, tt4=tt4):
                            return t.matmul(pB[:, hh * 512:(hh + 1) * 512],
                                            YJ[:, ki, tt4 * 128:(tt4 + 1) * 128],
                                            WPJ[:, ki, hh * 512:(hh + 1) * 512],
                                            start=(ki == 0), stop=False,
                                            skip_group_check=True)
                        mms.append(f)
                    def f(t, hh=hh):
                        return t.matmul(pB[:, hh * 512:(hh + 1) * 512], ONES[:],
                                        BPROW[:, hh * 512:(hh + 1) * 512],
                                        start=False, stop=False, skip_group_check=True)
                    mms.append(f)
                    def f(t, hh=hh, tt4=tt4):
                        return t.matmul(pB[:, hh * 512:(hh + 1) * 512],
                                        WTJ[0:8, tt4 * 128:(tt4 + 1) * 128],
                                        B2B[:, hh * 512:(hh + 1) * 512],
                                        start=False, stop=(hh == 1), skip_group_check=True)
                    mms.append(f)
                tP = mm_group(pB, mms, extra_waits=[yj_ready, ("V", vW), ("q0", qp),
                                                    ("q2", q2["bp"]), ("q2", q2["b2"])])
                evP = em("vector", lambda v, tt=tt: v.tensor_copy(ACC[:, tt, :], pB[:]),
                         waits=[("T", tP), last["ACCga"]], inc="V")
                ptr[id(pB)].evac = ("V", evP)
            last["QJ"] = ("T", r.cnt["T"])
            last["XNJ2"] = ("T", r.cnt["T"])
            last["YJ"] = ("T", r.cnt["T"])
            last["WTJ"] = ("T", r.cnt["T"])
        p2_T = r.cnt["T"]
        if debug and b == 0:
            em("gpsimd", lambda g: g.dma_start(d_wbuf, WBUF[:]),
               waits=[("V", r.cnt["V"])], inc="q2", inc_n=16)
            em("gpsimd", lambda g: g.dma_start(d_acc1, ACC[:]),
               waits=[("V", r.cnt["V"])], inc="q2", inc_n=16)
            em("gpsimd", lambda g: g.dma_start(d_xn, sxns),
               waits=[("q1", r.cnt["q1"])], inc="q2", inc_n=16)
            em("gpsimd", lambda g: g.dma_start(d_kt, KT[:]),
               waits=[("V", r.cnt["V"])], inc="q2", inc_n=16)
            em("gpsimd", lambda g: g.dma_start(d_vb, VB[:]),
               waits=[("V", r.cnt["V"])], inc="q2", inc_n=16)
            em("gpsimd", lambda g: g.dma_start(d_qj, QJ[:]),
               waits=[("V", r.cnt["V"])], inc="q2", inc_n=16)

        for e in range(NE):
            xr3_pend = {}
            xr3_pend[0] = em("gpsimd", lambda g: g.dma_start(XNJ3[:, 0], sxns[0]),
                             waits=[("q1", spv), last["XNJ3"][0], ("T", p2_T)],
                             inc="q2", inc_n=16)
            w2l = None
            for c8 in range(8):
                w2l = em("gpsimd", lambda g, e=e, c8=c8: g.dma_start(
                    W2E[:, c8 * 4:(c8 + 1) * 4, :],
                    sw2r[e][:, c8 * 4:(c8 + 1) * 4, :]),
                    waits=[("q2", q2_w2[e]), ("T", p2_T if e == 0 else last_w2_T)]
                    if c8 == 0 else (), inc="q2", inc_n=16)
            w2_ready = w2l
            for j in range(NJ):
                sl3 = j % 2
                xr3 = xr3_pend.pop(j)
                tH = None
                sG = None
                sG_of = {}
                for sb in range(16):
                    ws = w1c[0] % 3
                    w1c[0] += 1
                    w1dst = W1S[:, ws] if ws < 2 else W1SX[:]
                    q1s = em("sync", lambda s, e=e, sb=sb, w1dst=w1dst: s.dma_start(
                        w1dst, sw1[e, sb]),
                        waits=[("q2", q2_w1[e]), last["W1S"][ws], ("T", p2_T)],
                        inc="q0", inc_n=16)
                    for half in range(2):
                        hc = sb * 2 + half
                        ph = pS if hc % 2 == 0 else pE
                        mms = []
                        for ki in range(KI):
                            def f(t, ki=ki, half=half, ws=ws, sl3=sl3, ph=ph):
                                w1op = (W1S[:, ws, ki, half * 128:(half + 1) * 128]
                                        if ws < 2 else
                                        W1SX[:, ki, half * 128:(half + 1) * 128])
                                return t.matmul(
                                    ph[:, 0:512], w1op,
                                    XNJ3[:, sl3, ki, :], start=(ki == 0),
                                    stop=(ki == KI - 1), skip_group_check=True)
                            mms.append(f)
                        tH = mm_group(ph, mms, extra_waits=[("q0", q1s), ("q2", xr3)])
                        sG = em("scalar", lambda s, e=e, hc=hc, ph=ph: s.activation(
                            HBUF[:, hc, :], ph[:, 0:512], AF.Gelu,
                            bias=B1T[:, e, hc:hc + 1]),
                            waits=[("T", tH), ("V", v_b1_done)], inc="S")
                        sG_of[hc] = sG
                        ptr[id(ph)].evac = ("S", sG)
                    last["W1S"][ws] = ("T", tH)
                    if sb == 0 and j + 1 < NJ:
                        xr3_pend[j + 1] = em(
                            "gpsimd", lambda g, j=j: g.dma_start(
                                XNJ3[:, (j + 1) % 2], sxns[j + 1]),
                            waits=[("q1", spv), last["XNJ3"][(j + 1) % 2]],
                            inc="q2", inc_n=16)
                last["XNJ3"][sl3] = ("T", tH)
                if e == NE - 1 and j == NJ - 1:
                    w1_last_T = tH
                h_ready = ("S", sG)
                if debug and b == 0 and e == 0 and j == 0:
                    em("gpsimd", lambda g: g.dma_start(d_hb, HBUF[:]),
                       waits=[h_ready], inc="q2", inc_n=16)
                for tt4 in range(4):
                    tt = j * 4 + tt4
                    pw_ = pA if tt4 % 2 == 0 else pB
                    base_w = [("q2", w2_ready)] + [w for w in [pw(pw_)] if w]
                    tW2 = None
                    for hh in range(2):
                        for hc in range(32):
                            def f(t, hc=hc, hh=hh, tt4=tt4, pw_=pw_):
                                return t.matmul(
                                    pw_[:, hh * 512:(hh + 1) * 512],
                                    HBUF[:, hc, tt4 * 128:(tt4 + 1) * 128],
                                    W2E[:, hc, hh * 512:(hh + 1) * 512],
                                    start=(hc == 0), stop=(hc == 31 and hh == 1),
                                    skip_group_check=True)
                            w = list(base_w) if hc == 0 and hh == 0 else []
                            if hh == 0:
                                w.append(("S", sG_of[hc]))
                            tW2 = em("tensor", f, waits=w,
                                     inc="T" if (hc == 31 and hh == 1) else None)
                    ptr[id(pw_)].grp = tW2
                    last_w2_T = tW2
                    evF = em("vector", lambda v, tt=tt, e=e, pw_=pw_: v.scalar_tensor_tensor(
                        ACC[:, tt, :], pw_[:], WBUF[:, tt, e:e + 1], ACC[:, tt, :],
                        ALU.mult, ALU.add), waits=[("T", tW2)], inc="V")
                    ptr[id(pw_)].evac = ("V", evF)
                    if e == NE - 1 and tt % 2 == 1:
                        xa = em("gpsimd", lambda g, b=b, tt=tt: g.dma_start(
                            ACC[:, tt - 1:tt + 1, :],
                            dx[b].rearrange("(n p) c -> p n c", p=128)[:, tt - 1:tt + 1, :],
                            accum_op=ALU.add), waits=[("V", evF)], inc="q2", inc_n=16)
                        ov = em("sync", lambda s, b=b, tt=tt: s.dma_start(
                            dout[b].rearrange("(n p) c -> p n c", p=128)[:, tt - 1:tt + 1, :],
                            ACC[:, tt - 1:tt + 1, :]),
                            waits=[("q2", xa)], inc="q0", inc_n=16)
        last["KVseg"] = ("T", last_w2_T)
        last["Bseg"] = ("T", last_w2_T)
        p3_T = r.cnt["T"]

        qf = ("q0", ov)
        tf = ("T", p3_T)
        wf = ("T", w1_last_T)
        last["ACCga"] = qf
        last["XST"] = [wf, wf]
        last["XNT"] = [wf, wf]
        last["XNJ"] = ("q1", spv)
        last["KTM"] = wf
        last["LNSC"] = wf
        last["XNJ2"] = tf
        last["QJ"] = tf
        last["YJ"] = tf
        last["XNJ3"] = [wf, wf]
        last["W1S"] = [wf, wf, wf]

    em("gpsimd", lambda g: g.engine_nop(),
       waits=[("q1", r.cnt["q1"]), ("q0", r.cnt["q0"]), ("q2", r.cnt["q2"]),
              ("q3", r.cnt["q3"])], inc="G")

    from contextlib import ExitStack
    ctx = ExitStack()
    sems = {s: ctx.enter_context(nc.semaphore(s)) for s in
            ("T", "V", "S", "q0", "q1", "q2", "q3", "G")}
    with nc.Block() as block:
        @block.tensor
        def _(t):
            r.replay("tensor", t, sems)

        @block.vector
        def _(v):
            r.replay("vector", v, sems)

        @block.scalar
        def _(s):
            r.replay("scalar", s, sems)

        @block.sync
        def _(s):
            r.replay("sync", s, sems)

        @block.gpsimd
        def _(g):
            r.replay("gpsimd", g, sems)
    ctx.close()
    return nc


def make_masks():
    a = np.arange(128)[:, None, None]
    d = np.arange(4)[None, :, None]
    bb = np.arange(512)[None, None, :]
    return (a + 128 * d <= bb).astype(np.float32)


_NC_CACHE = {}


def _get_nc(B):
    if B not in _NC_CACHE:
        _NC_CACHE[B] = build_core_program(B)
    return _NC_CACHE[B]


def kernel(L0_attn_x, L0_attn_w_qkv, L0_attn_b_qkv, L0_attn_w_proj, L0_attn_b_proj,
           L0_ffn_W1, L0_ffn_b1, L0_ffn_W2, L0_ffn_b2, L0_ffn_gate_w, L0_ffn_gate_b,
           L0_ffn_temperature):
    x = np.ascontiguousarray(np.asarray(L0_attn_x, dtype=np.float32))
    nb = x.shape[0]
    B = nb // NCORES
    nc = _get_nc(B)
    common = dict(
        wqkv=np.ascontiguousarray(L0_attn_w_qkv, np.float32),
        bqkv=np.ascontiguousarray(L0_attn_b_qkv, np.float32).reshape(1, 3 * C),
        wproj=np.ascontiguousarray(L0_attn_w_proj, np.float32),
        bproj=np.ascontiguousarray(L0_attn_b_proj, np.float32).reshape(1, C),
        w1=np.ascontiguousarray(L0_ffn_W1, np.float32),
        b1=np.ascontiguousarray(L0_ffn_b1, np.float32),
        w2=np.ascontiguousarray(L0_ffn_W2, np.float32),
        b2=np.ascontiguousarray(L0_ffn_b2, np.float32),
        gw=np.ascontiguousarray(L0_ffn_gate_w, np.float32),
        gb=np.ascontiguousarray(L0_ffn_gate_b, np.float32).reshape(1, NE),
        temp=np.ascontiguousarray(L0_ffn_temperature, np.float32).reshape(1, 1),
        cmask=make_masks(),
        ident=np.eye(128, dtype=np.float32),
    )
    in_maps = []
    for c in range(NCORES):
        m = dict(common)
        m["x"] = x[c * B:(c + 1) * B]
        in_maps.append(m)
    res = run_bass_kernel_spmd(nc, in_maps, core_ids=list(range(NCORES)))
    return np.concatenate([r["out"] for r in res.results], axis=0)


# version 30 (final)
# speedup vs baseline: 1.2288x; 1.0672x over previous
import numpy as np
import concourse.bass as bass
import concourse.mybir as mybir
from concourse.bass_utils import run_bass_kernel_spmd

F32 = mybir.dt.float32
BF16 = mybir.dt.bfloat16
U8 = mybir.dt.uint8
AF = mybir.ActivationFunctionType
ALU = mybir.AluOpType
AX = mybir.AxisListType

NCORES = 8
T, C, NH, HD, NE, FF = 2048, 1024, 16, 64, 8, 4096
KI = C // 128
NTT = T // 128
NJ = T // 512
EXP_SHIFT = -12.0


class Rec:
    ENGS = ("tensor", "vector", "scalar", "sync", "gpsimd")

    def __init__(self):
        self.ops = {e: [] for e in self.ENGS}
        self.cnt = {s: 0 for s in ("T", "V", "S", "q0", "q1", "q2", "q3", "G")}

    SELF_SEM = {"vector": "V", "scalar": "S"}

    def emit(self, eng, fn, waits=(), inc=None, inc_n=1):
        waits = [w for w in waits if w is not None]
        for s, v in waits:
            assert v <= self.cnt[s], (s, v, self.cnt[s])
        own = self.SELF_SEM.get(eng)
        if own is not None:
            waits.append((own, self.cnt[own]))
            if inc is None:
                inc = own
                inc_n = 1
        post = None
        if inc is not None:
            self.cnt[inc] += inc_n
            post = self.cnt[inc]
        self.ops[eng].append((fn, tuple(waits), inc, inc_n))
        return post

    def replay(self, eng_name, eng, sems):
        waited = {}
        for fn, waits, inc, inc_n in self.ops[eng_name]:
            for s, v in waits:
                if waited.get(s, -1) < v:
                    eng.wait_ge(sems[s], v)
                    waited[s] = v
            ins = fn(eng)
            if inc is not None:
                if isinstance(ins, list):
                    ins = ins[-1]
                ins.then_inc(sems[inc], inc_n)


def build_core_program(B, debug=False):
    nc = bass.Bass("TRN2", target_bir_lowering=False)

    dx = nc.dram_tensor("x", [B, T, C], F32, kind="ExternalInput").ap()
    dwqkv = nc.dram_tensor("wqkv", [C, 3 * C], F32, kind="ExternalInput").ap()
    dbqkv = nc.dram_tensor("bqkv", [1, 3 * C], F32, kind="ExternalInput").ap()
    dwproj = nc.dram_tensor("wproj", [C, C], F32, kind="ExternalInput").ap()
    dbproj = nc.dram_tensor("bproj", [1, C], F32, kind="ExternalInput").ap()
    dw1 = nc.dram_tensor("w1", [NE, C, FF], F32, kind="ExternalInput").ap()
    db1 = nc.dram_tensor("b1", [NE, FF], F32, kind="ExternalInput").ap()
    dw2 = nc.dram_tensor("w2", [NE, FF, C], F32, kind="ExternalInput").ap()
    db2 = nc.dram_tensor("b2", [NE, C], F32, kind="ExternalInput").ap()
    dgw = nc.dram_tensor("gw", [C, NE], F32, kind="ExternalInput").ap()
    dgb = nc.dram_tensor("gb", [1, NE], F32, kind="ExternalInput").ap()
    dtemp = nc.dram_tensor("temp", [1, 1], F32, kind="ExternalInput").ap()
    dmask = nc.dram_tensor("cmask", [128, 4, 512], F32, kind="ExternalInput").ap()
    dident = nc.dram_tensor("ident", [128, 128], F32, kind="ExternalInput").ap()
    dout = nc.dram_tensor("out", [B, T, C], F32, kind="ExternalOutput").ap()
    swkvr = nc.dram_tensor("swkvr", [128, KI, 2 * C], BF16, kind="Internal").ap()
    swqr = nc.dram_tensor("swqr", [128, KI, C], BF16, kind="Internal").ap()
    swpr = nc.dram_tensor("swpr", [128, KI, C], BF16, kind="Internal").ap()
    sw1 = nc.dram_tensor("sw1", [NE, 16, 128, KI, 256], BF16, kind="Internal").ap()
    sw2r = nc.dram_tensor("sw2r", [NE, 128, 32, C], BF16, kind="Internal").ap()
    sxr = nc.dram_tensor("sxr", [max(B, 2), 128, NTT, C], F32, kind="Internal").ap()
    sxns = nc.dram_tensor("sxns", [NJ, 128, KI, 512], BF16, kind="Internal").ap()
    if debug:
        d_wbuf = nc.dram_tensor("d_wbuf", [128, NTT, NE], F32, kind="ExternalOutput").ap()
        d_acc1 = nc.dram_tensor("d_acc1", [128, NTT, C], F32, kind="ExternalOutput").ap()
        d_xn = nc.dram_tensor("d_xn", [NJ, 128, KI, 512], BF16, kind="ExternalOutput").ap()
        d_hb = nc.dram_tensor("d_hb", [128, 32, 512], BF16, kind="ExternalOutput").ap()
        d_kt = nc.dram_tensor("d_kt", [128, 8, T], BF16, kind="ExternalOutput").ap()
        d_vb = nc.dram_tensor("d_vb", [128, NTT, NH, HD + 1], BF16, kind="ExternalOutput").ap()
        d_qj = nc.dram_tensor("d_qj", [128, 8, 512], BF16, kind="ExternalOutput").ap()
        d_stat = nc.dram_tensor("d_stat", [128, 16], F32, kind="ExternalOutput").ap()
        d_xnt = nc.dram_tensor("d_xnt", [128, C], BF16, kind="ExternalOutput").ap()

    TOTAL = 211456
    MAPBASE = 16896
    nc.alloc_sbuf_tensor("PAD", [128, MAPBASE - 16512], U8)
    nc.alloc_sbuf_tensor("ARENA", [128, TOTAL], U8)

    off = [MAPBASE]

    class _Placed:
        __slots__ = ("t", "_at", "_end")

        def __init__(self, t, at, end):
            self.t = t
            self._at = at
            self._end = end

        def __getitem__(self, idx):
            return self.t[idx]

        def ap(self):
            return self.t.ap()

    def place(name, shape, dtype, at=None):
        esz = 2 if dtype == BF16 else 4
        width = int(np.prod(shape[1:])) * esz
        if at is None:
            at = (off[0] + 63) & ~63
            off[0] = at + width
            assert off[0] <= MAPBASE + TOTAL, (name, off[0])
        t = nc.alloc_sbuf_tensor_at(name, list(shape), dtype, offset=at)
        return _Placed(t, at, at + width)

    ACC = place("ACC", [128, NTT, C], F32)
    MASK = place("MASKB", [128, 4, 512], BF16)
    IDB = place("IDB", [128, 128], BF16)
    IDF = place("IDF", [128, 128], F32)
    CONST = place("CONSTS", [128, 8], F32)
    BQKVT = place("BQKVT", [128, 8], F32)
    B1T = place("B1T", [128, NE, 32], F32)
    GWB = place("GWB", [128, KI, NE], BF16)
    GBROW = place("GBROW", [1, NE], BF16)
    BKVROW = place("BKVROW", [1, 2 * C], BF16)
    BPROW = place("BPROW", [1, C], BF16)
    B2B = place("B2B", [NE, C], BF16)
    ONES = place("ONESB", [1, 128], BF16)
    RTEMP = place("RTEMP", [128, 1], F32)
    TTMP = place("TTMP", [1, 1], F32)
    WBUF = place("WBUF", [128, NTT, NE], F32)
    WBB = place("WBB", [128, 8], BF16)
    STAT = place("STATS", [128, 16], F32)
    GTMP = place("GTMP", [128, 32], F32)
    KT = place("KT", [128, 8, T], BF16)
    VB = place("VBV", [128, NTT, NH, HD + 1], BF16)
    W2E = place("W2E", [128, 32, C], BF16, at=KT._at)
    BQR = place("BQR", [1, C], F32, at=KT._at)
    B1R = place("B1R", [NE, FF], F32, at=KT._at)
    assert W2E._end <= VB._end
    WKV = place("WKV", [128, KI, 2 * C], BF16)
    WQQ = place("WQQ", [128, KI, C], BF16, at=WKV._at)
    WPJ = place("WPJ", [128, KI, C], BF16, at=WKV._at + KI * C * 2)
    HBUF = place("HBUF", [128, 32, 512], BF16, at=WKV._at)
    XST = place("XST", [128, 2, C], F32)
    XNT = place("XNT", [128, 2, C], BF16)
    XNJ = place("XNJ", [128, KI, 512], BF16)
    KTM = place("KTM", [128, C], BF16)
    LNSC = place("LNSC", [128, C], BF16)
    XNJ2 = place("XNJ2", [128, KI, 512], BF16, at=XST._at)
    EB = place("EB", [128, 2, 2, 512], BF16, at=XST._at)
    WTJ = place("WTJ", [8, 512], BF16, at=XST._at)
    YTMP = place("YTMP", [64, 2, 512], BF16, at=XST._at + 4096)
    RDEN = place("RDEN", [1, 2, 512], BF16, at=XST._at + 6144)
    QJ = place("QJ", [128, 8, 512], BF16, at=XNT._at)
    YJ = place("YJ", [128, 8, 512], BF16, at=XNT._at + 8192)
    XNJ3 = place("XNJ3", [128, 2, KI, 512], BF16, at=XNT._at)
    W1S = place("W1S", [128, 2, KI, 256], BF16, at=XST._at)
    W1SX = place("W1SX", [128, KI, 256], BF16, at=221376)
    assert W1SX._end <= MAPBASE + TOTAL
    XRS = place("XRS", [128, 2, C], F32, at=XNT._at)
    ORS = place("ORS", [128, 2, C], F32, at=XNT._at + 8192)
    assert max(LNSC._end, YJ._end, W1S._end, ORS._end) <= MAPBASE + TOTAL

    pA = nc.alloc_psum_tensor("pA", [128, 1024], F32)
    pB = nc.alloc_psum_tensor("pB", [128, 1024], F32)
    pS = nc.alloc_psum_tensor("pS", [128, 1024], F32)
    pE = nc.alloc_psum_tensor("pE", [128, 512], F32)
    pT = nc.alloc_psum_tensor("pT", [128, 1024], BF16)

    r = Rec()

    def em(eng, fn, waits=(), inc=None, inc_n=1):
        return r.emit(eng, fn, waits, inc, inc_n)

    class Tr:
        def __init__(self):
            self.evac = None
            self.grp = 0
    ptr = {id(p): Tr() for p in (pA, pB, pS, pE, pT)}

    def pw(p):
        return ptr[id(p)].evac

    def mm_group(p, mms, extra_waits=(), track=True):
        waits = [w for w in extra_waits if w is not None]
        if track:
            waits.append(pw(p))
        last = None
        for i, fn in enumerate(mms):
            last = em("tensor", fn, waits=waits if i == 0 else (),
                      inc="T" if i == len(mms) - 1 else None)
        if track:
            ptr[id(p)].grp = last
        return last

    em("gpsimd", lambda g: g.memset(ONES[:], 1.0), inc="G")
    em("gpsimd", lambda g: g.memset(CONST[:, 2:3], EXP_SHIFT), inc="G")
    em("gpsimd", lambda g: g.memset(CONST[:, 3:4], 0.125), inc="G")
    em("gpsimd", lambda g: g.memset(CONST[:, 4:5], -0.5), inc="G")
    em("gpsimd", lambda g: g.memset(CONST[:, 5:6], -1.0), inc="G")
    em("gpsimd", lambda g: g.memset(CONST[:, 6:7], 1e-5), inc="G")
    g_done = r.cnt["G"]
    q2 = {}
    q2["ident"] = em("gpsimd", lambda g: g.dma_start(IDB[:], dident), inc="q2", inc_n=16)
    q2["identf"] = em("gpsimd", lambda g: g.dma_start(IDF[:], dident), inc="q2", inc_n=16)
    q2["mask"] = em("gpsimd", lambda g: g.dma_start(MASK[:], dmask), inc="q2", inc_n=16)
    q2["gwb"] = em("gpsimd", lambda g: g.dma_start(
        GWB[:], dgw.rearrange("(ki p) e -> p ki e", p=128)), inc="q2", inc_n=16)
    q2["gb"] = em("gpsimd", lambda g: g.dma_start(GBROW[:], dgb), inc="q2", inc_n=16)
    q2["bkv"] = em("gpsimd", lambda g: g.dma_start(BKVROW[:], dbqkv[:, C:3 * C]),
                   inc="q2", inc_n=16)
    q2["bp"] = em("gpsimd", lambda g: g.dma_start(BPROW[:], dbproj), inc="q2", inc_n=16)
    q2["b2"] = em("gpsimd", lambda g: g.dma_start(B2B[:], db2), inc="q2", inc_n=16)
    q2["temp"] = em("gpsimd", lambda g: g.dma_start(TTMP[:], dtemp), inc="q2", inc_n=16)
    q2["bqr"] = em("gpsimd", lambda g: g.dma_start(BQR[:], dbqkv[:, 0:C]), inc="q2", inc_n=16)
    v_rt = em("vector", lambda v: v.reciprocal(STAT[0:1, 15:16], TTMP[:]),
              waits=[("q2", q2["temp"])], inc="V")
    em("vector", lambda v: v.tensor_copy(WBB[0:1, 0:1], STAT[0:1, 15:16]), inc="V")
    t_rt = em("tensor", lambda t: t.matmul(pE[:, 0:1], ONES[:], WBB[0:1, 0:1],
                                           start=True, stop=True, skip_group_check=True),
              waits=[("V", r.cnt["V"]), ("G", g_done)], inc="T")
    v_rt2 = em("vector", lambda v: v.tensor_copy(RTEMP[:], pE[:, 0:1]),
               waits=[("T", t_rt)], inc="V")
    ptr[id(pE)].evac = ("V", v_rt2)
    tb = None
    for fc in range(8):
        tb = em("tensor", lambda t, fc=fc: t.transpose(
            pE[:, 8 + fc:9 + fc], BQR[0:1, fc * 128:(fc + 1) * 128], IDF[0:1, 0:1]),
            waits=[("q2", q2["bqr"]), ("q2", q2["identf"]), pw(pE)] if fc == 0 else (),
            inc="T" if fc == 7 else None)
    ptr[id(pE)].grp = tb
    v_bq = em("vector", lambda v: v.tensor_copy(BQKVT[:], pE[:, 8:16]),
              waits=[("T", tb)], inc="V")
    ptr[id(pE)].evac = ("V", v_bq)
    q2["b1r"] = em("gpsimd", lambda g: g.dma_start(B1R[:], db1),
                   waits=[("T", tb)], inc="q2", inc_n=16)
    for hc in range(32):
        tb1 = em("tensor", lambda t, hc=hc: t.transpose(
            pE[:, 0:8], B1R[:, hc * 128:(hc + 1) * 128], IDF[0:8, 0:8]),
            waits=[("q2", q2["b1r"]), pw(pE)], inc="T")
        ptr[id(pE)].grp = tb1
        vb1 = em("vector", lambda v, hc=hc: v.tensor_copy(B1T[:, :, hc], pE[:, 0:8]),
                 waits=[("T", tb1)], inc="V")
        ptr[id(pE)].evac = ("V", vb1)
    b1t_done_T = r.cnt["T"]
    v_b1_done = r.cnt["V"]
    q2["wkvr"] = em("gpsimd", lambda g: g.dma_start(
        swkvr, dwqkv.rearrange("(ki p) c -> p ki c", p=128)[:, :, C:3 * C]),
        inc="q2", inc_n=16)
    q2["wqr"] = em("gpsimd", lambda g: g.dma_start(
        swqr, dwqkv.rearrange("(ki p) c -> p ki c", p=128)[:, :, 0:C]),
        inc="q2", inc_n=16)
    q2["wpr"] = em("gpsimd", lambda g: g.dma_start(
        swpr, dwproj.rearrange("(ki p) c -> p ki c", p=128)), inc="q2", inc_n=16)
    q2_w2, q2_w1 = {}, {}

    def emit_weight_casts(e):
        q2_w2[e] = em("gpsimd", lambda g, e=e: g.dma_start(
            sw2r[e], dw2[e].rearrange("(hc p) c -> p hc c", p=128)),
            inc="q2", inc_n=16)
        for sb in range(16):
            q2_w1[e] = em("gpsimd", lambda g, e=e, sb=sb: g.dma_start(
                sw1[e, sb],
                dw1[e].rearrange("(ki p) (sb f) -> sb p ki f", p=128, f=256)[sb]),
                inc="q2", inc_n=16)

    for e in range(NE):
        emit_weight_casts(e)

    nc.const_aps.register(F32, EXP_SHIFT, CONST[:, 2:3])
    nc.const_aps.register(F32, 0.125, CONST[:, 3:4])
    nc.const_aps.register(F32, -0.5, CONST[:, 4:5])
    nc.const_aps.register(F32, -1.0, CONST[:, 5:6])
    nc.const_aps.register(F32, 1e-5, CONST[:, 6:7])

    sxr_q0 = None
    for bb in range(1, B):
        sxr_q0 = ("q2", em("gpsimd", lambda g, bb=bb: g.dma_start(
            sxr[bb], dx[bb].rearrange("(n p) c -> p n c", p=128)),
            inc="q2", inc_n=16))

    def xsrc(b, t0, t1):
        if b == 0:
            return dx[0].rearrange("(n p) c -> p n c", p=128)[:, t0:t1, :]
        return sxr[b][:, t0:t1, :]

    NONE2 = [None, None]
    last = {
        "XST": list(NONE2), "XNT": list(NONE2), "XNJ": None, "KTM": None,
        "LNSC": None, "EB": list(NONE2), "XNJ3": list(NONE2),
        "W1S": [None, None, None],
        "XRS": list(NONE2), "ORS": list(NONE2), "XNJ2": None, "QJ": None, "ACCga": None,
        "YJ": None, "WTJ": None, "WBB": ("T", t_rt), "KVseg": ("T", b1t_done_T),
        "Bseg": None, "yun": list(NONE2),
    }
    last_w2_T = b1t_done_T
    w1c = [0]
    gW_final = None

    for b in range(B):
        vb_ones = em("vector", lambda v: v.memset(VB[:, :, :, HD:HD + 1], 1.0),
                     waits=[last["KVseg"]], inc="V")
        tt0 = 0
        xv_first = em("scalar", lambda s, b=b: s.dma_start(
            XST[:], xsrc(b, 0, 2)),
            waits=[last["XST"][0]] + ([sxr_q0] if b == 1 else []),
            inc="q1", inc_n=16)
        wkv_k4 = []
        for kk in range(4):
            w3 = em("sync", lambda s, kk=kk: s.dma_start(
                WKV[:, 2 * kk:2 * kk + 2, :], swkvr[:, 2 * kk:2 * kk + 2, :]),
                waits=[("q2", q2["wkvr"]), last["Bseg"]] if kk == 0 else (),
                inc="q3", inc_n=16)
            wkv_k4.append(("q3", w3))
        wkv_k = [wkv_k4[ki // 2] for ki in range(KI)]
        wkv_ready = wkv_k[-1]
        xnj_ready = None
        spv = None
        ktr_pend = None
        for j in range(NJ):
            for tt4 in range(4):
                tt = j * 4 + tt4
                sl = tt % 2
                if sl == 0:
                    if tt == tt0:
                        xv = xv_first
                    else:
                        xv = em("scalar", lambda s, b=b, tt=tt: s.dma_start(
                            XST[:], xsrc(b, tt, tt + 2)),
                            waits=[last["XST"][0]], inc="q1", inc_n=16)
                    xv_cur = ("q1", xv)
                    em("vector", lambda v: v.reduce_sum(
                        STAT[:, 0:2].rearrange("p c -> p c 1") if False else STAT[:, 0:2],
                        XST[:], AX.X), waits=[xv_cur], inc="V")
                    s_sq0 = em("scalar", lambda s: s.activation(
                        LNSC[:], XST[:, 0, :], AF.Square, accum_out=STAT[:, 2:3]),
                        waits=[xv_cur, last["LNSC"], ("G", g_done)], inc="S")
                    s_sq = em("scalar", lambda s: s.activation(
                        LNSC[:], XST[:, 1, :], AF.Square, accum_out=STAT[:, 3:4]),
                        waits=[("S", s_sq0)], inc="S")
                    em("vector", lambda v: v.tensor_scalar(
                        STAT[:, 4:6], STAT[:, 0:2], 1.0 / C, None, ALU.mult), inc="V")
                    em("vector", lambda v: v.tensor_scalar(
                        STAT[:, 6:8], STAT[:, 2:4], 1.0 / C, 1e-5, ALU.mult, ALU.add),
                        waits=[("S", s_sq)], inc="V")
                    em("vector", lambda v: v.tensor_mul(
                        STAT[:, 8:10], STAT[:, 4:6], STAT[:, 4:6]), inc="V")
                    v5 = em("vector", lambda v: v.tensor_sub(
                        STAT[:, 10:12], STAT[:, 6:8], STAT[:, 8:10]), inc="V")
                    sln = em("scalar", lambda s: s.activation(
                        STAT[:, 12:14], STAT[:, 10:12], AF.Ln),
                        waits=[("V", v5), ("G", g_done)], inc="S")
                    s3 = em("scalar", lambda s: s.activation(
                        STAT[:, 14:16], STAT[:, 12:14], AF.Exp, scale=-0.5),
                        waits=[("S", sln)], inc="S")
                    s3_cur = s3
                xv = xv_cur
                v6 = em("vector", lambda v, sl=sl: v.tensor_scalar(
                    XNT[:, sl, :], XST[:, sl, :], STAT[:, 4 + sl:5 + sl],
                    STAT[:, 14 + sl:15 + sl], ALU.subtract, ALU.mult),
                    waits=[("S", s3_cur), last["XNT"][sl]], inc="V")
                if sl == 1:
                    last["XST"][0] = ("V", v6)
                if debug and b == 0 and tt == 0:
                    em("gpsimd", lambda g: g.dma_start(d_stat, STAT[:]),
                       waits=[("V", v6), ("S", s3)], inc="q2", inc_n=16)
                    em("gpsimd", lambda g: g.dma_start(d_xnt, XNT[:, 0, :]),
                       waits=[("V", v6)], inc="q2", inc_n=16)
                tv = None
                for ki in range(KI):
                    tv = em("tensor", lambda t, ki=ki, sl=sl: t.transpose(
                        pT[:, ki * 128:(ki + 1) * 128],
                        XNT[:, sl, ki * 128:(ki + 1) * 128], IDB[:]),
                        waits=[("V", v6), ("q2", q2["ident"]), pw(pT)] if ki == 0 else (),
                        inc="T" if ki == KI - 1 else None)
                ptr[id(pT)].grp = tv
                last["XNT"][sl] = ("T", tv)
                ev = em("vector", lambda v, tt4=tt4: v.tensor_copy(
                    XNJ[:, :, tt4 * 128:(tt4 + 1) * 128],
                    pT[:].rearrange("p (ki f) -> p ki f", f=128)),
                    waits=[("T", tv)] + ([last["XNJ"]] if tt4 == 0 else []), inc="V")
                ptr[id(pT)].evac = ("V", ev)
                xnj_ready = ("V", ev)
                if ktr_pend is not None:
                    f = ktr_pend
                    ktr_pend = None
                    f()

                def emit_kv(dst, col0, bias_off, tt4, base_w):
                    last_mm = None
                    for hh in range(2):
                        for ki in range(KI):
                            def f(t, ki=ki, hh=hh, dst=dst, col0=col0, tt4=tt4):
                                return t.matmul(
                                    dst[:, hh * 512:(hh + 1) * 512],
                                    XNJ[:, ki, tt4 * 128:(tt4 + 1) * 128],
                                    WKV[:, ki, col0 + hh * 512:col0 + (hh + 1) * 512],
                                    start=(ki == 0), stop=False, skip_group_check=True)
                            w = list(base_w) if (ki == 0 and hh == 0) else []
                            if hh == 0:
                                w.append(wkv_k[ki])
                            last_mm = em("tensor", f, waits=w)
                        def f(t, hh=hh, dst=dst, bias_off=bias_off):
                            return t.matmul(
                                dst[:, hh * 512:(hh + 1) * 512], ONES[:],
                                BKVROW[:, bias_off + hh * 512:bias_off + (hh + 1) * 512],
                                start=False, stop=True, skip_group_check=True)
                        last_mm = em("tensor", f, inc="T" if hh == 1 else None)
                    return last_mm
                tK = emit_kv(pA, 0, 0, tt4,
                             [xnj_ready, ("q2", q2["bkv"]), ("G", g_done)] + (
                                 [pw(pA)] if pw(pA) else []))
                ptr[id(pA)].grp = tK
                tV = emit_kv(pB, C, C, tt4,
                             [xnj_ready] + ([pw(pB)] if pw(pB) else []))
                ptr[id(pB)].grp = tV
                gm = []
                for ki in range(KI):
                    def f(t, ki=ki, tt4=tt4):
                        return t.matmul(pS[:, 0:8], XNJ[:, ki, tt4 * 128:(tt4 + 1) * 128],
                                        GWB[:, ki, :], start=(ki == 0), stop=False,
                                        skip_group_check=True)
                    gm.append(f)
                gm.append(lambda t: t.matmul(pS[:, 0:8], ONES[:], GBROW[:],
                                             start=False, stop=True, skip_group_check=True))
                tG = mm_group(pS, gm, extra_waits=[xnj_ready, ("q2", q2["gwb"]),
                                                   ("q2", q2["gb"])])

                evK = em("vector", lambda v: v.tensor_copy(KTM[:], pA[:]),
                         waits=[("T", tK), last["KTM"]], inc="V")
                ptr[id(pA)].evac = ("V", evK)
                evV = em("vector", lambda v, tt=tt: v.tensor_copy(
                    VB[:, tt, :, 0:HD], pB[:].rearrange("p (h d) -> p h d", d=HD)),
                    waits=[("T", tV), ("V", vb_ones)], inc="V")
                ptr[id(pB)].evac = ("V", evV)
                evG = em("scalar", lambda s, tt=tt: s.activation(
                    WBUF[:, tt, :], pS[:, 0:8], AF.Copy, scale=RTEMP[:, 0:1]),
                    waits=[("T", tG), ("V", v_rt2)], inc="S")
                ptr[id(pS)].evac = ("S", evG)
                def emit_ktr(tt=tt, evK=evK):
                    ktv = None
                    for fc in range(8):
                        ktv = em("tensor", lambda t, fc=fc: t.transpose(
                            pT[:, fc * 128:(fc + 1) * 128],
                            KTM[:, fc * 128:(fc + 1) * 128], IDB[:]),
                            waits=[("V", evK), pw(pT)] if fc == 0 else (),
                            inc="T" if fc == 7 else None)
                    ptr[id(pT)].grp = ktv
                    last["KTM"] = ("T", ktv)
                    evKT = em("vector", lambda v, tt=tt: v.tensor_copy(
                        KT[:, :, tt * 128:(tt + 1) * 128],
                        pT[:].rearrange("p (fc f) -> p fc f", f=128)),
                        waits=[("T", ktv)] + ([last["KVseg"]] if tt == 0 else []),
                        inc="V")
                    ptr[id(pT)].evac = ("V", evKT)
                ktr_pend = emit_ktr
            if j == NJ - 1 and ktr_pend is not None:
                f = ktr_pend
                ktr_pend = None
                f()
            spv = em("scalar", lambda s, j=j: s.dma_start(sxns[j], XNJ[:]),
                     waits=[xnj_ready], inc="q1", inc_n=16)
            last["XNJ"] = ("q1", spv)
        p1_T = r.cnt["T"]
        p1_V = r.cnt["V"]
        p1_S = r.cnt["S"]
        for tt in range(NTT):
            em("vector", lambda v, tt=tt: v.reduce_max(
                STAT[:, 8:9], WBUF[:, tt, :], AX.X), waits=[("S", p1_S)], inc="V")
            em("vector", lambda v, tt=tt: v.tensor_scalar(
                GTMP[:, 8:16], WBUF[:, tt, :], STAT[:, 8:9], None, ALU.is_ge), inc="V")
            em("vector", lambda v, tt=tt: v.scalar_tensor_tensor(
                GTMP[:, 16:24], GTMP[:, 8:16], -1e30, WBUF[:, tt, :],
                ALU.mult, ALU.add), inc="V")
            em("vector", lambda v: v.reduce_max(STAT[:, 9:10], GTMP[:, 16:24], AX.X),
               inc="V")
            g5 = em("vector", lambda v, tt=tt: v.reduce_max(
                STAT[:, 10:11], WBUF[:, tt, :], AX.X, negate=True), inc="V")
            sE = em("scalar", lambda s, tt=tt: s.activation(
                GTMP[:, 24:32], WBUF[:, tt, :], AF.Exp, bias=STAT[:, 10:11]),
                waits=[("V", g5)], inc="S")
            em("vector", lambda v, tt=tt: v.tensor_scalar(
                GTMP[:, 8:16], WBUF[:, tt, :], STAT[:, 9:10], None, ALU.is_ge),
                waits=[("S", sE)], inc="V")
            em("vector", lambda v: v.tensor_mul(
                GTMP[:, 24:32], GTMP[:, 24:32], GTMP[:, 8:16]), inc="V")
            em("vector", lambda v: v.reduce_sum(STAT[:, 11:12], GTMP[:, 24:32], AX.X),
               inc="V")
            em("vector", lambda v: v.reciprocal(STAT[:, 12:13], STAT[:, 11:12]), inc="V")
            em("vector", lambda v, tt=tt: v.tensor_scalar(
                WBUF[:, tt, :], GTMP[:, 24:32], STAT[:, 12:13], None, ALU.mult),
                inc="V")

        qq = em("sync", lambda s: s.dma_start(WQQ[:], swqr),
                waits=[("T", p1_T), ("q2", q2["wqr"])], inc="q0", inc_n=16)
        qp = em("sync", lambda s: s.dma_start(WPJ[:], swpr),
                waits=[("q2", q2["wpr"])], inc="q0", inc_n=16)

        vY_last = None
        for j in range(NJ):
            xr = em("scalar", lambda s, j=j: s.dma_start(XNJ2[:], sxns[j]),
                    waits=[("q1", spv), last["XNJ2"]], inc="q1", inc_n=16)
            evQ = None
            for fc in range(8):
                pq = pS if fc % 2 == 0 else pE
                mms = []
                for ki in range(KI):
                    def f(t, ki=ki, fc=fc, pq=pq):
                        return t.matmul(pq[:, 0:512], WQQ[:, ki, fc * 128:(fc + 1) * 128],
                                        XNJ2[:, ki, :], start=(ki == 0), stop=(ki == KI - 1),
                                        skip_group_check=True)
                    mms.append(f)
                tQ = mm_group(pq, mms,
                              extra_waits=[("q1", xr), ("q0", qq), ("V", v_bq)])
                evQ = em("vector", lambda v, fc=fc, pq=pq: v.tensor_scalar(
                    QJ[:, fc, :], pq[:, 0:512], BQKVT[:, fc:fc + 1], None, ALU.add),
                    waits=[("T", tQ)] + ([last["QJ"]] if fc == 0 else []), inc="V")
                ptr[id(pq)].evac = ("V", evQ)
            qj_ready = ("V", evQ)

            nkc = 4 * j + 4
            norm_pend = []

            def emit_norm_a(h, avl):
                ys = h % 2
                lncell = ACC[0:1, 4 * j, 0:512]
                em("scalar", lambda s, ys=ys, lncell=lncell: s.activation(
                    lncell, pA[HD:HD + 1, ys * 512:ys * 512 + 512], AF.Ln),
                    waits=[("T", avl), last["ACCga"]], inc="S")
                vr2 = em("scalar", lambda s, ys=ys, lncell=lncell: s.activation(
                    RDEN[0:1, ys, :], lncell, AF.Exp, scale=-1.0), inc="S")
                return (vr2,)

            def emit_norm_b2(pends):
                tB = None
                w0 = [pw(pB)] + [("S", vr2) for _, vr2 in pends]
                for (h, vr2) in pends:
                    ys = h % 2
                    tB = em("tensor", lambda t, ys=ys: t.matmul(
                        pB[0:HD, ys * 512:ys * 512 + 512], ONES[0:1, 0:HD],
                        RDEN[0:1, ys, :], start=True, stop=True,
                        skip_group_check=True),
                        waits=w0 if ys == 0 else (), inc="T" if ys == 1 else None)
                ptr[id(pB)].grp = tB
                vY = None
                for (h, vr2) in pends:
                    ys = h % 2
                    p0 = ys * 64
                    fc = h // 2
                    sY = em("scalar", lambda s, ys=ys: s.copy(
                        YTMP[:, ys, :], pA[0:HD, ys * 512:ys * 512 + 512]), inc="S")
                    last["yun"][ys] = ("S", sY)
                    vY = em("vector", lambda v, p0=p0, fc=fc, ys=ys: v.tensor_mul(
                        YJ[p0:p0 + HD, fc, :], YTMP[:, ys, :],
                        pB[0:HD, ys * 512:ys * 512 + 512]),
                        waits=[("T", tB), ("S", sY)] + ([last["YJ"]] if h < 2 else []),
                        inc="V")
                ptr[id(pB)].evac = ("V", vY)
                return vY

            for fc in range(NH // 2):
                pS_of = [pS, pB]

                def emit_s2(kc, first_wait):
                    px = pS_of[kc % 2]
                    w = [first_wait, ("V", p1_V)] + ([pw(px)] if pw(px) else [])
                    for half in range(2):
                        def f(t, kc=kc, half=half, px=px, fc=fc):
                            p0 = half * 64
                            return t.matmul(
                                px[:, half * 512:(half + 1) * 512],
                                KT[p0:p0 + 64, fc, kc * 128:(kc + 1) * 128],
                                QJ[p0:p0 + 64, fc, :], start=True, stop=True,
                                skip_group_check=True)
                        tS = em("tensor", f, waits=w if half == 0 else (),
                                inc="T" if half == 1 else None)
                    ptr[id(px)].grp = tS
                    return tS

                def emit_exp2(kc, tS):
                    px = pS_of[kc % 2]
                    es = kc % 2
                    sEx = em("scalar", lambda s, es=es, px=px: s.activation(
                        EB[:, es, :, :].rearrange("p a q -> p (a q)"), px[:],
                        AF.Exp, bias=EXP_SHIFT, scale=0.125),
                        waits=[("T", tS), last["EB"][es], ("V", p1_V), ("G", g_done)],
                        inc="S")
                    ptr[id(px)].evac = ("S", sEx)
                    out = ("S", sEx)
                    d = kc - 4 * j
                    if d >= 0:
                        for half in range(2):
                            mv = em("vector", lambda v, es=es, half=half, d=d: v.tensor_mul(
                                EB[:, es, half, :], EB[:, es, half, :], MASK[:, d, :]),
                                waits=[out, ("q2", q2["mask"])], inc="V")
                            out = ("V", mv)
                    return out

                def emit_av2(kc, eready, fc=fc):
                    es = kc % 2
                    lastmm = None
                    for half in range(2):
                        h = 2 * fc + half
                        wv = [eready]
                        if kc == 0:
                            wv.append(last["yun"][half])
                        def f(t, kc=kc, es=es, half=half, h=h):
                            return t.matmul(
                                pA[0:HD + 1, half * 512:half * 512 + 512],
                                VB[:, kc, h, :], EB[:, es, half, :],
                                start=(kc == 0), stop=(kc == nkc - 1),
                                skip_group_check=True)
                        lastmm = em("tensor", f, waits=wv, inc="T")
                    last["EB"][es] = ("T", lastmm)
                    return lastmm

                er_prev = None
                avl = None
                for kc in range(nkc + 1):
                    if kc < nkc:
                        tS = emit_s2(kc, qj_ready)
                        er = emit_exp2(kc, tS)
                    if kc == 0 and norm_pend:
                        vY_last = emit_norm_b2(norm_pend)
                        norm_pend = []
                    if kc > 0:
                        avl = emit_av2(kc - 1, er_prev)
                    if kc < nkc:
                        er_prev = er
                for half in range(2):
                    h = 2 * fc + half
                    na = emit_norm_a(h, avl)[0]
                    norm_pend.append((h, na))
            if norm_pend:
                vY_last = emit_norm_b2(norm_pend)
            norm_pend = []
            yj_ready = ("V", r.cnt["V"])

            for tt4 in range(4):
                tt = j * 4 + tt4
                vc = em("vector", lambda v, tt=tt: v.tensor_copy(WBB[:], WBUF[:, tt, :]),
                        waits=[last["WBB"]], inc="V")
                tW = em("tensor", lambda t: t.transpose(pT[0:8, 0:128], WBB[:], IDB[:]),
                        waits=[("V", vc), pw(pT)], inc="T")
                ptr[id(pT)].grp = tW
                last["WBB"] = ("T", tW)
                vW = em("vector", lambda v, tt4=tt4: v.tensor_copy(
                    WTJ[0:8, tt4 * 128:(tt4 + 1) * 128], pT[0:8, 0:128]),
                    waits=[("T", tW), last["WTJ"]], inc="V")
                ptr[id(pT)].evac = ("V", vW)
                mms = []
                for hh in range(2):
                    for ki in range(KI):
                        def f(t, ki=ki, hh=hh, tt4=tt4):
                            return t.matmul(pB[:, hh * 512:(hh + 1) * 512],
                                            YJ[:, ki, tt4 * 128:(tt4 + 1) * 128],
                                            WPJ[:, ki, hh * 512:(hh + 1) * 512],
                                            start=(ki == 0), stop=False,
                                            skip_group_check=True)
                        mms.append(f)
                    def f(t, hh=hh):
                        return t.matmul(pB[:, hh * 512:(hh + 1) * 512], ONES[:],
                                        BPROW[:, hh * 512:(hh + 1) * 512],
                                        start=False, stop=False, skip_group_check=True)
                    mms.append(f)
                    def f(t, hh=hh, tt4=tt4):
                        return t.matmul(pB[:, hh * 512:(hh + 1) * 512],
                                        WTJ[0:8, tt4 * 128:(tt4 + 1) * 128],
                                        B2B[:, hh * 512:(hh + 1) * 512],
                                        start=False, stop=(hh == 1), skip_group_check=True)
                    mms.append(f)
                tP = mm_group(pB, mms, extra_waits=[yj_ready, ("V", vW), ("q0", qp),
                                                    ("q2", q2["bp"]), ("q2", q2["b2"])])
                evP = em("vector", lambda v, tt=tt: v.tensor_copy(ACC[:, tt, :], pB[:]),
                         waits=[("T", tP), last["ACCga"]], inc="V")
                ptr[id(pB)].evac = ("V", evP)
            last["QJ"] = ("T", r.cnt["T"])
            last["XNJ2"] = ("T", r.cnt["T"])
            last["YJ"] = ("T", r.cnt["T"])
            last["WTJ"] = ("T", r.cnt["T"])
        p2_T = r.cnt["T"]
        if debug and b == 0:
            em("gpsimd", lambda g: g.dma_start(d_wbuf, WBUF[:]),
               waits=[("V", r.cnt["V"])], inc="q2", inc_n=16)
            em("gpsimd", lambda g: g.dma_start(d_acc1, ACC[:]),
               waits=[("V", r.cnt["V"])], inc="q2", inc_n=16)
            em("gpsimd", lambda g: g.dma_start(d_xn, sxns),
               waits=[("q1", r.cnt["q1"])], inc="q2", inc_n=16)
            em("gpsimd", lambda g: g.dma_start(d_kt, KT[:]),
               waits=[("V", r.cnt["V"])], inc="q2", inc_n=16)
            em("gpsimd", lambda g: g.dma_start(d_vb, VB[:]),
               waits=[("V", r.cnt["V"])], inc="q2", inc_n=16)
            em("gpsimd", lambda g: g.dma_start(d_qj, QJ[:]),
               waits=[("V", r.cnt["V"])], inc="q2", inc_n=16)

        for e in range(NE):
            xr3_pend = {}
            xr3_pend[0] = em("gpsimd", lambda g: g.dma_start(XNJ3[:, 0], sxns[0]),
                             waits=[("q1", spv), last["XNJ3"][0], ("T", p2_T)],
                             inc="q2", inc_n=16)
            w2l = None
            for c8 in range(8):
                w2l = em("gpsimd", lambda g, e=e, c8=c8: g.dma_start(
                    W2E[:, c8 * 4:(c8 + 1) * 4, :],
                    sw2r[e][:, c8 * 4:(c8 + 1) * 4, :]),
                    waits=[("q2", q2_w2[e]), ("T", p2_T if e == 0 else last_w2_T)]
                    if c8 == 0 else (), inc="q2", inc_n=16)
            w2_ready = w2l
            for j in range(NJ):
                sl3 = j % 2
                xr3 = xr3_pend.pop(j)
                tH = None
                sG = None
                sG_of = {}
                for sb in range(16):
                    ws = w1c[0] % 3
                    w1c[0] += 1
                    w1dst = W1S[:, ws] if ws < 2 else W1SX[:]
                    q1s = em("sync", lambda s, e=e, sb=sb, w1dst=w1dst: s.dma_start(
                        w1dst, sw1[e, sb]),
                        waits=[("q2", q2_w1[e]), last["W1S"][ws], ("T", p2_T)],
                        inc="q0", inc_n=16)
                    for half in range(2):
                        hc = sb * 2 + half
                        ph = pS if hc % 2 == 0 else pE
                        mms = []
                        for ki in range(KI):
                            def f(t, ki=ki, half=half, ws=ws, sl3=sl3, ph=ph):
                                w1op = (W1S[:, ws, ki, half * 128:(half + 1) * 128]
                                        if ws < 2 else
                                        W1SX[:, ki, half * 128:(half + 1) * 128])
                                return t.matmul(
                                    ph[:, 0:512], w1op,
                                    XNJ3[:, sl3, ki, :], start=(ki == 0),
                                    stop=(ki == KI - 1), skip_group_check=True)
                            mms.append(f)
                        tH = mm_group(ph, mms, extra_waits=[("q0", q1s), ("q2", xr3)])
                        sG = em("scalar", lambda s, e=e, hc=hc, ph=ph: s.activation(
                            HBUF[:, hc, :], ph[:, 0:512], AF.Gelu,
                            bias=B1T[:, e, hc:hc + 1]),
                            waits=[("T", tH), ("V", v_b1_done)], inc="S")
                        sG_of[hc] = sG
                        ptr[id(ph)].evac = ("S", sG)
                    last["W1S"][ws] = ("T", tH)
                    if sb == 0 and j + 1 < NJ:
                        xr3_pend[j + 1] = em(
                            "gpsimd", lambda g, j=j: g.dma_start(
                                XNJ3[:, (j + 1) % 2], sxns[j + 1]),
                            waits=[("q1", spv), last["XNJ3"][(j + 1) % 2]],
                            inc="q2", inc_n=16)
                last["XNJ3"][sl3] = ("T", tH)
                if e == NE - 1 and j == NJ - 1:
                    w1_last_T = tH
                h_ready = ("S", sG)
                if debug and b == 0 and e == 0 and j == 0:
                    em("gpsimd", lambda g: g.dma_start(d_hb, HBUF[:]),
                       waits=[h_ready], inc="q2", inc_n=16)
                for tt4 in range(4):
                    tt = j * 4 + tt4
                    pw_ = pA if tt4 % 2 == 0 else pB
                    base_w = [("q2", w2_ready)] + [w for w in [pw(pw_)] if w]
                    tW2 = None
                    for hh in range(2):
                        for hc in range(32):
                            def f(t, hc=hc, hh=hh, tt4=tt4, pw_=pw_):
                                return t.matmul(
                                    pw_[:, hh * 512:(hh + 1) * 512],
                                    HBUF[:, hc, tt4 * 128:(tt4 + 1) * 128],
                                    W2E[:, hc, hh * 512:(hh + 1) * 512],
                                    start=(hc == 0), stop=(hc == 31 and hh == 1),
                                    skip_group_check=True)
                            w = list(base_w) if hc == 0 and hh == 0 else []
                            if hh == 0:
                                w.append(("S", sG_of[hc]))
                            tW2 = em("tensor", f, waits=w,
                                     inc="T" if (hc == 31 and hh == 1) else None)
                    ptr[id(pw_)].grp = tW2
                    last_w2_T = tW2
                    evF = em("vector", lambda v, tt=tt, e=e, pw_=pw_: v.scalar_tensor_tensor(
                        ACC[:, tt, :], pw_[:], WBUF[:, tt, e:e + 1], ACC[:, tt, :],
                        ALU.mult, ALU.add), waits=[("T", tW2)], inc="V")
                    ptr[id(pw_)].evac = ("V", evF)
                    if e == NE - 1 and tt % 2 == 1:
                        xa = em("gpsimd", lambda g, b=b, tt=tt: g.dma_start(
                            ACC[:, tt - 1:tt + 1, :], xsrc(b, tt - 1, tt + 1),
                            accum_op=ALU.add),
                            waits=[("V", evF)] + (
                                [sxr_q0] if (b == 1 and tt == 1) else []),
                            inc="q2", inc_n=16)
                        ov = em("sync", lambda s, b=b, tt=tt: s.dma_start(
                            dout[b].rearrange("(n p) c -> p n c", p=128)[:, tt - 1:tt + 1, :],
                            ACC[:, tt - 1:tt + 1, :]),
                            waits=[("q2", xa)], inc="q0", inc_n=16)
        last["KVseg"] = ("T", last_w2_T)
        last["Bseg"] = ("T", last_w2_T)
        p3_T = r.cnt["T"]

        qf = ("q0", ov)
        tf = ("T", p3_T)
        wf = ("T", w1_last_T)
        last["ACCga"] = qf
        last["XST"] = [wf, wf]
        last["XNT"] = [wf, wf]
        last["XNJ"] = ("q1", spv)
        last["KTM"] = wf
        last["LNSC"] = wf
        last["XNJ2"] = tf
        last["QJ"] = tf
        last["YJ"] = tf
        last["XNJ3"] = [wf, wf]
        last["W1S"] = [wf, wf, wf]

    em("gpsimd", lambda g: g.engine_nop(),
       waits=[("q1", r.cnt["q1"]), ("q0", r.cnt["q0"]), ("q2", r.cnt["q2"]),
              ("q3", r.cnt["q3"])], inc="G")

    from contextlib import ExitStack
    ctx = ExitStack()
    sems = {s: ctx.enter_context(nc.semaphore(s)) for s in
            ("T", "V", "S", "q0", "q1", "q2", "q3", "G")}
    with nc.Block() as block:
        @block.tensor
        def _(t):
            r.replay("tensor", t, sems)

        @block.vector
        def _(v):
            r.replay("vector", v, sems)

        @block.scalar
        def _(s):
            r.replay("scalar", s, sems)

        @block.sync
        def _(s):
            r.replay("sync", s, sems)

        @block.gpsimd
        def _(g):
            r.replay("gpsimd", g, sems)
    ctx.close()
    return nc


def make_masks():
    a = np.arange(128)[:, None, None]
    d = np.arange(4)[None, :, None]
    bb = np.arange(512)[None, None, :]
    return (a + 128 * d <= bb).astype(np.float32)


_NC_CACHE = {}


def _get_nc(B):
    if B not in _NC_CACHE:
        _NC_CACHE[B] = build_core_program(B)
    return _NC_CACHE[B]


def kernel(L0_attn_x, L0_attn_w_qkv, L0_attn_b_qkv, L0_attn_w_proj, L0_attn_b_proj,
           L0_ffn_W1, L0_ffn_b1, L0_ffn_W2, L0_ffn_b2, L0_ffn_gate_w, L0_ffn_gate_b,
           L0_ffn_temperature):
    x = np.ascontiguousarray(np.asarray(L0_attn_x, dtype=np.float32))
    nb = x.shape[0]
    B = nb // NCORES
    nc = _get_nc(B)
    common = dict(
        wqkv=np.ascontiguousarray(L0_attn_w_qkv, np.float32),
        bqkv=np.ascontiguousarray(L0_attn_b_qkv, np.float32).reshape(1, 3 * C),
        wproj=np.ascontiguousarray(L0_attn_w_proj, np.float32),
        bproj=np.ascontiguousarray(L0_attn_b_proj, np.float32).reshape(1, C),
        w1=np.ascontiguousarray(L0_ffn_W1, np.float32),
        b1=np.ascontiguousarray(L0_ffn_b1, np.float32),
        w2=np.ascontiguousarray(L0_ffn_W2, np.float32),
        b2=np.ascontiguousarray(L0_ffn_b2, np.float32),
        gw=np.ascontiguousarray(L0_ffn_gate_w, np.float32),
        gb=np.ascontiguousarray(L0_ffn_gate_b, np.float32).reshape(1, NE),
        temp=np.ascontiguousarray(L0_ffn_temperature, np.float32).reshape(1, 1),
        cmask=make_masks(),
        ident=np.eye(128, dtype=np.float32),
    )
    in_maps = []
    for c in range(NCORES):
        m = dict(common)
        m["x"] = x[c * B:(c + 1) * B]
        in_maps.append(m)
    res = run_bass_kernel_spmd(nc, in_maps, core_ids=list(range(NCORES)))
    return np.concatenate([r["out"] for r in res.results], axis=0)
